# version 09
# baseline (speedup 1.0000x reference)
import contextlib
import math
import numpy as np
import concourse.bass as bass
import concourse.mybir as mybir
from concourse.bass_utils import run_bass_kernel_spmd

F32 = mybir.dt.float32
F16 = mybir.dt.float16
BF16 = mybir.dt.bfloat16
AF = mybir.ActivationFunctionType
ALU = mybir.AluOpType
AX = mybir.AxisListType

ENGINES = ("sync", "tensor", "vector", "scalar", "gpsimd")

B = 2
N = 16384
D = 1024
NCORE = 8
SH = N // 4
H, DH = 16, 64
M = 64
DFF = 4096
EPS_KERN = 1e-6
EPS_LN = 1e-6
EPS_LN_OUTER = 1e-5
RATIO = 1.0 / math.sqrt(M)
DN = 1.0 / math.sqrt(math.sqrt(DH))
D_INNER = 2048
D_STATE = 16
D_CONV = 4
FF_DIM = 2048
N_ROUTED = 8
TOP_K = 2


class Prog:

    def __init__(self, nc: bass.Bass):
        self.nc = nc
        self.ctx = contextlib.ExitStack()
        self.sems = {e: self.ctx.enter_context(nc.semaphore(f"q_{e}"))
                     for e in ENGINES}
        self.counts = {e: 0 for e in ENGINES}
        self.schedule = {e: [] for e in ENGINES}
        self.last_wait = {e: {} for e in ENGINES}
        self.res = {}

    def sb(self, name, shape, dtype):
        self._nsb = getattr(self, "_nsb", 0) + 1
        return self.ctx.enter_context(
            self.nc.sbuf_tensor(f"{name}_s{self._nsb}", shape, dtype))

    def constcol(self, value):
        k = f"__const_{value!r}"
        if k not in self.res or self.res[k]["w"] is None:
            t = self.sb(f"c{len([x for x in self.res if x.startswith('__const')])}",
                        [128, 1], F32)
            self.emit("gpsimd", lambda e: e.memset(t[:], value), writes=[k],
                      tag=f"const{value}")
            self._constcache = getattr(self, "_constcache", {})
            self._constcache[k] = t
        return self._constcache[k][:], k

    def ps(self, name, shape, dtype=F32):
        self._nps = getattr(self, "_nps", 0) + 1
        return self.ctx.enter_context(
            self.nc.psum_tensor(f"{name}_p{self._nps}", shape, dtype))

    @contextlib.contextmanager
    def phase(self):
        old = self.ctx
        self.ctx = contextlib.ExitStack()
        try:
            yield
        finally:
            self.ctx.close()
            self.ctx = old
            for k in list(getattr(self, "_constcache", {})):
                del self._constcache[k]
                self.res.pop(k, None)

    def barrier(self):
        snap = {e: self.counts[e] for e in ENGINES if self.counts[e] > 0}
        for e in ENGINES:
            for (e2, v2) in snap.items():
                if v2 <= self.last_wait[e].get(e2, 0):
                    continue
                self.schedule[e].append(([(e2, v2)], lambda _: None, 0,
                                         "barrier"))
                self.last_wait[e][e2] = v2

    def emit(self, engine, fn, reads=(), writes=(), deps=(), inc=1, tag=""):
        alldeps = list(deps)
        for k in reads:
            r = self.res.get(k)
            if r and r["w"]:
                alldeps.append(r["w"])
        for k in writes:
            r = self.res.get(k)
            if r:
                if r["w"]:
                    alldeps.append(r["w"])
                alldeps.extend(r["r"])
        waits = []
        for d in alldeps:
            if d is None:
                continue
            e2, v2 = d
            assert v2 <= self.counts[e2], f"dep on future op {d} tag={tag}"
            if v2 <= self.last_wait[engine].get(e2, 0):
                continue
            waits.append((e2, v2))
            self.last_wait[engine][e2] = v2
        byeng = {}
        for (e2, v2) in waits:
            byeng[e2] = max(byeng.get(e2, 0), v2)
        waits = sorted(byeng.items())
        self.counts[engine] += inc
        tok = (engine, self.counts[engine])
        self.schedule[engine].append((waits, fn, inc, tag))
        for k in reads:
            self.res.setdefault(k, {"w": None, "r": []})["r"].append(tok)
        for k in writes:
            self.res[k] = {"w": tok, "r": []}
        return tok

    def dma(self, queue, out, in_, reads=(), writes=(), deps=(), tag="", **kw):
        return self.emit(queue, lambda e: e.dma_start(out=out, in_=in_, **kw),
                         reads=reads, writes=writes, deps=deps, inc=16, tag=tag)

    def final_wait(self, engine, sem_engine=None, value=None):
        se = sem_engine or engine
        val = value if value is not None else self.counts[se]
        self.schedule[engine].append(([(se, val)], lambda e: None, 0, "final"))

    def build(self):
        nc = self.nc
        sems = self.sems
        with nc.Block() as block:
            for eng_name in ENGINES:
                ops = self.schedule[eng_name]
                if not ops:
                    continue

                def make_body(ops=ops, eng_name=eng_name):
                    def body(eng):
                        for (waits, fn, inc, tag) in ops:
                            for (e2, v2) in waits:
                                eng.wait_ge(sems[e2], v2)
                            inst = fn(eng)
                            if inc and inst is not None:
                                inst.then_inc(sems[eng_name], inc)
                    return body

                getattr(block, eng_name)(make_body())

    def finish(self):
        self.ctx.close()


def ew(prog, engine, fname, *args, reads=(), writes=(), deps=(), tag="", **kw):
    def fn(e):
        return getattr(e, fname)(*args, **kw)
    return prog.emit(engine, fn, reads=reads, writes=writes, deps=deps, tag=tag)


def mm_group(prog, triples, reads=(), writes=(), deps=(), tag="", **mmkw):
    def fn(e):
        inst = None
        for (out, lhsT, rhs, start, stop) in triples:
            inst = e.matmul(out, lhsT, rhs, start=start, stop=stop, **mmkw)
        return inst
    return prog.emit("tensor", fn, reads=reads, writes=writes, deps=deps, tag=tag)


def transpose(prog, out_psum, in_sbuf, ident, reads=(), writes=(), deps=(), tag=""):
    return prog.emit("tensor", lambda e: e.transpose(out_psum, in_sbuf, ident),
                     reads=reads, writes=writes, deps=deps, tag=tag)


def act(prog, out, in_, func, bias=0.0, scale=1.0, reads=(), writes=(), deps=(),
        engine="scalar", tag=""):
    return prog.emit(engine,
                     lambda e: e.activation(out, in_, func, bias=bias, scale=scale),
                     reads=reads, writes=writes, deps=deps, tag=tag)


def make_identity(prog, ident_ap, key, deps=()):
    prog.emit("gpsimd", lambda e: e.memset(ident_ap, 0.0), deps=deps,
              writes=[key], tag="id0")
    n = ident_ap.shape[0]
    return prog.emit("gpsimd", lambda e: e.affine_select(
        out=ident_ap, in_=ident_ap, compare_op=ALU.not_equal, fill=1.0,
        base=0, pattern=[[-1, n]], channel_multiplier=1),
        reads=[key], writes=[key], tag="id1")


def make_range_sel(prog, ap3, key, cm, step_pats, lo, width, deps=(), fill=1.0):
    prog.emit("gpsimd", lambda e: e.memset(ap3, fill), deps=deps,
              writes=[key], tag="rs0")
    prog.emit("gpsimd", lambda e: e.affine_select(
        out=ap3, in_=ap3, compare_op=ALU.is_ge, fill=0.0, base=-lo,
        pattern=step_pats, channel_multiplier=cm),
        reads=[key], writes=[key], tag="rs1")
    neg = [[-s, c] for (s, c) in step_pats]
    return prog.emit("gpsimd", lambda e: e.affine_select(
        out=ap3, in_=ap3, compare_op=ALU.is_ge, fill=0.0, base=lo + width - 1,
        pattern=neg, channel_multiplier=-cm),
        reads=[key], writes=[key], tag="rs2")



def build_l1(sh=SH, tch=512, env=None):
    nch = sh // tch
    if env is None:
        nc = bass.Bass("TRN2", target_bir_lowering=False)
        x = nc.dram_tensor("x", [D, sh], F32, kind="ExternalInput").ap()
        wq = nc.dram_tensor("wq", [D, D], F32, kind="ExternalInput").ap()
        wk = nc.dram_tensor("wk", [D, D], F32, kind="ExternalInput").ap()
        wv = nc.dram_tensor("wv", [D, D], F32, kind="ExternalInput").ap()
        projT = nc.dram_tensor("projT", [DH, M], F32, kind="ExternalInput").ap()
        o_qf = nc.dram_tensor("qf", [D, sh], BF16, kind="ExternalOutput").ap()
        o_ddk = nc.dram_tensor("ddk", [D, sh], F32, kind="ExternalOutput").ap()
        o_v = nc.dram_tensor("vv", [D, sh], BF16, kind="ExternalOutput").ap()
        o_mx = nc.dram_tensor("mxloc", [1, 1], F32, kind="ExternalOutput").ap()

        p = Prog(nc)
    else:
        nc, p = env["nc"], env["p"]
        x = env["x"]
        wq = env["wq"]
        wk = env["wk"]
        wv = env["wv"]
        projT = env["projT"]
        o_qf = env["o_qf"]
        o_ddk = env["o_ddk"]
        o_v = env["o_v"]
        o_mx = env["o_mx"]
    LNR = math.log(RATIO)

    wtmp = p.sb("wtmp", [128, 2 * D], F32)
    wqb = p.sb("wqb", [128, 8 * D], BF16)
    wkb = p.sb("wkb", [128, 8 * D], BF16)
    wvb = p.sb("wvb", [128, 8 * D], BF16)
    pjb = p.sb("pjb", [128, M], BF16)
    pjf = p.sb("pjf", [128, M], F32)
    sel16 = p.sb("sel16", [16, 8 * 128], F32)
    hsel = p.sb("hsel", [128, 8 * 16], F32)
    identf = p.sb("identf", [128, 128], F32)
    runmax = p.sb("runmax", [128, 1], F32)

    xf = [p.sb(f"xf{i}", [128, 8 * tch], F32) for i in range(2)]
    xb = [p.sb(f"xb{i}", [128, 8 * tch], BF16) for i in range(2)]
    qb = p.sb("qb", [128, 8 * tch], BF16)
    kb = p.sb("kb", [128, 8 * tch], BF16)
    ddq_sb = p.sb("ddq_sb", [128, 8 * tch], F32)
    ddk_sb = p.sb("ddk_sb", [128, 8 * tch], F32)
    zsq = [p.sb(f"zsq{i}", [128, tch], F32) for i in range(2)]
    mxtm = p.sb("mxtm", [128, 16], F32)
    mx_fm = p.sb("mx_fm", [16, tch], F32)
    sq_sb = p.sb("sq_sb", [16, tch], F32)
    sk_sb = p.sb("sk_sb", [16, tch], F32)
    qf_sb = p.sb("qf_sb", [128, 8 * tch], BF16)
    vb_sb = p.sb("vb_sb", [128, 8 * tch], BF16)
    ddko = p.sb("ddko", [128, 8 * tch], F32)
    dtmp = [p.sb(f"dtmp{i}", [128, tch], F32) for i in range(2)]
    mx1 = p.sb("mx1", [128, 1], F32)
    mx2 = p.sb("mx2", [1, 1], F32)

    pp = [p.ps(f"pp{i}", [128, tch], F32) for i in range(3)]
    pdiag = [p.ps("pdiag_q", [16, tch], F32), p.ps("pdiag_k", [16, tch], F32)]
    ptr = [pp[0][:, 0:128], pp[1][:, 0:128]]
    pbc = [pp[0], pp[1]]
    pmx = pp[2][0:16, 0:128]

    make_identity(p, identf[:], "identf")
    make_range_sel(p, sel16[:].rearrange("h (m p) -> h m p", p=128), "sel16",
                   -64, [[128, 8], [1, 128]], 0, 64)
    make_range_sel(p, hsel[:].rearrange("p (m j) -> p m j", j=16), "hsel",
                   1, [[128, 8], [-64, 16]], 0, 64, fill=0.5)
    ew(p, "vector", "memset", runmax[:], -3e38, writes=["runmax"], tag="rmi")

    for wnm, src, dst, scale in [("q", wq, wqb, DN), ("k", wk, wkb, DN),
                                 ("v", wv, wvb, 1.0)]:
        for g in range(4):
            p.dma("sync", wtmp[:].rearrange("p (k m) -> p k m", m=D),
                  src.rearrange("(k p) m -> p k m", p=128)[:, 2 * g:2 * g + 2, :],
                  writes=["wtmp"], tag=f"w_{wnm}{g}")
            act(p, dst[:, g * 2 * D:(g + 1) * 2 * D], wtmp[:], AF.Copy,
                scale=scale, reads=["wtmp"], writes=[f"w{wnm}b"],
                tag=f"wc_{wnm}{g}")
    p.dma("sync", pjf[0:64, :], projT, writes=["pjf"], tag="pj0")
    p.dma("sync", pjf[64:128, :], projT, reads=["pjf"], deps=(), tag="pj1",
          writes=["pjf2"])
    act(p, pjb[:], pjf[:], AF.Copy, reads=["pjf", "pjf2"], writes=["pjb"],
        tag="pjc")

    x_r = x.rearrange("(k p) t -> p k t", p=128)
    qf_r = o_qf.rearrange("(k p) t -> p k t", p=128)
    v_r = o_v.rearrange("(k p) t -> p k t", p=128)
    ddk_r = o_ddk.rearrange("(k p) t -> p k t", p=128)

    for c in range(nch):
        bi = c % 2
        s = slice(c * tch, (c + 1) * tch)
        p.dma("sync", xf[bi][:], x_r[:, :, s], writes=[f"xf{bi}"], tag=f"ldx{c}")
        act(p, xb[bi][:], xf[bi][:], AF.Copy, reads=[f"xf{bi}", "wvb"],
            writes=[f"xb{bi}"], tag=f"xc{c}")

        for mc in range(8):
            for wi, (wkey, wb, dkey, dst_bf) in enumerate([
                    ("wqb", wqb, "qb", qb), ("wkb", wkb, "kb", kb),
                    ("wvb", wvb, "vb", vb_sb)]):
                bank = pp[wi]
                mm_group(p, [
                    (bank[:], wb[:, k * D + mc * 128:k * D + mc * 128 + 128],
                     xb[bi][:, k * tch:(k + 1) * tch], k == 0, k == 7)
                    for k in range(8)],
                    reads=[wkey, f"xb{bi}"], writes=[f"pp{wi}"],
                    tag=f"prj{c}_{mc}_{wi}")
                act(p, dst_bf[:, mc * tch:(mc + 1) * tch], bank[:], AF.Copy,
                    reads=[f"pp{wi}"], writes=[dkey, f"{dkey}_{mc}"],
                    tag=f"cast{c}_{mc}_{wi}")
                if wi < 2:
                    act(p, zsq[wi][:], bank[:], AF.Square,
                        reads=[f"pp{wi}"], writes=[f"zsq{wi}"],
                        tag=f"sq{c}_{mc}_{wi}")
                    mm_group(p, [(pdiag[wi][:], hsel[:, mc * 16:mc * 16 + 16],
                                  zsq[wi][:], mc == 0, mc == 7)],
                             reads=[f"zsq{wi}", "hsel"], writes=[f"pdiag{wi}"],
                             tag=f"dg{c}_{mc}_{wi}")

        p.dma("sync", v_r[:, :, s], vb_sb[:], reads=["vb"], tag=f"stv{c}")

        for mc in range(8):
            for wi, (skey, src_bf, ddkey, dd_dst) in enumerate([
                    ("qb", qb, "ddq", ddq_sb), ("kb", kb, "ddk", ddk_sb)]):
                bank = pp[wi]
                mm_group(p, [
                    (bank[0:64, :], pjb[0:64, :],
                     src_bf[0:64, mc * tch:(mc + 1) * tch], True, True),
                    (bank[64:128, :], pjb[64:128, :],
                     src_bf[64:128, mc * tch:(mc + 1) * tch], True, True)],
                    reads=[f"{skey}_{mc}", "pjb"], writes=[f"pp{wi}"],
                    tag=f"dd{c}_{mc}_{wi}")
                ew(p, "vector", "tensor_copy", dd_dst[:, mc * tch:(mc + 1) * tch],
                   bank[:], reads=[f"pp{wi}"], writes=[f"{ddkey}_{mc}", ddkey],
                   tag=f"ddev{c}_{mc}_{wi}")
                if wi == 1:
                    ew(p, "vector", "reduce_max", mx1[:], bank[:], axis=AX.X,
                       reads=[f"pp{wi}"], writes=["mx1"], tag=f"rmx{c}_{mc}")
                    ew(p, "vector", "tensor_max", runmax[:], runmax[:], mx1[:],
                       reads=["mx1", "runmax"], writes=["runmax"],
                       tag=f"rma{c}_{mc}")

        for tb in range(tch // 128):
            for mc in range(8):
                pb = ptr[mc % 2]
                transpose(p, pb,
                          ddq_sb[:, mc * tch + tb * 128:mc * tch + tb * 128 + 128],
                          identf[:], reads=[f"ddq_{mc}", "identf"],
                          writes=[f"pp{mc % 2}"], tag=f"trq{c}_{tb}_{mc}")
                ew(p, "vector", "reduce_max", mxtm[:, mc * 2:mc * 2 + 2],
                   pb.rearrange("t (h m) -> t h m", m=64), axis=AX.X,
                   reads=[f"pp{mc % 2}"], writes=["mxtm"],
                   tag=f"redq{c}_{tb}_{mc}")
            transpose(p, pmx, mxtm[:], identf[:],
                      reads=["mxtm", "identf"], writes=["pp2"],
                      tag=f"trmx{c}_{tb}")
            ew(p, "vector", "tensor_copy", mx_fm[:, tb * 128:(tb + 1) * 128],
               pmx, reads=["pp2"], writes=["mx_fm"], tag=f"mxev{c}_{tb}")

        ew(p, "vector", "tensor_add", sq_sb[:], pdiag[0][:], mx_fm[:],
           reads=["pdiag0", "mx_fm"], writes=["sq_sb"], tag=f"sqc{c}")
        ew(p, "vector", "tensor_copy", sk_sb[:], pdiag[1][:],
           reads=["pdiag1"], writes=["sk_sb"], tag=f"skc{c}")

        for mc in range(8):
            pb = pbc[mc % 2]
            pkey = f"pp{mc % 2}"
            mm_group(p, [(pb[:], sel16[:, mc * 128:(mc + 1) * 128], sq_sb[:],
                          True, True)],
                     reads=["sq_sb", "sel16"], writes=[pkey], tag=f"bcq{c}_{mc}")
            ew(p, "vector", "tensor_sub", dtmp[mc % 2][:],
               ddq_sb[:, mc * tch:(mc + 1) * tch], pb[:],
               reads=[f"ddq_{mc}", pkey], writes=[f"dtmp{mc % 2}"],
               tag=f"qfs{c}_{mc}")
            lnr_ap, lnr_key = p.constcol(LNR)
            act(p, qf_sb[:, mc * tch:(mc + 1) * tch], dtmp[mc % 2][:], AF.Exp,
                bias=lnr_ap, reads=[f"dtmp{mc % 2}", lnr_key], writes=["qf_sb"],
                tag=f"qfe{c}_{mc}")
            mm_group(p, [(pb[:], sel16[:, mc * 128:(mc + 1) * 128], sk_sb[:],
                          True, True)],
                     reads=["sk_sb", "sel16"], writes=[pkey], tag=f"bck{c}_{mc}")
            ew(p, "vector", "tensor_sub", ddko[:, mc * tch:(mc + 1) * tch],
               ddk_sb[:, mc * tch:(mc + 1) * tch], pb[:],
               reads=[f"ddk_{mc}", pkey], writes=["ddko"], tag=f"dks{c}_{mc}")

        p.dma("sync", qf_r[:, :, s], qf_sb[:], reads=["qf_sb"], tag=f"stqf{c}")
        p.dma("sync", ddk_r[:, :, s], ddko[:], reads=["ddko"], tag=f"stddk{c}")

    transpose(p, pmx[0:1, :], runmax[:], identf[:],
              reads=["runmax", "identf"], writes=["pp2"], tag="mxT")
    ew(p, "vector", "reduce_max", mx2[:], pmx[0:1, :], axis=AX.X,
       reads=["pp2"], writes=["mx2"], tag="mxR")
    p.dma("sync", o_mx, mx2[:], reads=["mx2"], tag="stmx")

    if env is None:
        p.final_wait("sync")
        p.build()
        p.finish()
        return nc



def build_l2(sh=SH, tch=512, debug=False, parts=7, env=None):
    nch = sh // tch
    if env is None:
        nc = bass.Bass("TRN2", target_bir_lowering=False)
        ddk = nc.dram_tensor("ddk", [D, sh], F32, kind="ExternalInput").ap()
        vv = nc.dram_tensor("vv", [D, sh], BF16, kind="ExternalInput").ap()
        mxall = nc.dram_tensor("mxall", [8, 1], F32, kind="ExternalInput").ap()
        o_kv = nc.dram_tensor("kvp", [128, 512], F32, kind="ExternalOutput").ap()
        o_kfs = nc.dram_tensor("kfsp", [128, 8], F32, kind="ExternalOutput").ap()
        o_vs = nc.dram_tensor("vsp", [1, 1024], F32, kind="ExternalOutput").ap()

        p = Prog(nc)
    else:
        nc, p = env["nc"], env["p"]
        ddk = env["ddk"]
        vv = env["vv"]
        mxall = env["mxall"]
        o_kv = env["o_kv"]
        o_kfs = env["o_kfs"]
        o_vs = env["o_vs"]
    LNR = math.log(RATIO)

    identb = p.sb("identb", [128, 128], BF16)
    mx8 = p.sb("mx8", [1, 8], F32)
    bias1 = p.sb("bias1", [1, 1], F32)
    bias128 = p.sb("bias128", [128, 1], F32)
    ones1f = p.sb("ones1f", [1, 128], F32)
    onesb = p.sb("onesb", [128, 1], BF16)
    ddf = [p.sb(f"ddf{i}", [128, 8 * tch], F32) for i in range(2)]
    vbf = [p.sb(f"vbf{i}", [128, 8 * tch], BF16) for i in range(2)]
    kfb = [p.sb(f"kfb{i}", [128, 8 * tch], BF16) for i in range(2)]
    ktm = p.sb("ktm", [128, 128], BF16)
    vtm = p.sb("vtm", [128, 128], BF16)
    kvev = p.sb("kvev", [128, 512], F32)
    kfsev = p.sb("kfsev", [128, 8], F32)
    vsev = p.sb("vsev", [1, 1024], F32)

    pkv = p.ps("pkv", [128, 512], F32)
    pkfs = p.ps("pkfs", [128, 8], F32)
    pvs = [p.ps(f"pvs{i}", [1, 512], F32) for i in range(2)]
    ptk = p.ps("ptk", [128, 128], BF16)
    ptv = p.ps("ptv", [128, 128], BF16)
    pb1 = p.ps("pb1", [128, 1], F32)

    make_identity(p, identb[:], "identb")
    ew(p, "gpsimd", "memset", ones1f[:], 1.0, writes=["ones1f"], tag="o1")
    ew(p, "gpsimd", "memset", onesb[:], 1.0, writes=["onesb"], tag="o2")

    p.dma("sync", mx8[:], mxall.rearrange("a b -> b a"), writes=["mx8"], tag="ldmx")
    ew(p, "vector", "reduce_max", bias1[:], mx8[:], axis=AX.X,
       reads=["mx8"], writes=["bias1"], tag="redmx")
    ew(p, "vector", "tensor_scalar", bias1[:], bias1[:], -1.0, LNR,
       ALU.mult, ALU.add, reads=["bias1"], writes=["bias1"], tag="biasc")
    mm_group(p, [(pb1[:], ones1f[:], bias1[:], True, True)],
             reads=["ones1f", "bias1"], writes=["pb1"], tag="biasbc")
    ew(p, "vector", "tensor_copy", bias128[:], pb1[:], reads=["pb1"],
       writes=["bias128"], tag="biasev")

    ddk_r = ddk.rearrange("(k p) t -> p k t", p=128)
    vv_r = vv.rearrange("(k p) t -> p k t", p=128)
    if debug:
        o_kf = nc.dram_tensor("kfdbg", [128, 8 * tch], BF16, kind="ExternalOutput").ap()
        o_ktm = nc.dram_tensor("ktmdbg", [128, 128], BF16, kind="ExternalOutput").ap()

    nmm = 0
    for c in range(nch):
        bi = c % 2
        s = slice(c * tch, (c + 1) * tch)
        p.dma("sync", ddf[bi][:], ddk_r[:, :, s], writes=[f"ddf{bi}"], tag=f"lddd{c}")
        p.dma("sync", vbf[bi][:], vv_r[:, :, s], writes=[f"vbf{bi}"], tag=f"ldv{c}")
        act(p, kfb[bi][:], ddf[bi][:], AF.Exp, bias=bias128[:, 0:1],
            reads=[f"ddf{bi}", "bias128"], writes=[f"kfb{bi}"], tag=f"kfe{c}")
        if debug and c == 0:
            p.dma("sync", o_kf, kfb[bi][:], reads=[f"kfb{bi}"], tag="dbgkf")

        for mc in range(8):
            for tb in range(tch // 128):
                col = mc * tch + tb * 128
                transpose(p, ptk[:], kfb[bi][:, col:col + 128], identb[:],
                          reads=[f"kfb{bi}", "identb"], writes=["ptk"],
                          tag=f"trk{c}_{mc}_{tb}")
                transpose(p, ptv[:], vbf[bi][:, col:col + 128], identb[:],
                          reads=[f"vbf{bi}", "identb"], writes=["ptv"],
                          tag=f"trv{c}_{mc}_{tb}")
                ew(p, "vector", "tensor_copy", ktm[:], ptk[:], reads=["ptk"],
                   writes=["ktm"], tag=f"ektm{c}_{mc}_{tb}")
                if debug and c == 0 and mc == 0 and tb == 0:
                    p.dma("sync", o_ktm, ktm[:], reads=["ktm"], tag="dbgktm")
                ew(p, "vector", "tensor_copy", vtm[:], ptv[:], reads=["ptv"],
                   writes=["vtm"], tag=f"evtm{c}_{mc}_{tb}")
                first = (c == 0 and tb == 0 and mc == 0)
                trip = []
                for par in range(2):
                    if parts & 1:
                        trip.append((pkv[par * 64:par * 64 + 64, mc * 64:mc * 64 + 64],
                                     ktm[:, par * 64:par * 64 + 64],
                                     vtm[:, par * 64:par * 64 + 64], first, False))
                    if parts & 2:
                        trip.append((pkfs[par * 64:par * 64 + 64, mc:mc + 1],
                                     ktm[:, par * 64:par * 64 + 64],
                                     onesb[:], first, False))
                vsb = (mc * 128 // 512) % 2
                if parts & 4:
                    trip.append((pvs[vsb][:, (mc * 128) % 512:(mc * 128) % 512 + 128],
                                 onesb[:], vtm[:],
                                 (c == 0 and tb == 0 and mc % 4 == 0), False))
                mm_group(p, trip, reads=["ktm", "vtm", "onesb"],
                         writes=["pkv_acc"], tag=f"kv{c}_{mc}_{tb}",
                         skip_group_check=True)
                nmm += 1

    ew(p, "vector", "tensor_copy", kvev[:], pkv[:], reads=["pkv_acc"],
       writes=["kvev"], tag="evkv")
    ew(p, "vector", "tensor_copy", kfsev[:], pkfs[:], reads=["pkv_acc"],
       writes=["kfsev"], tag="evkfs")
    ew(p, "vector", "tensor_copy", vsev[:, 0:512], pvs[0][:], reads=["pkv_acc"],
       writes=["vsev"], tag="evvs0")
    ew(p, "vector", "tensor_copy", vsev[:, 512:1024], pvs[1][:], reads=["pkv_acc"],
       writes=["vsev2"], tag="evvs1")
    p.dma("sync", o_kv, kvev[:], reads=["kvev"], tag="stkv")
    p.dma("sync", o_kfs, kfsev[:], reads=["kfsev"], tag="stkfs")
    p.dma("sync", o_vs, vsev[:], reads=["vsev", "vsev2"], tag="stvs")

    if env is None:
        p.final_wait("sync")
        p.build()
        p.finish()
        return nc


def ln_fm(p, pref, z, zkey, dst, dstkey, aux, eps, nsl=8, tch=512,
          gamma=None, beta=None, gkey=None, dst_bf=None, dstbfkey=None,
          wid=None):
    K = aux["keys"]
    nfeat = float(nsl * 128)
    w = tch if wid is None else wid
    for oc in range(nsl):
        sl = slice(oc * tch, oc * tch + w)
        act(p, aux["zsq"][:, 0:w], z[:, sl], AF.Square,
            reads=[zkey], writes=[K["zsq"]], tag=f"{pref}_sq{oc}")
        mm_group(p, [(aux["ps1"][:, 0:w], aux["ones128"][:], z[:, sl],
                      oc == 0, oc == nsl - 1)],
                 reads=[zkey, K["ones128"]], writes=[K["ps1"]],
                 tag=f"{pref}_s1_{oc}", skip_group_check=True)
        mm_group(p, [(aux["ps2"][:, 0:w], aux["ones128"][:], aux["zsq"][:, 0:w],
                      oc == 0, oc == nsl - 1)],
                 reads=[K["zsq"], K["ones128"]], writes=[K["ps2"]],
                 tag=f"{pref}_s2_{oc}", skip_group_check=True)
    ew(p, "vector", "tensor_scalar_mul", aux["mean"][:, 0:w], aux["ps1"][:, 0:w],
       1.0 / nfeat, reads=[K["ps1"]], writes=[K["mean"]], tag=f"{pref}_mean")
    ew(p, "vector", "tensor_mul", aux["msq"][:, 0:w], aux["mean"][:, 0:w],
       aux["mean"][:, 0:w],
       reads=[K["mean"]], writes=[K["msq"]], tag=f"{pref}_msq")
    ew(p, "vector", "scalar_tensor_tensor", aux["var"][:, 0:w], aux["ps2"][:, 0:w],
       1.0 / nfeat, aux["msq"][:, 0:w], ALU.mult, ALU.subtract,
       reads=[K["ps2"], K["msq"]], writes=[K["var"]], tag=f"{pref}_var")
    act(p, aux["lnv"][:, 0:w], aux["var"][:, 0:w], AF.Ln, bias=aux["epscol"][0:1, :],
        reads=[K["var"], K["epscol"]], writes=[K["lnv"]], tag=f"{pref}_ln")
    act(p, aux["rstd"][:, 0:w], aux["lnv"][:, 0:w], AF.Exp, scale=-0.5,
        reads=[K["lnv"]], writes=[K["rstd"]], tag=f"{pref}_rstd")
    mm_group(p, [(aux["pbm"][:, 0:w], aux["ones1r"][:], aux["mean"][:, 0:w],
                  True, True)],
             reads=[K["mean"], K["ones1r"]], writes=[K["pbm"]],
             tag=f"{pref}_bcm")
    mm_group(p, [(aux["pba"][:, 0:w], aux["ones1r"][:], aux["rstd"][:, 0:w],
                  True, True)],
             reads=[K["rstd"], K["ones1r"]], writes=[K["pba"]],
             tag=f"{pref}_bca")
    for oc in range(nsl):
        sl = slice(oc * tch, oc * tch + w)
        ew(p, "vector", "tensor_sub", aux["utile"][:, 0:w], z[:, sl],
           aux["pbm"][:, 0:w],
           reads=[zkey, K["pbm"]], writes=[K["utile"]], tag=f"{pref}_u{oc}")
        if gamma is None:
            ew(p, "vector", "tensor_mul", dst[:, sl], aux["utile"][:, 0:w],
               aux["pba"][:, 0:w], reads=[K["utile"], K["pba"]], writes=[dstkey],
               tag=f"{pref}_w{oc}")
        else:
            ew(p, "vector", "tensor_mul", aux["utile"][:, 0:w],
               aux["utile"][:, 0:w],
               aux["pba"][:, 0:w], reads=[K["utile"], K["pba"]], writes=[K["utile"]],
               tag=f"{pref}_w{oc}")
            ew(p, "vector", "tensor_scalar", dst[:, sl], aux["utile"][:, 0:w],
               gamma[:, oc:oc + 1], beta[:, oc:oc + 1], ALU.mult, ALU.add,
               reads=[K["utile"], gkey], writes=[dstkey], tag=f"{pref}_g{oc}")
        if dst_bf is not None:
            act(p, dst_bf[:, sl], dst[:, sl], AF.Copy,
                reads=[dstkey], writes=[dstbfkey], tag=f"{pref}_bf{oc}")


def make_ln_aux(p, pref, tch=512):
    aux = dict(
        ones128=p.sb(f"{pref}_o128", [128, 1], F32),
        ones1r=p.sb(f"{pref}_o1r", [1, 128], F32),
        zsq=p.sb(f"{pref}_zsq", [128, tch], F32),
        mean=p.sb(f"{pref}_mean", [1, tch], F32),
        msq=p.sb(f"{pref}_msq", [1, tch], F32),
        var=p.sb(f"{pref}_var", [1, tch], F32),
        lnv=p.sb(f"{pref}_lnv", [1, tch], F32),
        rstd=p.sb(f"{pref}_rstd", [1, tch], F32),
    )
    keys = {k: f"{pref}_{k}" for k in
            ("ones128", "ones1r", "zsq", "mean", "msq", "var", "lnv", "rstd",
             "ps1", "ps2", "pbm", "pba", "utile", "epscol")}
    aux["keys"] = keys
    ew(p, "gpsimd", "memset", aux["ones128"][:], 1.0, writes=[keys["ones128"]],
       tag=f"{pref}i1")
    ew(p, "gpsimd", "memset", aux["ones1r"][:], 1.0, writes=[keys["ones1r"]],
       tag=f"{pref}i2")
    return aux



def build_l3(sh=SH, tch=512, ttok=N, env=None):
    nch = sh // tch
    if env is None:
        nc = bass.Bass("TRN2", target_bir_lowering=False)
        x = nc.dram_tensor("x", [D, sh], F32, kind="ExternalInput").ap()
        qf = nc.dram_tensor("qf", [D, sh], BF16, kind="ExternalInput").ap()
        kvp4 = nc.dram_tensor("kvp4", [4, 128, 512], F32, kind="ExternalInput").ap()
        kfsp4 = nc.dram_tensor("kfsp4", [4, 128, 8], F32, kind="ExternalInput").ap()
        vsp4 = nc.dram_tensor("vsp4", [4, 1, 1024], F32, kind="ExternalInput").ap()
        wo = nc.dram_tensor("wo", [D, D], F32, kind="ExternalInput").ap()
        w1 = nc.dram_tensor("w1", [D, DFF], F32, kind="ExternalInput").ap()
        w2 = nc.dram_tensor("w2", [DFF, D], F32, kind="ExternalInput").ap()
        b1c = nc.dram_tensor("b1c", [128, 32], F32, kind="ExternalInput").ap()
        b2c = nc.dram_tensor("b2c", [128, 8], F32, kind="ExternalInput").ap()
        lng = {}
        for nm in ("ln1g", "ln1b", "ln2g", "ln2b"):
            lng[nm] = nc.dram_tensor(nm, [128, 8], F32, kind="ExternalInput").ap()
        o_x1 = nc.dram_tensor("x1", [D, sh], F32, kind="ExternalOutput").ap()

        p = Prog(nc)
    else:
        nc, p = env["nc"], env["p"]
        x = env["x"]
        qf = env["qf"]
        kvp4 = env["kvp4"]
        kfsp4 = env["kfsp4"]
        vsp4 = env["vsp4"]
        wo = env["wo"]
        w1 = env["w1"]
        w2 = env["w2"]
        b1c = env["b1c"]
        b2c = env["b2c"]
        lng = env["lng"]
        o_x1 = env["o_x1"]
        mxg4 = env.get("mxg4")
        mx8 = env.get("mx8")
        x_h = env.get("x_h")
        qf_h = env.get("qf_h")
        mask1 = env.get("mask1")
        o_x1h = env.get("o_x1h")
    if env is None:
        mxg4 = mx8 = x_h = qf_h = mask1 = o_x1h = None
    w1b = nc.dram_tensor("w1b_scr", [128, 8 * DFF], BF16, kind="Internal").ap()
    w2b = nc.dram_tensor("w2b_scr", [128, 32 * D], BF16, kind="Internal").ap()
    CEPS = RATIO * EPS_KERN

    wob = p.sb("wob", [128, 8 * D], BF16)
    sel16 = p.sb("sel16", [16, 8 * 128], F32)
    kvs = p.sb("kvs", [128, 4 * 512], F32)
    kfss = p.sb("kfss", [128, 4 * 8], F32)
    vss = p.sb("vss", [1, 4 * 1024], F32)
    kvb = p.sb("kvb", [128, 512], BF16)
    denv = p.sb("denv", [128, 8], F32)
    zall = p.sb("zall", [128, 8 * 16], BF16)
    zallf = p.sb("zallf", [128, 8 * 16], F32)
    denc = p.sb("denc", [16, 1], F32)
    cepsc = p.sb("cepsc", [128, 1], BF16)
    rden = p.sb("rden", [16, tch], F32)
    denf = p.sb("denf", [16, tch], F32)
    xc = [p.sb("xc0", [128, 8 * tch], F32)] * 2
    qfc = [p.sb("qfc0", [128, 8 * tch], BF16)] * 2
    qsb = p.sb("qsb", [128, 8 * tch], BF16)
    attnb = p.sb("attnb", [128, 8 * tch], BF16)
    z1 = p.sb("z1", [128, 8 * tch], F32)
    out1f = p.sb("out1f", [128, 8 * tch], F32)
    out1b = p.sb("out1b", [128, 8 * tch], BF16)
    hb = p.sb("hb", [128, 32 * tch], BF16)
    xbtmp = p.sb("xbtmp", [128, 2 * tch], F32)
    etmp = p.sb("etmp", [128, 2 * tch], F32)
    htmp = p.sb("htmp", [128, tch], F32)
    w1t = [p.sb(f"w1t{i}", [128, 8 * 128], BF16) for i in range(2)]
    w2t = [p.sb("w2t0", [128, 32 * 128], BF16)] * 2
    aux = make_ln_aux(p, "ln", tch)
    wtmp = z1
    wtmpb = hb[:, 0:DFF]

    pden = p.ps("pden", [16, tch], F32)
    pA = p.ps("pA", [128, tch], F32)
    pB = p.ps("pB", [128, tch], F32)
    pC = p.ps("pC", [128, tch], F32)
    pD = p.ps("pD", [128, tch], F32)
    ps1 = p.ps("ps1", [1, tch], F32)
    ps2 = p.ps("ps2", [1, tch], F32)
    aux.update(ps1=ps1, ps2=ps2, pbm=pA, pba=pB, utile=htmp)
    aux["keys"].update(ps1="ps1", ps2="ps2", pbm="pA", pba="pB", utile="htmp",
                       epscol="epsc")

    make_range_sel(p, sel16[:].rearrange("h (m p) -> h m p", p=128), "sel16",
                   -64, [[128, 8], [1, 128]], 0, 64)
    ec1, eck1 = p.constcol(EPS_LN)
    ec2, eck2 = p.constcol(EPS_LN_OUTER)

    p.dma("sync", kvs[:].rearrange("p (g t) -> p g t", g=4),
          kvp4.rearrange("g p t -> p g t"), writes=["kvs"], tag="ldkv")
    p.dma("sync", kfss[:].rearrange("p (g t) -> p g t", g=4),
          kfsp4.rearrange("g p t -> p g t"), writes=["kfss"], tag="ldkfs")
    p.dma("sync", vss[:].rearrange("p (g t) -> p g t", g=4),
          vsp4.rearrange("g p t -> p g t"), writes=["vss"], tag="ldvs")
    if mxg4 is not None:
        mx8s = p.sb("mx8s", [1, 8], F32)
        mxg4s = p.sb("mxg4s", [1, 4], F32)
        nmx = p.sb("nmx", [1, 1], F32)
        arow = p.sb("arow", [1, 4], F32)
        acol = p.sb("acol", [128, 4], F32)
        p.dma("sync", mx8s[:], mx8, writes=["mx8s"], tag="lmx8")
        p.dma("sync", mxg4s[:], mxg4, writes=["mxg4s"], tag="lmxg4")
        ew(p, "vector", "reduce_max", nmx[:], mx8s[:], axis=AX.X,
           reads=["mx8s"], writes=["nmx"], tag="gmax")
        ew(p, "vector", "tensor_scalar_mul", nmx[:], nmx[:], -1.0,
           reads=["nmx"], writes=["nmx"], tag="negmax")
        act(p, arow[:], mxg4s[:], AF.Exp, bias=nmx[0:1, 0:1],
            reads=["mxg4s", "nmx"], writes=["arow"], tag="alpha")
        mm_group(p, [(pC[0:128, 0:4], aux["ones1r"][:], arow[:], True, True)],
                 reads=["arow", "ln_ones1r"], writes=["pC"], tag="abc")
        ew(p, "vector", "tensor_copy", acol[:], pC[0:128, 0:4], reads=["pC"],
           writes=["acol"], tag="aev")
        ew(p, "vector", "tensor_scalar_mul", kvs[:, 0:512], kvs[:, 0:512],
           acol[:, 0:1], reads=["kvs", "acol"], writes=["kvs"], tag="akv0")
        ew(p, "vector", "tensor_scalar_mul", kfss[:, 0:8], kfss[:, 0:8],
           acol[:, 0:1], reads=["kfss", "acol"], writes=["kfss"], tag="akf0")
        for i in range(1, 4):
            ew(p, "vector", "scalar_tensor_tensor", kvs[:, 0:512],
               kvs[:, i * 512:(i + 1) * 512], acol[:, i:i + 1], kvs[:, 0:512],
               ALU.mult, ALU.add, reads=["kvs", "acol"], writes=["kvs"],
               tag=f"akv{i}")
            ew(p, "vector", "scalar_tensor_tensor", kfss[:, 0:8],
               kfss[:, i * 8:(i + 1) * 8], acol[:, i:i + 1], kfss[:, 0:8],
               ALU.mult, ALU.add, reads=["kfss", "acol"], writes=["kfss"],
               tag=f"akf{i}")
            ew(p, "vector", "tensor_add", vss[:, 0:1024], vss[:, 0:1024],
               vss[:, i * 1024:(i + 1) * 1024], reads=["vss"], writes=["vss"],
               tag=f"svs{i}")
    else:
        for i in range(1, 4):
            ew(p, "vector", "tensor_add", kvs[:, 0:512], kvs[:, 0:512],
               kvs[:, i * 512:(i + 1) * 512], reads=["kvs"], writes=["kvs"],
               tag=f"skv{i}")
            ew(p, "vector", "tensor_add", kfss[:, 0:8], kfss[:, 0:8],
               kfss[:, i * 8:(i + 1) * 8], reads=["kfss"], writes=["kfss"],
               tag=f"skfs{i}")
            ew(p, "vector", "tensor_add", vss[:, 0:1024], vss[:, 0:1024],
               vss[:, i * 1024:(i + 1) * 1024], reads=["vss"], writes=["vss"],
               tag=f"svs{i}")
    trip = []
    for h in range(16):
        trip.append((pC[(h % 2) * 64:(h % 2) * 64 + 64,
                        (h // 2) * 64:(h // 2) * 64 + 64],
                     aux["ones1r"][:, 0:64],
                     vss[0:1, h * 64:(h + 1) * 64], h < 2, False))
    mm_group(p, trip, reads=["vss", "ln_ones1r"], writes=["pC"], tag="kvc",
             skip_group_check=True)
    ew(p, "vector", "scalar_tensor_tensor", kvs[:, 0:512], pC[:], CEPS,
       kvs[:, 0:512], ALU.mult, ALU.add, reads=["pC", "kvs"], writes=["kvs"],
       tag="kvcorr")
    act(p, kvb[:], kvs[:, 0:512], AF.Copy, reads=["kvs"], writes=["kvb"],
        tag="kvcast")
    ew(p, "vector", "tensor_scalar_add", denv[:], kfss[:, 0:8],
       CEPS * float(ttok), reads=["kfss"], writes=["denv"], tag="denv")
    ew(p, "gpsimd", "memset", zallf[:], 0.0, writes=["zallf"], tag="z0")
    for h in range(16):
        ew(p, "vector", "tensor_copy",
           zallf[(h % 2) * 64:(h % 2) * 64 + 64, (h // 2) * 16 + h:(h // 2) * 16 + h + 1],
           denv[(h % 2) * 64:(h % 2) * 64 + 64, h // 2:h // 2 + 1],
           reads=["denv", "zallf"], writes=["zallf"], tag=f"zc{h}")
    act(p, zall[:], zallf[:], AF.Copy, reads=["zallf"], writes=["zall"],
        tag="zcast")
    ew(p, "gpsimd", "memset", cepsc[:], 1.0, writes=["cepsc"], tag="cep")
    trip = [(pden[:, 0:1], zall[:, mc * 16:(mc + 1) * 16], cepsc[:],
             mc == 0, mc == 7) for mc in range(8)]
    mm_group(p, trip, reads=["zall", "cepsc"], writes=["pden"], tag="denc")
    ew(p, "vector", "tensor_scalar_mul", denc[:], pden[:, 0:1], CEPS,
       reads=["pden"], writes=["denc"], tag="dencev")
    for g in range(2):
        p.dma("sync", wtmp[:].rearrange("p (k m) -> p k m", m=D),
              wo.rearrange("(k p) m -> p k m", p=128)[:, 4 * g:4 * g + 4, :],
              writes=["wtmp"], tag=f"wo{g}")
        act(p, wob[:, g * 4 * D:(g + 1) * 4 * D], wtmp[:], AF.Copy,
            reads=["wtmp"], writes=["wob"], tag=f"woc{g}")
    for g in range(8):
        p.dma("sync", wtmp[:],
              w1.rearrange("(k p) m -> p k m", p=128)[:, g, :],
              writes=["wtmp"], tag=f"w1l{g}")
        act(p, wtmpb, wtmp[:], AF.Copy, reads=["wtmp"], writes=["wtmpb"],
            tag=f"w1c{g}")
        p.dma("sync", w1b[:, g * DFF:(g + 1) * DFF], wtmpb,
              reads=["wtmpb"], writes=["w1b_hbm"], tag=f"w1s{g}")
    for g in range(8):
        p.dma("sync", wtmp[:].rearrange("p (k m) -> p k m", m=D),
              w2.rearrange("(k p) m -> p k m", p=128)[:, 4 * g:4 * g + 4, :],
              writes=["wtmp"], tag=f"w2l{g}")
        act(p, wtmpb, wtmp[:], AF.Copy, reads=["wtmp"],
            writes=["wtmpb"], tag=f"w2c{g}")
        p.dma("sync", w2b[:, g * 4 * D:(g + 1) * 4 * D], wtmpb,
              reads=["wtmpb"], writes=["w2b_hbm"], tag=f"w2s{g}")

    b1s = p.sb("b1s", [128, 32], F32)
    b2s = p.sb("b2s", [128, 8], F32)
    lngs = {nm: p.sb(nm + "s", [128, 8], F32) for nm in lng}
    p.dma("sync", b1s[:], b1c, writes=["b1s"], tag="ldb1")
    p.dma("sync", b2s[:], b2c, writes=["b2s"], tag="ldb2")
    for nm in lng:
        p.dma("sync", lngs[nm][:], lng[nm], writes=[nm], tag="ld" + nm)

    x_r = x.rearrange("(k p) t -> p k t", p=128)
    qf_r = qf.rearrange("(k p) t -> p k t", p=128)
    x1_r = o_x1.rearrange("(k p) t -> p k t", p=128)

    for c in range(nch):
        bi = c % 2
        s = slice(c * tch, (c + 1) * tch)
        p.dma("sync", xc[bi][:], x_r[:, :, s], writes=["xc0"], tag=f"lx{c}")
        p.dma("sync", qfc[bi][:], qf_r[:, :, s], writes=["qfc0"], tag=f"lq{c}")

        trip = [(pden[:], zall[:, mc * 16:(mc + 1) * 16],
                 qfc[bi][:, mc * tch:(mc + 1) * tch], mc == 0, mc == 7)
                for mc in range(8)]
        mm_group(p, trip, reads=["zall", "qfc0"], writes=["pden"],
                 tag=f"den{c}")
        ew(p, "vector", "tensor_scalar", denf[:], pden[:], denc[:, 0:1], None,
           ALU.add, reads=["pden", "denc"], writes=["denf"], tag=f"denf{c}")
        ew(p, "vector", "reciprocal", rden[:], denf[:], reads=["denf"],
           writes=["rden"], tag=f"rden{c}")

        for mc in range(8):
            pb = pA if mc % 2 == 0 else pB
            pbk = "pA" if mc % 2 == 0 else "pB"
            mm_group(p, [(pb[:], sel16[:, mc * 128:(mc + 1) * 128], rden[:],
                          True, True)],
                     reads=["rden", "sel16"], writes=[pbk], tag=f"rb{c}_{mc}")
            ew(p, "vector", "scalar_tensor_tensor",
               qsb[:, mc * tch:(mc + 1) * tch],
               qfc[bi][:, mc * tch:(mc + 1) * tch], CEPS, pb[:],
               ALU.add, ALU.mult, reads=["qfc0", pbk],
               writes=[f"qsb_{mc}", "qsb"], tag=f"qs{c}_{mc}")
        for mc in range(8):
            pb = pC if mc % 2 == 0 else pD
            pbk = "pC" if mc % 2 == 0 else "pD"
            trip = []
            for par in range(2):
                trip.append((pb[par * 64:par * 64 + 64, :],
                             kvb[par * 64:par * 64 + 64, mc * 64:mc * 64 + 64],
                             qsb[par * 64:par * 64 + 64, mc * tch:(mc + 1) * tch],
                             True, True))
            mm_group(p, trip, reads=["kvb", f"qsb_{mc}"], writes=[pbk],
                     tag=f"at{c}_{mc}")
            act(p, attnb[:, mc * tch:(mc + 1) * tch], pb[:], AF.Copy,
                reads=[pbk], writes=["attnb"], tag=f"atc{c}_{mc}")

        for oc in range(8):
            pb = pC if oc % 2 == 0 else pD
            pbk = "pC" if oc % 2 == 0 else "pD"
            trip = [(pb[:], wob[:, k * D + oc * 128:k * D + oc * 128 + 128],
                     attnb[:, k * tch:(k + 1) * tch], k == 0, k == 7)
                    for k in range(8)]
            mm_group(p, trip, reads=["wob", "attnb"], writes=[pbk],
                     tag=f"ao{c}_{oc}")
            ew(p, "vector", "tensor_add", z1[:, oc * tch:(oc + 1) * tch],
               xc[bi][:, oc * tch:(oc + 1) * tch], pb[:],
               reads=["xc0", pbk], writes=["z1"], tag=f"z1{c}_{oc}")

        aux["epscol"] = ec1
        aux["keys"]["epscol"] = eck1
        ln_fm(p, f"ln1_{c}", z1, "z1", out1f, "out1f", aux, EPS_LN,
              gamma=lngs["ln1g"], beta=lngs["ln1b"], gkey="ln1g",
              dst_bf=out1b, dstbfkey="out1b")

        for fc in range(32):
            wi = fc % 2
            p.dma("gpsimd", w1t[wi][:].rearrange("p (k f) -> p k f", f=128),
                  w1b.rearrange("p (k f) -> p k f", f=DFF)[:, :, fc * 128:(fc + 1) * 128],
                  reads=["w1b_hbm"], writes=[f"w1t{wi}"], tag=f"lw1_{c}_{fc}")
            pb = pC if fc % 2 == 0 else pD
            pbk = "pC" if fc % 2 == 0 else "pD"
            trip = [(pb[:], w1t[wi][:, k * 128:(k + 1) * 128],
                     out1b[:, k * tch:(k + 1) * tch], k == 0, k == 7)
                    for k in range(8)]
            mm_group(p, trip, reads=[f"w1t{wi}", "out1b"], writes=[pbk],
                     tag=f"f1{c}_{fc}")
            bsl = slice((fc % 2) * tch, (fc % 2) * tch + tch)
            xk_, ek_ = f"xbt{fc % 2}", f"ebt{fc % 2}"
            ew(p, "vector", "tensor_scalar", xbtmp[:, bsl], pb[:],
               b1s[:, fc:fc + 1], None, ALU.add, reads=[pbk, "b1s"],
               writes=[xk_], tag=f"xb{c}_{fc}")
            ew(p, "vector", "tensor_scalar_min", etmp[:, bsl], xbtmp[:, bsl], 0.0,
               reads=[xk_], writes=[ek_], tag=f"xn{c}_{fc}")
            act(p, etmp[:, bsl], etmp[:, bsl], AF.Exp, reads=[ek_], writes=[ek_],
                tag=f"ex{c}_{fc}")
            ew(p, "vector", "scalar_tensor_tensor", etmp[:, bsl], xbtmp[:, bsl],
               0.0, etmp[:, bsl], ALU.max, ALU.add, reads=[xk_, ek_],
               writes=[ek_], tag=f"hm{c}_{fc}")
            ew(p, "vector", "tensor_scalar_add", hb[:, fc * tch:(fc + 1) * tch],
               etmp[:, bsl], -1.0, reads=[ek_], writes=["hb", f"hb_{fc}"],
               tag=f"hc{c}_{fc}")

        for oc in range(8):
            wi = oc % 2
            p.dma("gpsimd", w2t[wi][:].rearrange("p (k f) -> p k f", f=128),
                  w2b.rearrange("p (k f) -> p k f", f=D)[:, :, oc * 128:(oc + 1) * 128],
                  reads=["w2b_hbm"], writes=["w2t0"], tag=f"lw2_{c}_{oc}")
            pb = pC if oc % 2 == 0 else pD
            pbk = "pC" if oc % 2 == 0 else "pD"
            trip = [(pb[:], w2t[wi][:, k * 128:(k + 1) * 128],
                     hb[:, k * tch:(k + 1) * tch], k == 0, k == 31)
                    for k in range(32)]
            mm_group(p, trip, reads=["w2t0", "hb"], writes=[pbk],
                     tag=f"f2{c}_{oc}")
            ew(p, "vector", "scalar_tensor_tensor",
               z1[:, oc * tch:(oc + 1) * tch], pb[:], b2s[:, oc:oc + 1],
               out1f[:, oc * tch:(oc + 1) * tch], ALU.add, ALU.add,
               reads=[pbk, "b2s", "out1f"], writes=["z1"], tag=f"z2{c}_{oc}")
        ln_fm(p, f"ln2_{c}", z1, "z1", out1f, "out1f", aux, EPS_LN,
              gamma=lngs["ln2g"], beta=lngs["ln2b"], gkey="ln2g")
        for oc in range(8):
            sl = slice(oc * tch, (oc + 1) * tch)
            ew(p, "vector", "tensor_add", z1[:, sl], out1f[:, sl],
               xc[bi][:, sl], reads=["out1f", "xc0"], writes=["z1"],
               tag=f"z3{c}_{oc}")
        aux["epscol"] = ec2
        aux["keys"]["epscol"] = eck2
        ln_fm(p, f"lno_{c}", z1, "z1", out1f, "out1f", aux, EPS_LN_OUTER)
        p.dma("sync", x1_r[:, :, s], out1f[:], reads=["out1f"], tag=f"sx1{c}")

    if x_h is not None:
        W4 = 4
        xh4 = p.sb("xh4", [128, 32], F32)
        qh4 = p.sb("qh4", [128, 32], BF16)
        qs4 = p.sb("qs4", [128, 32], BF16)
        at4 = p.sb("at4", [128, 32], BF16)
        msk = p.sb("msk", [128, 1], F32)
        x1h4 = p.sb("x1h4", [128, 32], F32)
        aux["epscol"] = ec1
        aux["keys"]["epscol"] = eck1
        p.dma("sync", xh4[:], x_h, writes=["xh4"], tag="hlx")
        p.dma("sync", qh4[:], qf_h, writes=["qh4"], tag="hlq")
        p.dma("sync", msk[:], mask1, writes=["msk"], tag="hlm")
        trip = [(pden[:, 0:W4], zall[:, mc * 16:(mc + 1) * 16],
                 qh4[:, mc * W4:(mc + 1) * W4], mc == 0, mc == 7)
                for mc in range(8)]
        mm_group(p, trip, reads=["zall", "qh4"], writes=["pden"], tag="hden")
        ew(p, "vector", "tensor_scalar", denf[:, 0:W4], pden[:, 0:W4],
           denc[:, 0:1], None, ALU.add, reads=["pden", "denc"],
           writes=["denf"], tag="hdenf")
        ew(p, "vector", "reciprocal", rden[:, 0:W4], denf[:, 0:W4],
           reads=["denf"], writes=["rden"], tag="hrden")
        for mc in range(8):
            pb = pA if mc % 2 == 0 else pB
            pbk = "pA" if mc % 2 == 0 else "pB"
            mm_group(p, [(pb[:, 0:W4], sel16[:, mc * 128:(mc + 1) * 128],
                          rden[:, 0:W4], True, True)],
                     reads=["rden", "sel16"], writes=[pbk], tag=f"hrb{mc}")
            ew(p, "vector", "scalar_tensor_tensor", qs4[:, mc * W4:(mc + 1) * W4],
               qh4[:, mc * W4:(mc + 1) * W4], CEPS, pb[:, 0:W4],
               ALU.add, ALU.mult, reads=["qh4", pbk], writes=["qs4"],
               tag=f"hqs{mc}")
        for mc in range(8):
            pb = pC if mc % 2 == 0 else pD
            pbk = "pC" if mc % 2 == 0 else "pD"
            trip = [(pb[par * 64:par * 64 + 64, 0:W4],
                     kvb[par * 64:par * 64 + 64, mc * 64:mc * 64 + 64],
                     qs4[par * 64:par * 64 + 64, mc * W4:(mc + 1) * W4],
                     True, True) for par in range(2)]
            mm_group(p, trip, reads=["kvb", "qs4"], writes=[pbk], tag=f"hat{mc}")
            act(p, at4[:, mc * W4:(mc + 1) * W4], pb[:, 0:W4], AF.Copy,
                reads=[pbk], writes=["at4"], tag=f"hatc{mc}")
        for oc in range(8):
            pb = pC if oc % 2 == 0 else pD
            pbk = "pC" if oc % 2 == 0 else "pD"
            trip = [(pb[:, 0:W4], wob[:, k * D + oc * 128:k * D + oc * 128 + 128],
                     at4[:, k * W4:(k + 1) * W4], k == 0, k == 7)
                    for k in range(8)]
            mm_group(p, trip, reads=["wob", "at4"], writes=[pbk], tag=f"hao{oc}")
            ew(p, "vector", "tensor_add", z1[:, oc * tch:oc * tch + W4],
               xh4[:, oc * W4:(oc + 1) * W4], pb[:, 0:W4],
               reads=["xh4", pbk], writes=["z1"], tag=f"hz1{oc}")
        ln_fm(p, "hln1", z1, "z1", out1f, "out1f", aux, EPS_LN,
              gamma=lngs["ln1g"], beta=lngs["ln1b"], gkey="ln1g",
              dst_bf=out1b, dstbfkey="out1b", wid=W4)
        for fc in range(32):
            wi = fc % 2
            p.dma("gpsimd", w1t[wi][:].rearrange("p (k f) -> p k f", f=128),
                  w1b.rearrange("p (k f) -> p k f", f=DFF)[:, :, fc * 128:(fc + 1) * 128],
                  reads=["w1b_hbm"], writes=[f"w1t{wi}"], tag=f"hlw1_{fc}")
            pb = pC if fc % 2 == 0 else pD
            pbk = "pC" if fc % 2 == 0 else "pD"
            trip = [(pb[:, 0:W4], w1t[wi][:, k * 128:(k + 1) * 128],
                     out1b[:, k * tch:k * tch + W4], k == 0, k == 7)
                    for k in range(8)]
            mm_group(p, trip, reads=[f"w1t{wi}", "out1b"], writes=[pbk],
                     tag=f"hf1{fc}")
            ew(p, "vector", "tensor_scalar", xbtmp[:, 0:W4], pb[:, 0:W4],
               b1s[:, fc:fc + 1], None, ALU.add, reads=[pbk, "b1s"],
               writes=["xbtmp"], tag=f"hxb{fc}")
            ew(p, "vector", "tensor_scalar_min", etmp[:, 0:W4], xbtmp[:, 0:W4],
               0.0, reads=["xbtmp"], writes=["etmp"], tag=f"hxn{fc}")
            act(p, etmp[:, 0:W4], etmp[:, 0:W4], AF.Exp, reads=["etmp"],
                writes=["etmp"], tag=f"hex{fc}")
            ew(p, "vector", "scalar_tensor_tensor", htmp[:, 0:W4],
               xbtmp[:, 0:W4], 0.0, etmp[:, 0:W4], ALU.max, ALU.add,
               reads=["xbtmp", "etmp"], writes=["htmp"], tag=f"hhm{fc}")
            ew(p, "vector", "tensor_scalar_add", hb[:, fc * tch:fc * tch + W4],
               htmp[:, 0:W4], -1.0, reads=["htmp"], writes=["hb"],
               tag=f"hhc{fc}")
        for oc in range(8):
            wi = oc % 2
            p.dma("gpsimd", w2t[wi][:].rearrange("p (k f) -> p k f", f=128),
                  w2b.rearrange("p (k f) -> p k f", f=D)[:, :, oc * 128:(oc + 1) * 128],
                  reads=["w2b_hbm"], writes=["w2t0"], tag=f"hlw2_{oc}")
            pb = pC if oc % 2 == 0 else pD
            pbk = "pC" if oc % 2 == 0 else "pD"
            trip = [(pb[:, 0:W4], w2t[wi][:, k * 128:(k + 1) * 128],
                     hb[:, k * tch:k * tch + W4], k == 0, k == 31)
                    for k in range(32)]
            mm_group(p, trip, reads=["w2t0", "hb"], writes=[pbk], tag=f"hf2{oc}")
            ew(p, "vector", "scalar_tensor_tensor",
               z1[:, oc * tch:oc * tch + W4], pb[:, 0:W4], b2s[:, oc:oc + 1],
               out1f[:, oc * tch:oc * tch + W4], ALU.add, ALU.add,
               reads=[pbk, "b2s", "out1f"], writes=["z1"], tag=f"hz2{oc}")
        ln_fm(p, "hln2", z1, "z1", out1f, "out1f", aux, EPS_LN,
              gamma=lngs["ln2g"], beta=lngs["ln2b"], gkey="ln2g", wid=W4)
        for oc in range(8):
            ew(p, "vector", "tensor_add", z1[:, oc * tch:oc * tch + W4],
               out1f[:, oc * tch:oc * tch + W4], xh4[:, oc * W4:(oc + 1) * W4],
               reads=["out1f", "xh4"], writes=["z1"], tag=f"hz3{oc}")
        aux["epscol"] = ec2
        aux["keys"]["epscol"] = eck2
        ln_fm(p, "hlno", z1, "z1", out1f, "out1f", aux, EPS_LN_OUTER, wid=W4)
        for oc in range(8):
            ew(p, "vector", "tensor_scalar_mul", x1h4[:, oc * W4:(oc + 1) * W4],
               out1f[:, oc * tch:oc * tch + W4], msk[:, 0:1],
               reads=["out1f", "msk"], writes=["x1h4"], tag=f"hmk{oc}")
        p.dma("sync", o_x1h, x1h4[:], reads=["x1h4"], tag="hsx1h")

    if env is None:
        p.final_wait("sync")
        p.build()
        p.finish()
        return nc



def build_l4(sh=SH, tch=512, env=None):
    nch = sh // tch
    if env is None:
        nc = bass.Bass("TRN2", target_bir_lowering=False)
        x1 = nc.dram_tensor("x1", [D, sh], F32, kind="ExternalInput").ap()
        xprev = nc.dram_tensor("xprev", [128, 8], F32, kind="ExternalInput").ap()
        wrT = nc.dram_tensor("wrT", [D, D], F32, kind="ExternalInput").ap()
        wkT = nc.dram_tensor("wkT", [D, D], F32, kind="ExternalInput").ap()
        wvT = nc.dram_tensor("wvT", [D, D], F32, kind="ExternalInput").ap()
        murc = nc.dram_tensor("murc", [128, 8], F32, kind="ExternalInput").ap()
        mukc = nc.dram_tensor("mukc", [128, 8], F32, kind="ExternalInput").ap()
        o_x2 = nc.dram_tensor("x2", [D, sh], F32, kind="ExternalOutput").ap()

        p = Prog(nc)
    else:
        nc, p = env["nc"], env["p"]
        x1 = env["x1"]
        xprev = env["xprev"]
        wrT = env["wrT"]
        wkT = env["wkT"]
        wvT = env["wvT"]
        murc = env["murc"]
        mukc = env["mukc"]
        o_x2 = env["o_x2"]
        x1h = env.get("x1h")
        mask1 = env.get("mask1")
        o_x2p3 = env.get("o_x2p3")
    if env is None:
        x1h = mask1 = o_x2p3 = None
    wrb = p.sb("wrb", [128, 8 * D], BF16)
    wkb = p.sb("wkb", [128, 8 * D], BF16)
    wvb = p.sb("wvb", [128, 8 * D], BF16)
    xbuf = p.sb("xbuf", [128, 8 * (tch + 1)], F32)
    dbuf = p.sb("dbuf", [128, 8 * tch], F32)
    xrb = p.sb("xrb", [128, 8 * tch], BF16)
    xkb = p.sb("xkb", [128, 8 * tch], BF16)
    rbuf = p.sb("rbuf", [128, 8 * tch], F32)
    kab = p.sb("kab", [128, 8 * tch], BF16)
    ktmp = p.sb("ktmp", [128, tch], F32)
    z1 = p.sb("z1", [128, 8 * tch], F32)
    x2o = p.sb("x2o", [128, 8 * tch], F32)
    murs = p.sb("murs", [128, 8], F32)
    muks = p.sb("muks", [128, 8], F32)
    aux = make_ln_aux(p, "ln", tch)
    wtmp = z1

    pC = p.ps("pC", [128, tch], F32)
    pD = p.ps("pD", [128, tch], F32)
    pA = p.ps("pA", [128, tch], F32)
    pB = p.ps("pB", [128, tch], F32)
    ps1 = p.ps("ps1", [1, tch], F32)
    ps2 = p.ps("ps2", [1, tch], F32)
    aux.update(ps1=ps1, ps2=ps2, pbm=pA, pba=pB, utile=ktmp)
    aux["keys"].update(ps1="ps1", ps2="ps2", pbm="pA", pba="pB", utile="ktmp")
    ec2, eck2 = p.constcol(EPS_LN_OUTER)
    aux["epscol"] = ec2
    aux["keys"]["epscol"] = eck2

    for wnm, src, dst in [("wrb", wrT, wrb), ("wkb", wkT, wkb),
                          ("wvb", wvT, wvb)]:
        for g in range(2):
            p.dma("sync", wtmp[:].rearrange("p (k m) -> p k m", m=D),
                  src.rearrange("(k p) m -> p k m", p=128)[:, 4 * g:4 * g + 4, :],
                  writes=["wtmp"], tag=f"wl{wnm}{g}")
            act(p, dst[:, g * 4 * D:(g + 1) * 4 * D], wtmp[:], AF.Copy,
                reads=["wtmp"], writes=[wnm], tag=f"wc{wnm}{g}")
    p.dma("sync", murs[:], murc, writes=["murs"], tag="lmur")
    p.dma("sync", muks[:], mukc, writes=["muks"], tag="lmuk")

    x1_r = x1.rearrange("(k p) t -> p k t", p=128)
    x2_r = o_x2.rearrange("(k p) t -> p k t", p=128)
    xb3 = xbuf[:].rearrange("p (k t) -> p k t", t=tch + 1)

    for c in range(nch):
        s = slice(c * tch, (c + 1) * tch)
        if c == 0:
            p.dma("sync", xb3[:, :, 0:1], xprev.rearrange("p (k o) -> p k o", o=1),
                  writes=["xh"], tag=f"lh{c}", allow_slow_non_contiguous=True)
        else:
            p.dma("sync", xb3[:, :, 0:1], x1_r[:, :, c * tch - 1:c * tch],
                  writes=["xh"], tag=f"lh{c}", allow_slow_non_contiguous=True)
        p.dma("sync", xb3[:, :, 1:tch + 1], x1_r[:, :, s], writes=["xc"],
              tag=f"lx{c}")
        ew(p, "vector", "tensor_sub", dbuf[:].rearrange("p (k t) -> p k t", t=tch),
           xb3[:, :, 1:tch + 1], xb3[:, :, 0:tch],
           reads=["xh", "xc"], writes=["dbuf"], tag=f"d{c}")
        for k in range(8):
            ew(p, "vector", "scalar_tensor_tensor",
               xrb[:, k * tch:(k + 1) * tch], dbuf[:, k * tch:(k + 1) * tch],
               murs[:, k:k + 1], xb3[:, k, 0:tch], ALU.mult, ALU.add,
               reads=["dbuf", "murs", "xc", "xh"], writes=["xrb"],
               tag=f"xr{c}_{k}")
            ew(p, "vector", "scalar_tensor_tensor",
               xkb[:, k * tch:(k + 1) * tch], dbuf[:, k * tch:(k + 1) * tch],
               muks[:, k:k + 1], xb3[:, k, 0:tch], ALU.mult, ALU.add,
               reads=["dbuf", "muks", "xc", "xh"], writes=["xkb"],
               tag=f"xk{c}_{k}")
        for oc in range(8):
            pb, pbk = (pC, "pC") if oc % 2 == 0 else (pD, "pD")
            mm_group(p, [(pb[:], wrb[:, k * D + oc * 128:k * D + oc * 128 + 128],
                          xrb[:, k * tch:(k + 1) * tch], k == 0, k == 7)
                         for k in range(8)],
                     reads=["wrb", "xrb"], writes=[pbk], tag=f"r{c}_{oc}")
            act(p, rbuf[:, oc * tch:(oc + 1) * tch], pb[:], AF.Sigmoid,
                reads=[pbk], writes=["rbuf"], tag=f"rs{c}_{oc}")
        for oc in range(8):
            pb, pbk = (pC, "pC") if oc % 2 == 0 else (pD, "pD")
            mm_group(p, [(pb[:], wkb[:, k * D + oc * 128:k * D + oc * 128 + 128],
                          xkb[:, k * tch:(k + 1) * tch], k == 0, k == 7)
                         for k in range(8)],
                     reads=["wkb", "xkb"], writes=[pbk], tag=f"k{c}_{oc}")
            act(p, ktmp[:], pb[:], AF.Relu, reads=[pbk], writes=["ktmp"],
                tag=f"kr{c}_{oc}")
            act(p, kab[:, oc * tch:(oc + 1) * tch], ktmp[:], AF.Square,
                reads=["ktmp"], writes=["kab"], tag=f"kq{c}_{oc}")
        for oc in range(8):
            pb, pbk = (pC, "pC") if oc % 2 == 0 else (pD, "pD")
            mm_group(p, [(pb[:], wvb[:, k * D + oc * 128:k * D + oc * 128 + 128],
                          kab[:, k * tch:(k + 1) * tch], k == 0, k == 7)
                         for k in range(8)],
                     reads=["wvb", "kab"], writes=[pbk], tag=f"v{c}_{oc}")
            ew(p, "vector", "tensor_mul", z1[:, oc * tch:(oc + 1) * tch],
               rbuf[:, oc * tch:(oc + 1) * tch], pb[:],
               reads=["rbuf", pbk], writes=["z1"], tag=f"zm{c}_{oc}")
            ew(p, "vector", "tensor_add", z1[:, oc * tch:(oc + 1) * tch],
               z1[:, oc * tch:(oc + 1) * tch], xb3[:, oc, 1:tch + 1],
               reads=["z1", "xc"], writes=["z1"], tag=f"za{c}_{oc}")
        ln_fm(p, f"lno_{c}", z1, "z1", x2o, "x2o", aux, EPS_LN_OUTER)
        p.dma("sync", x2_r[:, :, s], x2o[:], reads=["x2o"], tag=f"sx2{c}")

    if x1h is not None:
        W3 = 3
        xh4 = p.sb("xh4", [128, 32], F32)
        msk = p.sb("msk", [128, 1], F32)
        d3 = p.sb("d3", [128, 24], F32)
        xr3 = p.sb("xr3", [128, 24], BF16)
        xk3 = p.sb("xk3", [128, 24], BF16)
        r3 = p.sb("r3", [128, 24], F32)
        ka3 = p.sb("ka3", [128, 24], BF16)
        x23 = p.sb("x23", [128, 24], F32)
        p.dma("sync", xh4[:], x1h, writes=["xh4"], tag="h4lx")
        p.dma("sync", msk[:], mask1, writes=["msk"], tag="h4lm")
        xh3 = xh4[:].rearrange("p (k t) -> p k t", t=4)
        for k in range(8):
            ew(p, "vector", "tensor_sub", d3[:, k * W3:(k + 1) * W3],
               xh3[:, k, 1:4], xh3[:, k, 0:3], reads=["xh4"], writes=["d3"],
               tag=f"h4d{k}")
            ew(p, "vector", "scalar_tensor_tensor", xr3[:, k * W3:(k + 1) * W3],
               d3[:, k * W3:(k + 1) * W3], murs[:, k:k + 1], xh3[:, k, 0:3],
               ALU.mult, ALU.add, reads=["d3", "murs", "xh4"], writes=["xr3"],
               tag=f"h4r{k}")
            ew(p, "vector", "scalar_tensor_tensor", xk3[:, k * W3:(k + 1) * W3],
               d3[:, k * W3:(k + 1) * W3], muks[:, k:k + 1], xh3[:, k, 0:3],
               ALU.mult, ALU.add, reads=["d3", "muks", "xh4"], writes=["xk3"],
               tag=f"h4k{k}")
        for oc in range(8):
            pb, pbk = (pC, "pC") if oc % 2 == 0 else (pD, "pD")
            trip = [(pb[:, 0:W3], wrb[:, k * D + oc * 128:k * D + oc * 128 + 128],
                     xr3[:, k * W3:(k + 1) * W3], k == 0, k == 7)
                    for k in range(8)]
            mm_group(p, trip, reads=["wrb", "xr3"], writes=[pbk], tag=f"h4rm{oc}")
            act(p, r3[:, oc * W3:(oc + 1) * W3], pb[:, 0:W3], AF.Sigmoid,
                reads=[pbk], writes=["r3"], tag=f"h4rs{oc}")
        for oc in range(8):
            pb, pbk = (pC, "pC") if oc % 2 == 0 else (pD, "pD")
            trip = [(pb[:, 0:W3], wkb[:, k * D + oc * 128:k * D + oc * 128 + 128],
                     xk3[:, k * W3:(k + 1) * W3], k == 0, k == 7)
                    for k in range(8)]
            mm_group(p, trip, reads=["wkb", "xk3"], writes=[pbk], tag=f"h4km{oc}")
            act(p, ktmp[:, 0:W3], pb[:, 0:W3], AF.Relu, reads=[pbk],
                writes=["ktmp"], tag=f"h4kr{oc}")
            act(p, ka3[:, oc * W3:(oc + 1) * W3], ktmp[:, 0:W3], AF.Square,
                reads=["ktmp"], writes=["ka3"], tag=f"h4kq{oc}")
        for oc in range(8):
            pb, pbk = (pC, "pC") if oc % 2 == 0 else (pD, "pD")
            trip = [(pb[:, 0:W3], wvb[:, k * D + oc * 128:k * D + oc * 128 + 128],
                     ka3[:, k * W3:(k + 1) * W3], k == 0, k == 7)
                    for k in range(8)]
            mm_group(p, trip, reads=["wvb", "ka3"], writes=[pbk], tag=f"h4vm{oc}")
            ew(p, "vector", "tensor_mul", z1[:, oc * tch:oc * tch + W3],
               r3[:, oc * W3:(oc + 1) * W3], pb[:, 0:W3],
               reads=["r3", pbk], writes=["z1"], tag=f"h4zm{oc}")
            ew(p, "vector", "tensor_add", z1[:, oc * tch:oc * tch + W3],
               z1[:, oc * tch:oc * tch + W3], xh3[:, oc, 1:4],
               reads=["z1", "xh4"], writes=["z1"], tag=f"h4za{oc}")
        ln_fm(p, "h4lno", z1, "z1", x2o, "x2o", aux, EPS_LN_OUTER, wid=W3)
        for oc in range(8):
            ew(p, "vector", "tensor_scalar_mul", x23[:, oc * W3:(oc + 1) * W3],
               x2o[:, oc * tch:oc * tch + W3], msk[:, 0:1],
               reads=["x2o", "msk"], writes=["x23"], tag=f"h4mk{oc}")
        p.dma("sync", o_x2p3, x23[:], reads=["x23"], tag="h4sx2")

    if env is None:
        p.final_wait("sync")
        p.build()
        p.finish()
        return nc



def build_l5(sh=SH, tch=512, env=None):
    nch = sh // tch
    NIC = D_INNER // 128
    if env is None:
        nc = bass.Bass("TRN2", target_bir_lowering=False)
        x2 = nc.dram_tensor("x2", [D, sh], F32, kind="ExternalInput").ap()
        x2p3 = nc.dram_tensor("x2p3", [128, 24], F32, kind="ExternalInput").ap()
        winT = nc.dram_tensor("winT", [D, 2 * D_INNER], F32, kind="ExternalInput").ap()
        wdtT = nc.dram_tensor("wdtT", [D_INNER, D_INNER], F32, kind="ExternalInput").ap()
        cwc = nc.dram_tensor("cwc", [128, 4 * NIC], F32, kind="ExternalInput").ap()
        cbc = nc.dram_tensor("cbc", [128, NIC], F32, kind="ExternalInput").ap()
        bdtc = nc.dram_tensor("bdtc", [128, NIC], F32, kind="ExternalInput").ap()
        dtbc = nc.dram_tensor("dtbc", [128, NIC], F32, kind="ExternalInput").ap()
        alogc = nc.dram_tensor("alogc", [128, NIC * D_STATE], F32, kind="ExternalInput").ap()
        bparc = nc.dram_tensor("bparc", [128, NIC * D_STATE], F32, kind="ExternalInput").ap()
        o_dt = nc.dram_tensor("dt", [D_INNER, sh], F32, kind="ExternalOutput").ap()
        o_uu = nc.dram_tensor("uu", [D_INNER, sh], F32, kind="ExternalOutput").ap()
        o_sg = nc.dram_tensor("sg", [D_INNER, sh], F32, kind="ExternalOutput").ap()
        o_send = nc.dram_tensor("send", [128, NIC * D_STATE], F32, kind="ExternalOutput").ap()
        o_gend = nc.dram_tensor("gend", [128, NIC * D_STATE], F32, kind="ExternalOutput").ap()

        p = Prog(nc)
    else:
        nc, p = env["nc"], env["p"]
        x2 = env["x2"]
        x2p3 = env["x2p3"]
        winT = env["winT"]
        wdtT = env["wdtT"]
        cwc = env["cwc"]
        cbc = env["cbc"]
        bdtc = env["bdtc"]
        dtbc = env["dtbc"]
        alogc = env["alogc"]
        bparc = env["bparc"]
        o_dt = env["o_dt"]
        o_uu = env["o_uu"]
        o_sg = env["o_sg"]
        o_send = env["o_send"]
        o_gend = env["o_gend"]
    wdtb_scr = nc.dram_tensor("wdtb_scr", [128, 16 * D_INNER], BF16, kind="Internal").ap()
    winb_scr = nc.dram_tensor("winb_scr", [128, 8 * 2 * D_INNER], BF16, kind="Internal").ap()
    win_t = [p.sb(f"win_t{i}", [128, 8 * 128], BF16) for i in range(2)]
    xbld = p.sb("xbld", [128, NIC * (tch + 3)], F32)
    carry = p.sb("carry", [128, NIC * 3], F32)
    x2c = p.sb("x2c", [128, 8 * tch], F32)
    x2cb = p.sb("x2cb", [128, 8 * tch], BF16)
    x2p3b = p.sb("x2p3b", [128, 24], BF16)
    ubf = p.sb("ubf", [128, NIC * tch], BF16)
    usb = p.sb("usb", [128, tch], F32)
    sgsb = p.sb("sgsb", [128, tch], F32)
    dtsb = p.sb("dtsb", [128, tch], F32)
    xbt = p.sb("xbt", [128, tch], F32)
    ebt = p.sb("ebt", [128, tch], F32)
    wdt_t = [p.sb(f"wdt_t{i}", [128, 16 * 128], BF16) for i in range(2)]
    wtmpf = p.sb("wtmpf", [128, 2048], F32)
    wtmpb = p.sb("wtmpb", [128, 2048], BF16)
    cws = p.sb("cws", [128, 4 * NIC], F32)
    cbs = p.sb("cbs", [128, NIC], F32)
    bsum = p.sb("bsum", [128, NIC], F32)
    negA = p.sb("negA", [128, NIC * D_STATE], F32)
    bpar = p.sb("bpar", [128, NIC * D_STATE], F32)
    send = p.sb("send_sb", [128, NIC * D_STATE], F32)
    gend = p.sb("gend_sb", [128, NIC * D_STATE], F32)
    bdt = p.sb("bdt", [128, sh], F32)
    buu = p.sb("buu", [128, sh], F32)
    bdtu = p.sb("bdtu", [128, sh], F32)
    bdA = p.sb("bdA", [128, sh], F32)
    bdBu = p.sb("bdBu", [128, sh], F32)
    bS = p.sb("bS", [128, sh], F32)
    dtsum = p.sb("dtsum", [128, 1], F32)

    pC = p.ps("pC", [128, tch], F32)
    pD = p.ps("pD", [128, tch], F32)
    pH = p.ps("pH", [128, 3], F32)

    for g in range(8):
        for hh in range(2):
            p.dma("sync", wtmpf[:],
                  winT.rearrange("(k p) m -> p k m", p=128)[:, g, hh * 2048:(hh + 1) * 2048],
                  writes=["wtmpf"], tag=f"wi{g}_{hh}")
            act(p, wtmpb[:], wtmpf[:], AF.Copy,
                reads=["wtmpf"], writes=["wtmpb"], tag=f"wic{g}_{hh}")
            p.dma("sync", winb_scr[:, g * 4096 + hh * 2048:g * 4096 + (hh + 1) * 2048],
                  wtmpb[:], reads=["wtmpb"], writes=["winb_hbm"], tag=f"wis{g}_{hh}")
    for g in range(16):
        p.dma("sync", wtmpf[:],
              wdtT.rearrange("(k p) m -> p k m", p=128)[:, g, :],
              writes=["wtmpf"], tag=f"wd{g}")
        act(p, wtmpb[:], wtmpf[:], AF.Copy, reads=["wtmpf"], writes=["wtmpb"],
            tag=f"wdc{g}")
        p.dma("sync", wdtb_scr[:, g * D_INNER:(g + 1) * D_INNER], wtmpb[:],
              reads=["wtmpb"], writes=["wdtb_hbm"], tag=f"wds{g}")
    for nm, src, dst, key in [("cw", cwc, cws, "cws"), ("cb", cbc, cbs, "cbs"),
                              ("alog", alogc, negA, "negA0"),
                              ("bpar", bparc, bpar, "bpar")]:
        p.dma("sync", dst[:], src, writes=[key], tag=f"ld{nm}")
    p.dma("sync", bsum[:], bdtc, writes=["bsum"], tag="ldbdt")
    p.dma("sync", wtmpf[:, 0:NIC], dtbc, writes=["dtb_tmp"], tag="lddtb")
    ew(p, "vector", "tensor_add", bsum[:], bsum[:], wtmpf[:, 0:NIC],
       reads=["bsum", "dtb_tmp"], writes=["bsum"], tag="bsum")
    act(p, negA[:], negA[:], AF.Exp, reads=["negA0"], writes=["negA1"], tag="nega1")
    ew(p, "vector", "tensor_scalar_mul", negA[:], negA[:], -1.0,
       reads=["negA1"], writes=["negA"], tag="nega2")
    x2p3f = p.sb("x2p3f", [128, 24], F32)
    p.dma("sync", x2p3f[:], x2p3, writes=["x2p3f"], tag="ldp3")

    x2_r = x2.rearrange("(k p) t -> p k t", p=128)
    xbld3 = xbld[:].rearrange("p (i t) -> p i t", t=tch + 3)
    dt_r = o_dt.rearrange("(i p) t -> p i t", p=128)
    uu_r = o_uu.rearrange("(i p) t -> p i t", p=128)
    sg_r = o_sg.rearrange("(i p) t -> p i t", p=128)

    for c in range(nch):
        s = slice(c * tch, (c + 1) * tch)
        p.dma("sync", x2c[:], x2_r[:, :, s], writes=["x2c"], tag=f"lx{c}")
        act(p, x2cb[:], x2c[:], AF.Copy, reads=["x2c"], writes=["x2cb"],
            tag=f"xc{c}")
        if c == 0:
            act(p, x2p3b[:], x2p3f[:], AF.Copy,
                reads=["x2p3f"], writes=["x2p3bb"], tag="p3c")
            for ic in range(NIC):
                wi = ic % 2
                p.dma("gpsimd", win_t[wi][:].rearrange("p (k f) -> p k f", f=128),
                      winb_scr.rearrange("p (k f) -> p k f", f=4096)[:, :, ic * 128:(ic + 1) * 128],
                      reads=["winb_hbm"], writes=[f"win_t{wi}"], tag=f"hl{ic}")
                mm_group(p, [(pH[:], win_t[wi][:, k * 128:(k + 1) * 128],
                              x2p3b[:, k * 3:(k + 1) * 3], k == 0, k == 7)
                             for k in range(8)],
                         reads=[f"win_t{wi}", "x2p3bb"], writes=["pH"],
                         tag=f"hx{ic}")
                ew(p, "vector", "tensor_copy", carry[:, ic * 3:(ic + 1) * 3],
                   pH[:], reads=["pH"], writes=["carry"], tag=f"hc{ic}")
        ew(p, "vector", "tensor_copy",
           xbld3[:, :, 0:3], carry[:].rearrange("p (i t) -> p i t", t=3),
           reads=["carry"], writes=["xbh"], tag=f"pc{c}")
        for ic in range(NIC):
            wi = ic % 2
            p.dma("gpsimd", win_t[wi][:].rearrange("p (k f) -> p k f", f=128),
                  winb_scr.rearrange("p (k f) -> p k f", f=4096)[:, :, ic * 128:(ic + 1) * 128],
                  reads=["winb_hbm"], writes=[f"win_t{wi}"], tag=f"xzl{c}_{ic}")
            pb, pbk = (pC, "pC") if ic % 2 == 0 else (pD, "pD")
            mm_group(p, [(pb[:], win_t[wi][:, k * 128:(k + 1) * 128],
                          x2cb[:, k * tch:(k + 1) * tch], k == 0, k == 7)
                         for k in range(8)],
                     reads=[f"win_t{wi}", "x2cb"], writes=[pbk], tag=f"xz{c}_{ic}")
            ew(p, "vector", "tensor_copy", xbld3[:, ic, 3:tch + 3], pb[:],
               reads=[pbk], writes=[f"xb_{ic}", "xbB"], tag=f"xze{c}_{ic}")
        for ic in range(NIC):
            wi = ic % 2
            j = 2048 + ic * 128
            p.dma("gpsimd", win_t[wi][:].rearrange("p (k f) -> p k f", f=128),
                  winb_scr.rearrange("p (k f) -> p k f", f=4096)[:, :, j:j + 128],
                  reads=["winb_hbm"], writes=[f"win_t{wi}"], tag=f"gl{c}_{ic}")
            pb, pbk = (pC, "pC") if ic % 2 == 0 else (pD, "pD")
            mm_group(p, [(pb[:], win_t[wi][:, k * 128:(k + 1) * 128],
                          x2cb[:, k * tch:(k + 1) * tch], k == 0, k == 7)
                         for k in range(8)],
                     reads=[f"win_t{wi}", "x2cb"], writes=[pbk], tag=f"g{c}_{ic}")
            act(p, sgsb[:], pb[:], AF.Silu, reads=[pbk], writes=["sgsb"],
                tag=f"gs{c}_{ic}")
            p.dma("sync", sg_r[:, ic, s], sgsb[:], reads=["sgsb"],
                  tag=f"ssg{c}_{ic}")
        for ic in range(NIC):
            ew(p, "vector", "tensor_scalar", usb[:], xbld3[:, ic, 0:tch],
               cws[:, ic * 4:ic * 4 + 1], None, ALU.mult,
               reads=[f"xb_{ic}", "xbh", "cws"], writes=["usb"], tag=f"cv{c}_{ic}_0")
            for kk in range(1, 4):
                ew(p, "vector", "scalar_tensor_tensor", usb[:],
                   xbld3[:, ic, kk:kk + tch], cws[:, ic * 4 + kk:ic * 4 + kk + 1],
                   usb[:], ALU.mult, ALU.add,
                   reads=[f"xb_{ic}", "xbh", "cws", "usb"], writes=["usb"],
                   tag=f"cv{c}_{ic}_{kk}")
            act(p, usb[:], usb[:], AF.Silu, bias=cbs[:, ic:ic + 1],
                reads=["usb", "cbs"], writes=["usb"], tag=f"cs{c}_{ic}")
            p.dma("sync", uu_r[:, ic, s], usb[:], reads=["usb"], tag=f"su{c}_{ic}")
            act(p, ubf[:, ic * tch:(ic + 1) * tch], usb[:], AF.Copy,
                reads=["usb"], writes=["ubf"], tag=f"ub{c}_{ic}")
        ew(p, "vector", "tensor_copy", carry[:].rearrange("p (i t) -> p i t", t=3),
           xbld3[:, :, tch:tch + 3], reads=["xbB", "xbh"], writes=["carry"],
           tag=f"sc{c}")
        for io in range(NIC):
            wi = io % 2
            p.dma("gpsimd", wdt_t[wi][:].rearrange("p (k f) -> p k f", f=128),
                  wdtb_scr.rearrange("p (k f) -> p k f", f=D_INNER)[:, :, io * 128:(io + 1) * 128],
                  reads=["wdtb_hbm"], writes=[f"wdt_t{wi}"], tag=f"lwd{c}_{io}")
            pb, pbk = (pC, "pC") if io % 2 == 0 else (pD, "pD")
            mm_group(p, [(pb[:], wdt_t[wi][:, k * 128:(k + 1) * 128],
                          ubf[:, k * tch:(k + 1) * tch], k == 0, k == 15)
                         for k in range(16)],
                     reads=[f"wdt_t{wi}", "ubf"], writes=[pbk], tag=f"dl{c}_{io}")
            ew(p, "vector", "tensor_scalar", xbt[:], pb[:], bsum[:, io:io + 1],
               None, ALU.add, reads=[pbk, "bsum"], writes=["xbt"],
               tag=f"da{c}_{io}")
            act(p, ebt[:], xbt[:], AF.Abs, reads=["xbt"], writes=["ebt"],
                tag=f"db{c}_{io}")
            act(p, ebt[:], ebt[:], AF.Exp, scale=-1.0, reads=["ebt"],
                writes=["ebt"], tag=f"dc{c}_{io}")
            act(p, ebt[:], ebt[:], AF.Ln, bias=1.0, reads=["ebt"],
                writes=["ebt"], tag=f"dd{c}_{io}")
            ew(p, "vector", "scalar_tensor_tensor", dtsb[:], xbt[:], 0.0,
               ebt[:], ALU.max, ALU.add, reads=["xbt", "ebt"], writes=["dtsb"],
               tag=f"de{c}_{io}")
            p.dma("sync", dt_r[:, io, s], dtsb[:], reads=["dtsb"],
                  tag=f"sdt{c}_{io}")

    for ci in range(NIC):
        p.dma("sync", bdt[:], dt_r[:, ci, :], writes=["bdt"], tag=f"bl{ci}")
        p.dma("sync", buu[:], uu_r[:, ci, :], writes=["buu"], tag=f"bu{ci}")
        ew(p, "vector", "tensor_mul", bdtu[:], bdt[:], buu[:],
           reads=["bdt", "buu"], writes=["bdtu"], tag=f"bm{ci}")
        ew(p, "vector", "reduce_sum", dtsum[:], bdt[:], axis=AX.X,
           reads=["bdt"], writes=["dtsum"], tag=f"bs{ci}")
        for n in range(D_STATE):
            col = ci * D_STATE + n
            act(p, bdA[:], bdt[:], AF.Exp, scale=negA[:, col:col + 1],
                reads=["bdt", "negA"], writes=["bdA"], tag=f"bA{ci}_{n}")
            act(p, bdBu[:], bdtu[:], AF.Copy, scale=bpar[:, col:col + 1],
                reads=["bdtu", "bpar"], writes=["bdBu"], tag=f"bB{ci}_{n}")
            ew(p, "vector", "tensor_tensor_scan", bS[:], bdA[:], bdBu[:], 0.0,
               ALU.mult, ALU.add, reads=["bdA", "bdBu"], writes=["bS"],
               tag=f"bscan{ci}_{n}")
            ew(p, "vector", "tensor_copy", send[:, col:col + 1],
               bS[:, sh - 1:sh], reads=["bS"], writes=["send"],
               tag=f"bse{ci}_{n}")
            act(p, gend[:, col:col + 1], dtsum[:], AF.Exp,
                scale=negA[:, col:col + 1], reads=["dtsum", "negA"],
                writes=["gend"], tag=f"bg{ci}_{n}")
    p.dma("sync", o_send, send[:], reads=["send"], tag="stsend")
    p.dma("sync", o_gend, gend[:], reads=["gend"], tag="stgend")

    if env is None:
        p.final_wait("sync")
        p.build()
        p.finish()
        return nc



def build_l6(sh=SH, tch=512, env=None):
    nch = sh // tch
    NIC = D_INNER // 128
    if env is None:
        nc = bass.Bass("TRN2", target_bir_lowering=False)
        dt = nc.dram_tensor("dt", [D_INNER, sh], F32, kind="ExternalInput").ap()
        uu = nc.dram_tensor("uu", [D_INNER, sh], F32, kind="ExternalInput").ap()
        sg = nc.dram_tensor("sg", [D_INNER, sh], F32, kind="ExternalInput").ap()
        x2 = nc.dram_tensor("x2", [D, sh], F32, kind="ExternalInput").ap()
        send4 = nc.dram_tensor("send4", [4, 128, 256], F32, kind="ExternalInput").ap()
        gend4 = nc.dram_tensor("gend4", [4, 128, 256], F32, kind="ExternalInput").ap()
        sel4 = nc.dram_tensor("sel4", [128, 4], F32, kind="ExternalInput").ap()
        alogc = nc.dram_tensor("alogc", [128, 256], F32, kind="ExternalInput").ap()
        bparc = nc.dram_tensor("bparc", [128, 256], F32, kind="ExternalInput").ap()
        cparc = nc.dram_tensor("cparc", [128, 256], F32, kind="ExternalInput").ap()
        dparc = nc.dram_tensor("dparc", [128, 16], F32, kind="ExternalInput").ap()
        woutT = nc.dram_tensor("woutT", [D_INNER, D], F32, kind="ExternalInput").ap()
        o_x3 = nc.dram_tensor("x3", [D, sh], F32, kind="ExternalOutput").ap()

        p = Prog(nc)
    else:
        nc, p = env["nc"], env["p"]
        dt = env["dt"]
        uu = env["uu"]
        sg = env["sg"]
        x2 = env["x2"]
        send4 = env["send4"]
        gend4 = env["gend4"]
        sel4 = env["sel4"]
        alogc = env["alogc"]
        bparc = env["bparc"]
        cparc = env["cparc"]
        dparc = env["dparc"]
        woutT = env["woutT"]
        o_x3 = env["o_x3"]
    ygb_scr = nc.dram_tensor("ygb_scr", [D_INNER, sh], BF16, kind="Internal").ap()
    woutb = p.sb("woutb", [128, NIC * D], BF16)
    negA = p.sb("negA", [128, 256], F32)
    bpar = p.sb("bpar", [128, 256], F32)
    cpar = p.sb("cpar", [128, 256], F32)
    dpar = p.sb("dpar", [128, 16], F32)
    sends = p.sb("sends", [128, 4 * 256], F32)
    gends = p.sb("gends", [128, 4 * 256], F32)
    sel = p.sb("sel", [128, 4], F32)
    s0 = p.sb("s0", [128, 256], F32)
    Tst = p.sb("Tst", [128, 256], F32)
    bdt = p.sb("bdt", [128, sh], F32)
    buu = p.sb("buu", [128, sh], F32)
    bdtu = p.sb("bdtu", [128, sh], F32)
    bdA = p.sb("bdA", [128, sh], F32)
    bdBu = p.sb("bdBu", [128, sh], F32)
    bS = p.sb("bS", [128, sh], F32)
    yacc = p.sb("yacc", [128, sh], F32)
    ybt = p.sb("ybt", [128, sh], BF16)
    wtmpf = p.sb("wtmpf", [128, D], F32)
    aux = make_ln_aux(p, "ln", tch)
    if sh >= 8 * tch:
        x2c = bdt[:][:, 0:8 * tch]
        ygc = buu[:].bitcast(BF16)[:, 0:NIC * tch]
        z1 = bdtu[:][:, 0:8 * tch]
        x3o = bdBu[:][:, 0:8 * tch]
    else:
        x2c = p.sb("x2c", [128, 8 * tch], F32)[:]
        ygc = p.sb("ygc", [128, NIC * tch], BF16)[:]
        z1 = p.sb("z1x", [128, 8 * tch], F32)[:]
        x3o = p.sb("x3ox", [128, 8 * tch], F32)[:]

    pC = p.ps("pC", [128, tch], F32)
    pD = p.ps("pD", [128, tch], F32)
    pA = p.ps("pA", [128, tch], F32)
    pB = p.ps("pB", [128, tch], F32)
    ps1 = p.ps("ps1", [1, tch], F32)
    ps2 = p.ps("ps2", [1, tch], F32)
    aux.update(ps1=ps1, ps2=ps2, pbm=pA, pba=pB, utile=bdA[:, 0:tch])
    aux["keys"].update(ps1="ps1", ps2="ps2", pbm="pA", pba="pB", utile="bdA")
    ec2, eck2 = p.constcol(EPS_LN_OUTER)
    aux["epscol"] = ec2
    aux["keys"]["epscol"] = eck2

    for g in range(NIC):
        p.dma("sync", wtmpf[:],
              woutT.rearrange("(k p) m -> p k m", p=128)[:, g, :],
              writes=["wtmpf"], tag=f"wo{g}")
        act(p, woutb[:, g * D:(g + 1) * D], wtmpf[:], AF.Copy,
            reads=["wtmpf"], writes=["woutb"], tag=f"woc{g}")
    p.dma("sync", negA[:], alogc, writes=["negA0"], tag="lda")
    act(p, negA[:], negA[:], AF.Exp, reads=["negA0"], writes=["negA1"], tag="na1")
    ew(p, "vector", "tensor_scalar_mul", negA[:], negA[:], -1.0,
       reads=["negA1"], writes=["negA"], tag="na2")
    p.dma("sync", bpar[:], bparc, writes=["bpar"], tag="ldb")
    p.dma("sync", cpar[:], cparc, writes=["cpar"], tag="ldc")
    p.dma("sync", dpar[:], dparc, writes=["dpar"], tag="ldd")
    p.dma("sync", sends[:].rearrange("p (g t) -> p g t", g=4),
          send4.rearrange("g p t -> p g t"), writes=["sends"], tag="lds")
    p.dma("sync", gends[:].rearrange("p (g t) -> p g t", g=4),
          gend4.rearrange("g p t -> p g t"), writes=["gends"], tag="ldg")
    p.dma("sync", sel[:], sel4, writes=["sel"], tag="ldsel")
    ew(p, "vector", "memset", Tst[:], 0.0, writes=["Tst"], tag="T0")
    ew(p, "vector", "memset", s0[:], 0.0, writes=["s0"], tag="s00")
    for j in range(4):
        ew(p, "vector", "scalar_tensor_tensor", s0[:], Tst[:], sel[:, j:j + 1],
           s0[:], ALU.mult, ALU.add, reads=["Tst", "sel", "s0"], writes=["s0"],
           tag=f"s0a{j}")
        if j < 3:
            ew(p, "vector", "tensor_mul", Tst[:], Tst[:],
               gends[:, j * 256:(j + 1) * 256],
               reads=["Tst", "gends"], writes=["Tst"], tag=f"Tm{j}")
            ew(p, "vector", "tensor_add", Tst[:], Tst[:],
               sends[:, j * 256:(j + 1) * 256],
               reads=["Tst", "sends"], writes=["Tst"], tag=f"Ta{j}")
    dt_r = dt.rearrange("(i p) t -> p i t", p=128)
    uu_r = uu.rearrange("(i p) t -> p i t", p=128)
    sg_r = sg.rearrange("(i p) t -> p i t", p=128)
    yg_r = ygb_scr.rearrange("(i p) t -> p i t", p=128)
    x2_r = x2.rearrange("(k p) t -> p k t", p=128)
    x3_r = o_x3.rearrange("(k p) t -> p k t", p=128)

    for ci in range(NIC):
        p.dma("sync", bdt[:], dt_r[:, ci, :], writes=["bdt"], tag=f"bl{ci}")
        p.dma("sync", buu[:], uu_r[:, ci, :], writes=["buu"], tag=f"bu{ci}")
        ew(p, "vector", "tensor_mul", bdtu[:], bdt[:], buu[:],
           reads=["bdt", "buu"], writes=["bdtu"], tag=f"bm{ci}")
        for n in range(D_STATE):
            col = ci * D_STATE + n
            act(p, bdA[:], bdt[:], AF.Exp, scale=negA[:, col:col + 1],
                reads=["bdt", "negA", "bdA"], writes=["bdA"], tag=f"bA{ci}_{n}")
            act(p, bdBu[:], bdtu[:], AF.Copy, scale=bpar[:, col:col + 1],
                reads=["bdtu", "bpar"], writes=["bdBu"], tag=f"bB{ci}_{n}")
            ew(p, "vector", "tensor_tensor_scan", bS[:], bdA[:], bdBu[:],
               s0[:, col:col + 1], ALU.mult, ALU.add,
               reads=["bdA", "bdBu", "s0"], writes=["bS"], tag=f"bsc{ci}_{n}")
            if n == 0:
                ew(p, "vector", "tensor_scalar_mul", yacc[:], bS[:],
                   cpar[:, col:col + 1], reads=["bS", "cpar"], writes=["yacc"],
                   tag=f"by{ci}_{n}")
            else:
                ew(p, "vector", "scalar_tensor_tensor", yacc[:], bS[:],
                   cpar[:, col:col + 1], yacc[:], ALU.mult, ALU.add,
                   reads=["bS", "cpar", "yacc"], writes=["yacc"],
                   tag=f"by{ci}_{n}")
        ew(p, "vector", "scalar_tensor_tensor", yacc[:], buu[:],
           dpar[:, ci:ci + 1],
           yacc[:], ALU.mult, ALU.add, reads=["buu", "dpar", "yacc"],
           writes=["yacc"], tag=f"bd{ci}")
        p.dma("sync", bdA[:], sg_r[:, ci, :], writes=["bdA"], tag=f"bg{ci}")
        ew(p, "vector", "tensor_mul", ybt[:], yacc[:], bdA[:],
           reads=["yacc", "bdA"], writes=["ybt"], tag=f"byg{ci}")
        p.dma("sync", yg_r[:, ci, :], ybt[:], reads=["ybt"], writes=["yg_hbm"],
              tag=f"sy{ci}")

    for c in range(nch):
        s = slice(c * tch, (c + 1) * tch)
        p.dma("sync", x2c, x2_r[:, :, s], writes=["bdt"], tag=f"lx{c}")
        p.dma("sync", ygc.rearrange("p (i t) -> p i t", t=tch),
              yg_r[:, :, s], reads=["yg_hbm"], writes=["buu"], tag=f"ly{c}")
        for oc in range(8):
            pb, pbk = (pC, "pC") if oc % 2 == 0 else (pD, "pD")
            mm_group(p, [(pb[:], woutb[:, k * D + oc * 128:k * D + oc * 128 + 128],
                          ygc[:, k * tch:(k + 1) * tch], k == 0, k == NIC - 1)
                         for k in range(NIC)],
                     reads=["woutb", "buu"], writes=[pbk], tag=f"op{c}_{oc}")
            ew(p, "vector", "tensor_add", z1[:, oc * tch:(oc + 1) * tch],
               x2c[:, oc * tch:(oc + 1) * tch], pb[:],
               reads=["bdt", pbk], writes=["bdtu"], tag=f"z{c}_{oc}")
        ln_fm(p, f"lno_{c}", z1, "bdtu", x3o, "bdBu", aux, EPS_LN_OUTER)
        p.dma("sync", x3_r[:, :, s], x3o, reads=["bdBu"], tag=f"sx3{c}")

    if env is None:
        p.final_wait("sync")
        p.build()
        p.finish()
        return nc




def build_l7(sh=SH, tch=512, env=None):
    SC = 1024
    nsc = sh // SC
    nc5 = SC // tch
    NE = 9
    if env is None:
        nc = bass.Bass("TRN2", target_bir_lowering=False)
        x3 = nc.dram_tensor("x3", [D, sh], F32, kind="ExternalInput").ap()
        wg = nc.dram_tensor("wg", [D, 8], F32, kind="ExternalInput").ap()
        we1f = nc.dram_tensor("we1f", [NE * D, FF_DIM], F32, kind="ExternalInput").ap()
        we3f = nc.dram_tensor("we3f", [NE * D, FF_DIM], F32, kind="ExternalInput").ap()
        we2f = nc.dram_tensor("we2f", [NE * FF_DIM, D], F32, kind="ExternalInput").ap()
        o_y = nc.dram_tensor("y", [D, sh], F32, kind="ExternalOutput").ap()

        p = Prog(nc)
    else:
        nc, p = env["nc"], env["p"]
        x3 = env["x3"]
        wg = env["wg"]
        we1f = env["we1f"]
        we3f = env["we3f"]
        we2f = env["we2f"]
        o_y = env["o_y"]
    we1b = nc.dram_tensor("we1b", [128, NE * 8 * FF_DIM], BF16, kind="Internal").ap()
    we3b = nc.dram_tensor("we3b", [128, NE * 8 * FF_DIM], BF16, kind="Internal").ap()
    we2b = nc.dram_tensor("we2b", [128, NE * 16 * D], BF16, kind="Internal").ap()
    wgb = p.sb("wgb", [128, 8 * 8], BF16)
    identf = p.sb("identf", [128, 128], F32)
    w1t3 = [p.sb(f"w1t3{i}", [128, 8 * 128], BF16) for i in range(2)]
    w3t3 = [p.sb(f"w3t3{i}", [128, 8 * 128], BF16) for i in range(2)]
    w2t2 = [p.sb(f"w2t2{i}", [128, 16 * 128], BF16) for i in range(2)]
    racc = p.sb("racc", [128, 8 * SC], F32)
    x3b = p.sb("x3b", [128, 8 * SC], BF16)
    x3f = p.sb("x3f", [128, 8 * tch], F32)
    hwb = p.sb("hwb", [128, 16 * tch], BF16)
    stmp = p.sb("stmp", [128, tch], F32)
    stmp2 = [p.sb(f"stmp2{i}", [128, tch], F32) for i in range(2)]
    g_fm = p.sb("g_fm", [8, SC], F32)
    w_fmt = p.sb("w_fmt", [8, SC], F32)
    g_tm = p.sb("g_tm", [128, 8], F32)
    top8 = p.sb("top8", [128, 8], F32)
    rep = p.sb("rep", [128, 8], F32)
    gz = p.sb("gz", [128, 8], F32)
    w_tm = p.sb("w_tm", [128, 8], F32)
    s2c = p.sb("s2c", [128, 1], F32)
    wbs = p.sb("wbs", [128, tch], F32)
    aux = make_ln_aux(p, "ln", tch)
    z1 = hwb[:].bitcast(F32)
    wtmpf = racc[:].bitcast(F32)[:, 0:FF_DIM]
    wtmpb = x3b[:][:, 0:FF_DIM]
    yo = x3f

    pC = p.ps("pC", [128, tch], F32)
    pD = p.ps("pD", [128, tch], F32)
    pE = p.ps("pE", [128, tch], F32)
    pA = p.ps("pA", [128, tch], F32)
    pB = p.ps("pB", [128, tch], F32)
    ps1 = p.ps("ps1", [1, tch], F32)
    ps2 = p.ps("ps2", [1, tch], F32)
    pg = p.ps("pg", [16, tch], F32)
    aux.update(ps1=ps1, ps2=ps2, pbm=pA, pba=pB, utile=stmp)
    aux["keys"].update(ps1="ps1", ps2="ps2", pbm="pA", pba="pB", utile="stmp")
    ec2, eck2 = p.constcol(EPS_LN_OUTER)
    aux["epscol"] = ec2
    aux["keys"]["epscol"] = eck2

    make_identity(p, identf[:], "identf")
    esel = p.sb("esel", [8, 8 * 128], F32)
    p.emit("gpsimd", lambda e: e.memset(esel[:], 0.0), writes=["esel"], tag="es0")
    p.emit("gpsimd", lambda e: e.affine_select(
        out=esel[:].rearrange("j (e q) -> j e q", q=128),
        in_=esel[:].rearrange("j (e q) -> j e q", q=128),
        compare_op=ALU.not_equal, fill=1.0, base=0,
        pattern=[[-1, 8], [0, 128]], channel_multiplier=1),
        reads=["esel"], writes=["esel"], tag="es1")
    p.dma("sync", wtmpf[:, 0:64].rearrange("p (k e) -> p k e", e=8),
          wg.rearrange("(k p) e -> p k e", p=128), writes=["wtmpf"], tag="lwg")
    act(p, wgb[:], wtmpf[:, 0:64], AF.Copy, reads=["wtmpf"], writes=["wgb"],
        tag="cwg")
    wtmpf_sliced = True
    for (srcT, dstT, key) in [(we1f, we1b, "we1_hbm"), (we3f, we3b, "we3_hbm")]:
        for e in range(NE):
            for k in range(8):
                p.dma("sync", wtmpf[:, 0:FF_DIM],
                      srcT.rearrange("(q p) m -> p q m", p=128)[:, e * 8 + k, :],
                      writes=["wtmpf"], tag=f"pl{key}{e}_{k}")
                act(p, wtmpb[:, 0:FF_DIM], wtmpf[:, 0:FF_DIM], AF.Copy,
                    reads=["wtmpf"], writes=["wtmpb"], tag=f"pc{key}{e}_{k}")
                p.dma("sync",
                      dstT.rearrange("p (q g kk f) -> p q g kk f", g=16, kk=8,
                                     f=128)[:, e, :, k, :],
                      wtmpb[:, 0:FF_DIM].rearrange("p (g f) -> p g f", f=128),
                      reads=["wtmpb"], writes=[key], tag=f"ps{key}{e}_{k}")
    for e in range(NE):
        for k in range(16):
            p.dma("sync", wtmpf[:, 0:D],
                  we2f.rearrange("(q p) m -> p q m", p=128)[:, e * 16 + k, :],
                  writes=["wtmpf"], tag=f"plw2{e}_{k}")
            act(p, wtmpb[:, 0:D], wtmpf[:, 0:D], AF.Copy,
                reads=["wtmpf"], writes=["wtmpb"], tag=f"pcw2{e}_{k}")
            p.dma("sync", we2b[:, (e * 16 + k) * D:(e * 16 + k + 1) * D],
                  wtmpb[:, 0:D], reads=["wtmpb"], writes=["we2_hbm"],
                  tag=f"psw2{e}_{k}")
    p.res.setdefault("racc", {"w": None, "r": []})["r"].extend(
        p.res["wtmpf"]["r"])
    p.res.setdefault("x3b", {"w": None, "r": []})["r"].extend(
        p.res["wtmpb"]["r"])

    x3_r = x3.rearrange("(k p) t -> p k t", p=128)
    y_r = o_y.rearrange("(k p) t -> p k t", p=128)

    for sc in range(nsc):
        scs = slice(sc * SC, (sc + 1) * SC)
        for c5 in range(nc5):
            s5 = slice(sc * SC + c5 * tch, sc * SC + (c5 + 1) * tch)
            p.dma("sync", x3f[:], x3_r[:, :, s5], writes=["x3f"], tag=f"lx{sc}_{c5}")
            act(p, x3b[:].rearrange("p (k t) -> p k t", t=SC)[:, :, c5 * tch:(c5 + 1) * tch],
                x3f[:].rearrange("p (k t) -> p k t", t=tch), AF.Copy,
                reads=["x3f"], writes=["x3b"], tag=f"cx{sc}_{c5}")
        for c5 in range(nc5):
            trip = [(pg[0:8, :], wgb[:, k * 8:(k + 1) * 8],
                     x3b[:].rearrange("p (k t) -> p k t", t=SC)[:, k, c5 * tch:(c5 + 1) * tch],
                     k == 0, k == 7) for k in range(8)]
            mm_group(p, trip, reads=["x3b", "wgb"], writes=["pg"],
                     tag=f"gm{sc}_{c5}")
            act(p, g_fm[:, c5 * tch:(c5 + 1) * tch], pg[0:8, :], AF.Sigmoid,
                reads=["pg"], writes=["g_fm"], tag=f"gs{sc}_{c5}")
        for tb in range(SC // 128):
            transpose(p, pC[0:128, 0:8], g_fm[:, tb * 128:(tb + 1) * 128],
                      identf[0:8, 0:8], reads=["g_fm", "identf"], writes=["pC"],
                      tag=f"gt{sc}_{tb}")
            ew(p, "vector", "tensor_copy", g_tm[:], pC[0:128, 0:8],
               reads=["pC"], writes=["g_tm"], tag=f"ge{sc}_{tb}")
            ew(p, "vector", "max", out=top8[:], in_=g_tm[:],
               reads=["g_tm"], writes=["top8"], tag=f"gx{sc}_{tb}")
            ew(p, "vector", "tensor_copy", rep[:, 0:2], top8[:, 0:2],
               reads=["top8"], writes=["rep"], tag=f"gr{sc}_{tb}")
            ew(p, "vector", "memset", rep[:, 2:8], -1e30,
               reads=[], writes=["rep2"], deps=[p.res["rep"]["w"]],
               tag=f"gm2{sc}_{tb}")
            ew(p, "vector", "match_replace", gz[:], rep[:], g_tm[:], 0.0,
               reads=["rep", "rep2", "g_tm"], writes=["gz"], tag=f"gmr{sc}_{tb}")
            ew(p, "vector", "tensor_sub", w_tm[:], g_tm[:], gz[:],
               reads=["g_tm", "gz"], writes=["w_tm"], tag=f"gw{sc}_{tb}")
            ew(p, "vector", "tensor_add", s2c[:], top8[:, 0:1], top8[:, 1:2],
               reads=["top8"], writes=["s2c"], tag=f"g2{sc}_{tb}")
            ew(p, "vector", "reciprocal", s2c[:], s2c[:],
               reads=["s2c"], writes=["s2c"], tag=f"g3{sc}_{tb}")
            ew(p, "vector", "tensor_scalar_mul", w_tm[:], w_tm[:], s2c[:, 0:1],
               reads=["w_tm", "s2c"], writes=["w_tm"], tag=f"g4{sc}_{tb}")
            transpose(p, pC[0:8, 0:128], w_tm[:], identf[:],
                      reads=["w_tm", "identf"], writes=["pC"], tag=f"g5{sc}_{tb}")
            ew(p, "vector", "tensor_copy", w_fmt[:, tb * 128:(tb + 1) * 128],
               pC[0:8, 0:128], reads=["pC"], writes=["w_fmt"], tag=f"g6{sc}_{tb}")
        for e in range(NE):
            w2e = w2e2[e % 2]
            w2key = f"w2e2{e % 2}"
            p.dma("sync", w2e[:], we2b[:, e * 16 * D:(e + 1) * 16 * D],
                  reads=["we2_hbm"], writes=[w2key], tag=f"le2_{sc}_{e}")
            for c5 in range(nc5):
                if e < 8:
                    mm_group(p, [(pB[:], esel[:, e * 128:(e + 1) * 128],
                                  w_fmt[:, c5 * tch:(c5 + 1) * tch],
                                  True, True)],
                             reads=["w_fmt", "esel"], writes=["pB"],
                             tag=f"wb{sc}_{e}_{c5}")
                    ew(p, "vector", "tensor_copy", wbs[:], pB[:], reads=["pB"],
                       writes=["wbs"], tag=f"wbe{sc}_{e}_{c5}")
                for fcg in range(16):
                    wi = fcg % 2
                    tb0 = (e * 16 + fcg) * 1024
                    p.dma("sync", w1t3[wi][:], we1b[:, tb0:tb0 + 1024],
                          reads=["we1_hbm"], writes=[f"w1t3{wi}"],
                          tag=f"l1{sc}_{e}_{c5}_{fcg}")
                    p.dma("gpsimd", w3t3[wi][:], we3b[:, tb0:tb0 + 1024],
                          reads=["we3_hbm"], writes=[f"w3t3{wi}"],
                          tag=f"l3{sc}_{e}_{c5}_{fcg}")
                    pb1, pk1 = (pC, "pC")
                    pb3, pk3 = (pD, "pD")
                    mm_group(p, [(pb1[:], w1t3[wi][:, k * 128:(k + 1) * 128],
                                  x3b[:].rearrange("p (k t) -> p k t", t=SC)
                                  [:, k, c5 * tch:(c5 + 1) * tch], k == 0, k == 7)
                                 for k in range(8)],
                             reads=[f"w1t3{wi}", "x3b"], writes=[pk1],
                             tag=f"e1{sc}_{e}_{c5}_{fcg}")
                    mm_group(p, [(pb3[:], w3t3[wi][:, k * 128:(k + 1) * 128],
                                  x3b[:].rearrange("p (k t) -> p k t", t=SC)
                                  [:, k, c5 * tch:(c5 + 1) * tch], k == 0, k == 7)
                                 for k in range(8)],
                             reads=[f"w3t3{wi}", "x3b"], writes=[pk3],
                             tag=f"e3{sc}_{e}_{c5}_{fcg}")
                    st_ = stmp2[fcg % 2]
                    stk = f"stmp2{fcg % 2}"
                    act(p, st_[:], pb1[:], AF.Silu, reads=[pk1],
                        writes=[stk], tag=f"sl{sc}_{e}_{c5}_{fcg}")
                    if e < 8:
                        ew(p, "vector", "tensor_mul", st_[:], st_[:], pb3[:],
                           reads=[stk, pk3], writes=[stk],
                           tag=f"h1{sc}_{e}_{c5}_{fcg}")
                        ew(p, "vector", "tensor_mul",
                           hwb[:, fcg * tch:(fcg + 1) * tch], st_[:], wbs[:],
                           reads=[stk, "wbs"], writes=["hwb", f"hwb_{fcg}"],
                           tag=f"h2{sc}_{e}_{c5}_{fcg}")
                    else:
                        ew(p, "vector", "tensor_mul",
                           hwb[:, fcg * tch:(fcg + 1) * tch], st_[:], pb3[:],
                           reads=[stk, pk3], writes=["hwb", f"hwb_{fcg}"],
                           tag=f"h2{sc}_{e}_{c5}_{fcg}")
                for oc in range(8):
                    pb, pbk = (pE, "pE") if oc % 2 == 0 else (pA, "pA")
                    mm_group(p, [(pb[:], w2e[:, k * D + oc * 128:
                                             k * D + oc * 128 + 128],
                                  hwb[:, k * tch:(k + 1) * tch], k == 0, k == 15)
                                 for k in range(16)],
                             reads=[w2key, "hwb"], writes=[pbk],
                             tag=f"e2{sc}_{e}_{c5}_{oc}")
                    rsl = racc[:, oc * SC + c5 * tch:oc * SC + (c5 + 1) * tch]
                    if e == 0:
                        ew(p, "vector", "tensor_copy", rsl, pb[:],
                           reads=[pbk], writes=["racc"], tag=f"r0{sc}_{e}_{c5}_{oc}")
                    else:
                        ew(p, "vector", "tensor_add", rsl, rsl, pb[:],
                           reads=[pbk, "racc"], writes=["racc"],
                           tag=f"ra{sc}_{e}_{c5}_{oc}")
        for c5 in range(nc5):
            s5 = slice(sc * SC + c5 * tch, sc * SC + (c5 + 1) * tch)
            p.dma("sync", x3f[:], x3_r[:, :, s5], writes=["x3f"], tag=f"fx{sc}_{c5}")
            for oc in range(8):
                ew(p, "vector", "scalar_tensor_tensor",
                   z1[:, oc * tch:(oc + 1) * tch],
                   racc[:, oc * SC + c5 * tch:oc * SC + (c5 + 1) * tch],
                   1.0 / 3.0, x3f[:, oc * tch:(oc + 1) * tch],
                   ALU.mult, ALU.add, reads=["racc", "x3f"], writes=["hwb"],
                   tag=f"fz{sc}_{c5}_{oc}")
            ln_fm(p, f"lno_{sc}_{c5}", z1, "hwb", yo, "x3f", aux, EPS_LN_OUTER)
            p.dma("sync", y_r[:, :, s5], yo[:], reads=["x3f"], tag=f"sy{sc}_{c5}")

    if env is None:
        p.final_wait("sync")
        p.build()
        p.finish()
        return nc



_NC_CACHE = {}


def _get_nc(name, builder):
    if name not in _NC_CACHE:
        _NC_CACHE[name] = builder()
    return _NC_CACHE[name]


def _rsh(v, n):
    return np.ascontiguousarray(np.asarray(v).reshape(n, 128).T)


def _rsh2(m):
    k = m.shape[1]
    return np.ascontiguousarray(
        m.reshape(16, 128, k).transpose(1, 0, 2).reshape(128, 16 * k))


def _fm3(cols):
    n = cols.shape[1]
    return np.ascontiguousarray(
        cols.reshape(8, 128, n).transpose(1, 0, 2).reshape(128, 8 * n))


def kernel(**inputs):
    return kernel_v3(**inputs)


def kernel_unfused(**inputs):
    ii = inputs
    x_full = np.asarray(ii["L0_attn_x"])
    shards = []
    for b in range(2):
        for q in range(4):
            shards.append(np.ascontiguousarray(
                x_full[b, q * SH:(q + 1) * SH, :].T))

    wq = np.ascontiguousarray(ii["L0_attn_Wq"].reshape(D, D))
    wk = np.ascontiguousarray(ii["L0_attn_Wk"].reshape(D, D))
    wv = np.ascontiguousarray(ii["L0_attn_Wv"].reshape(D, D))
    projT = np.ascontiguousarray(ii["L0_attn_proj"].T)
    nc1 = _get_nc("l1", build_l1)
    r1 = run_bass_kernel_spmd(nc1, [dict(x=shards[c], wq=wq, wk=wk, wv=wv,
                                         projT=projT) for c in range(8)],
                              core_ids=list(range(8))).results

    mxall = np.stack([r1[c]["mxloc"][0] for c in range(8)]).astype(np.float32)
    nc2 = _get_nc("l2", build_l2)
    r2 = run_bass_kernel_spmd(nc2, [dict(ddk=r1[c]["ddk"], vv=r1[c]["vv"],
                                         mxall=mxall) for c in range(8)],
                              core_ids=list(range(8))).results

    wo = np.ascontiguousarray(ii["L0_attn_Wo"].reshape(D, D))
    nc3 = _get_nc("l3", build_l3)
    im3 = []
    for c in range(8):
        g0 = (c // 4) * 4
        im3.append(dict(
            x=shards[c], qf=r1[c]["qf"],
            kvp4=np.stack([r2[g0 + j]["kvp"] for j in range(4)]),
            kfsp4=np.stack([r2[g0 + j]["kfsp"] for j in range(4)]),
            vsp4=np.stack([r2[g0 + j]["vsp"] for j in range(4)]),
            wo=wo, w1=np.asarray(ii["L0_attn_W1"]), w2=np.asarray(ii["L0_attn_W2"]),
            b1c=_rsh(ii["L0_attn_b1"], 32), b2c=_rsh(ii["L0_attn_b2"], 8),
            ln1g=_rsh(ii["L0_attn_ln1_g"], 8), ln1b=_rsh(ii["L0_attn_ln1_b"], 8),
            ln2g=_rsh(ii["L0_attn_ln2_g"], 8), ln2b=_rsh(ii["L0_attn_ln2_b"], 8)))
    r3 = run_bass_kernel_spmd(nc3, im3, core_ids=list(range(8))).results

    nc4 = _get_nc("l4", build_l4)
    wrT = np.ascontiguousarray(np.asarray(ii["L0_ffn_W_r"]).T)
    wkT4 = np.ascontiguousarray(np.asarray(ii["L0_ffn_W_k"]).T)
    wvT4 = np.ascontiguousarray(np.asarray(ii["L0_ffn_W_v"]).T)
    murc = _rsh(ii["L0_ffn_mu_r"], 8)
    mukc = _rsh(ii["L0_ffn_mu_k"], 8)
    im4 = []
    for c in range(8):
        if c % 4 == 0:
            xprev = np.zeros((128, 8), np.float32)
        else:
            xprev = _fm3(r3[c - 1]["x1"][:, SH - 1:SH])[:, :]
        im4.append(dict(x1=r3[c]["x1"], xprev=np.ascontiguousarray(xprev),
                        wrT=wrT, wkT=wkT4, wvT=wvT4, murc=murc, mukc=mukc))
    r4 = run_bass_kernel_spmd(nc4, im4, core_ids=list(range(8))).results

    nc5 = _get_nc("l5", build_l5)
    winT = np.ascontiguousarray(np.asarray(ii["L1_attn_W_in"]).T)
    wdtT = np.ascontiguousarray(np.asarray(ii["L1_attn_W_dt"]).T)
    cwc = _rsh2(np.asarray(ii["L1_attn_conv_w"]))
    cbc = _rsh(ii["L1_attn_conv_b"], 16)
    bdtc = _rsh(ii["L1_attn_b_dt"], 16)
    dtbc = _rsh(ii["L1_attn_dt_bias"], 16)
    alogc = _rsh2(np.asarray(ii["L1_attn_A_log"]))
    bparc = _rsh2(np.asarray(ii["L1_attn_B_param"]))
    im5 = []
    for c in range(8):
        if c % 4 == 0:
            p3 = np.zeros((128, 24), np.float32)
        else:
            p3 = _fm3(r4[c - 1]["x2"][:, SH - 3:SH]).reshape(128, 24)
        im5.append(dict(x2=r4[c]["x2"], x2p3=np.ascontiguousarray(p3),
                        winT=winT, wdtT=wdtT, cwc=cwc, cbc=cbc, bdtc=bdtc,
                        dtbc=dtbc, alogc=alogc, bparc=bparc))
    r5 = run_bass_kernel_spmd(nc5, im5, core_ids=list(range(8))).results

    nc6 = _get_nc("l6", build_l6)
    cparc = _rsh2(np.asarray(ii["L1_attn_C_param"]))
    dparc = _rsh(ii["L1_attn_D_param"], 16)
    woutT = np.ascontiguousarray(np.asarray(ii["L1_attn_W_out"]).T)
    im6 = []
    for c in range(8):
        g0 = (c // 4) * 4
        send4 = np.stack([r5[g0 + j]["send"] for j in range(4)])
        gend4 = np.stack([r5[g0 + j]["gend"] for j in range(4)])
        sel = np.zeros((128, 4), np.float32)
        sel[:, c % 4] = 1.0
        im6.append(dict(dt=r5[c]["dt"], uu=r5[c]["uu"], sg=r5[c]["sg"],
                        x2=r4[c]["x2"], send4=send4, gend4=gend4, sel4=sel,
                        alogc=alogc, bparc=bparc, cparc=cparc, dparc=dparc,
                        woutT=woutT))
    r6 = run_bass_kernel_spmd(nc6, im6, core_ids=list(range(8))).results

    nc7 = _get_nc("l7", build_l7)
    we1f = np.ascontiguousarray(np.concatenate(
        [np.asarray(ii["L1_ffn_We1"]).reshape(8 * D, FF_DIM),
         np.asarray(ii["L1_ffn_Ws1"])], 0))
    we3f = np.ascontiguousarray(np.concatenate(
        [np.asarray(ii["L1_ffn_We3"]).reshape(8 * D, FF_DIM),
         np.asarray(ii["L1_ffn_Ws3"])], 0))
    we2f = np.ascontiguousarray(np.concatenate(
        [np.asarray(ii["L1_ffn_We2"]).reshape(8 * FF_DIM, D),
         np.asarray(ii["L1_ffn_Ws2"])], 0))
    wg7 = np.asarray(ii["L1_ffn_Wg"])
    im7 = [dict(x3=r6[c]["x3"], wg=wg7, we1f=we1f, we3f=we3f, we2f=we2f)
           for c in range(8)]
    r7 = run_bass_kernel_spmd(nc7, im7, core_ids=list(range(8))).results

    out = np.empty((2, N, D), np.float32)
    for b in range(2):
        for q in range(4):
            out[b, q * SH:(q + 1) * SH, :] = r7[b * 4 + q]["y"].T
    return out



def build_fused(sh=SH, tch=512):
    nc = bass.Bass("TRN2", target_bir_lowering=False)
    EIn = lambda nm, shp, dt=F32: nc.dram_tensor(nm, shp, dt, kind="ExternalInput").ap()
    Int = lambda nm, shp, dt=F32: nc.dram_tensor(nm, shp, dt, kind="Internal").ap()
    io = {}
    io["x"] = EIn("x", [D, sh])
    io["wq"] = EIn("wq", [D, D]); io["wk"] = EIn("wk", [D, D])
    io["wv"] = EIn("wv", [D, D]); io["projT"] = EIn("projT", [DH, M])
    io["wo"] = EIn("wo", [D, D]); io["w1"] = EIn("w1", [D, DFF])
    io["w2"] = EIn("w2", [DFF, D]); io["b1c"] = EIn("b1c", [128, 32])
    io["b2c"] = EIn("b2c", [128, 8])
    io["lng"] = {nm: EIn(nm, [128, 8]) for nm in ("ln1g", "ln1b", "ln2g", "ln2b")}
    io["wrT"] = EIn("wrT", [D, D]); io["wkT"] = EIn("wkT", [D, D])
    io["wvT"] = EIn("wvT", [D, D]); io["murc"] = EIn("murc", [128, 8])
    io["mukc"] = EIn("mukc", [128, 8])
    io["winT"] = EIn("winT", [D, 2 * D_INNER])
    io["wdtT"] = EIn("wdtT", [D_INNER, D_INNER])
    io["cwc"] = EIn("cwc", [128, 64]); io["cbc"] = EIn("cbc", [128, 16])
    io["bdtc"] = EIn("bdtc", [128, 16]); io["dtbc"] = EIn("dtbc", [128, 16])
    io["alogc"] = EIn("alogc", [128, 256]); io["bparc"] = EIn("bparc", [128, 256])
    io["cparc"] = EIn("cparc", [128, 256]); io["dparc"] = EIn("dparc", [128, 16])
    io["woutT"] = EIn("woutT", [D_INNER, D])
    io["wg"] = EIn("wg", [D, 8])
    io["we1f"] = EIn("we1f", [9 * D, FF_DIM])
    io["we3f"] = EIn("we3f", [9 * D, FF_DIM])
    io["we2f"] = EIn("we2f", [9 * FF_DIM, D])
    io["sel1"] = EIn("sel1", [128, 4])
    io["sel3"] = EIn("sel3", [128, 4])
    io["sel4"] = EIn("sel4", [128, 4])
    io["o_y"] = nc.dram_tensor("y", [D, sh], F32, kind="ExternalOutput").ap()

    io["qf"] = Int("qf_i", [D, sh], BF16)
    io["ddk"] = Int("ddk_i", [D, sh])
    io["vv"] = Int("vv_i", [D, sh], BF16)
    mxl = Int("mxl_i", [1, 1])
    mxg = Int("mxg_i", [1, 1])
    mxall = Int("mxall_i", [8, 1])
    kvp = Int("kvp_i", [128, 512]); kvg = Int("kvg_i", [128, 512])
    kfsp = Int("kfsp_i", [128, 8]); kfsg = Int("kfsg_i", [128, 8])
    vsp = Int("vsp_i", [1, 1024]); vsg = Int("vsg_i", [1, 1024])
    kvp4 = Int("kvp4_i", [4, 128, 512])
    kfsp4 = Int("kfsp4_i", [4, 128, 8])
    vsp4 = Int("vsp4_i", [4, 1, 1024])
    x1 = Int("x1_i", [D, sh])
    x1last = Int("x1last_i", [128, 8]); x1g = Int("x1g_i", [4 * 128, 8])
    xprev = Int("xprev_i", [128, 8])
    x2 = Int("x2_i", [D, sh])
    x2l3 = Int("x2l3_i", [128, 24]); x2g = Int("x2g_i", [4 * 128, 24])
    x2p3 = Int("x2p3_i", [128, 24])
    dtD = Int("dt_i", [D_INNER, sh]); uuD = Int("uu_i", [D_INNER, sh])
    sgD = Int("sg_i", [D_INNER, sh])
    sendl = Int("sendl_i", [128, 256]); gendl = Int("gendl_i", [128, 256])
    sendg = Int("sendg_i", [4 * 128, 256]); gendg = Int("gendg_i", [4 * 128, 256])
    x3 = Int("x3_i", [D, sh])

    p = Prog(nc)
    env = {"nc": nc, "p": p}

    def coll(kind, op, groups, ins, outs, rkeys, wkeys, tag):
        return p.emit("gpsimd", lambda e: e.collective_compute(
            kind, op, replica_groups=groups, ins=[a.opt() for a in ins],
            outs=[a.opt() for a in outs]),
            reads=rkeys, writes=wkeys, inc=16, tag=tag)

    G2 = [[0, 1, 2, 3], [4, 5, 6, 7]]
    G1 = [list(range(8))]

    with p.phase():
        e = dict(env, x=io["x"], wq=io["wq"], wk=io["wk"], wv=io["wv"],
                 projT=io["projT"], o_qf=io["qf"], o_ddk=io["ddk"],
                 o_v=io["vv"], o_mx=mxl)
        build_l1(sh=sh, tch=tch, env=e)
    p.barrier()
    coll("AllReduce", ALU.max, G1, [mxl], [mxg], ["mxl"], ["mxg"], "cmx")
    with p.phase():
        t = p.sb("mxt", [1, 1], F32)
        p.dma("sync", t[:], mxg, reads=["mxg"], writes=["mxt"], tag="gmx")
        for j in range(8):
            p.dma("sync", mxall[j:j + 1, :], t[:], reads=["mxt"],
                  writes=["mxall"], tag=f"smx{j}")
    p.barrier()
    with p.phase():
        e = dict(env, ddk=io["ddk"], vv=io["vv"], mxall=mxall,
                 o_kv=kvp, o_kfs=kfsp, o_vs=vsp)
        build_l2(sh=sh, tch=tch, env=e)
    p.barrier()
    coll("AllReduce", ALU.add, G2, [kvp], [kvg], ["kvp_d"], ["kvg"], "ckv")
    coll("AllReduce", ALU.add, G2, [kfsp], [kfsg], ["kfsp_d"], ["kfsg"], "ckf")
    coll("AllReduce", ALU.add, G2, [vsp], [vsg], ["vsp_d"], ["vsg"], "cvs")
    with p.phase():
        zt = p.sb("zt", [128, 512], F32)
        tt = p.sb("tt", [128, 512], F32)
        ew(p, "vector", "memset", zt[:], 0.0, writes=["zt"], tag="z")
        p.dma("sync", tt[:], kvg, reads=["kvg"], writes=["tt"], tag="gkv")
        p.dma("sync", kvp4[0], tt[:], reads=["tt"], writes=["kvp4"], tag="skv0")
        for j in range(1, 4):
            p.dma("sync", kvp4[j], zt[:], reads=["zt"], writes=["kvp4"],
                  tag=f"skvz{j}")
        p.dma("sync", tt[:, 0:8], kfsg, reads=["kfsg"], writes=["tt2"], tag="gkf")
        p.dma("sync", kfsp4[0], tt[:, 0:8], reads=["tt2"], writes=["kfsp4"],
              tag="skf0")
        for j in range(1, 4):
            p.dma("sync", kfsp4[j], zt[:, 0:8], reads=["zt"], writes=["kfsp4"],
                  tag=f"skfz{j}")
        p.dma("sync", tt[0:1, 0:512], vsg[:, 0:512], reads=["vsg"],
              writes=["tt3"], tag="gvsa")
        p.dma("sync", tt[1:2, 0:512], vsg[:, 512:1024], reads=["vsg"],
              writes=["tt3"], tag="gvsb")
        p.dma("sync", vsp4[0][:, 0:512], tt[0:1, 0:512], reads=["tt3"],
              writes=["vsp4"], tag="svs0")
        p.dma("sync", vsp4[0][:, 512:1024], tt[1:2, 0:512], reads=["tt3"],
              writes=["vsp4"], tag="svs1")
        for j in range(1, 4):
            p.dma("sync", vsp4[j][:, 0:512], zt[0:1, :], reads=["zt"],
                  writes=["vsp4"], tag=f"svsz{j}a")
            p.dma("sync", vsp4[j][:, 512:1024], zt[0:1, :], reads=["zt"],
                  writes=["vsp4"], tag=f"svsz{j}b")
    p.barrier()
    with p.phase():
        e = dict(env, x=io["x"], qf=io["qf"], kvp4=kvp4, kfsp4=kfsp4,
                 vsp4=vsp4, wo=io["wo"], w1=io["w1"], w2=io["w2"],
                 b1c=io["b1c"], b2c=io["b2c"], lng=io["lng"], o_x1=x1)
        build_l3(sh=sh, tch=tch, ttok=N, env=e)
    p.barrier()
    with p.phase():
        t = p.sb("hx", [128, 8], F32)
        p.dma("sync", t[:].rearrange("p (k o) -> p k o", o=1),
              x1.rearrange("(k p) t -> p k t", p=128)[:, :, sh - 1:sh],
              reads=["x1_d"], writes=["hx"], tag="ghx",
              allow_slow_non_contiguous=True)
        p.dma("sync", x1last, t[:], reads=["hx"], writes=["x1last"], tag="shx")
    p.barrier()
    coll("AllGather", ALU.bypass, G2, [x1last], [x1g], ["x1last"], ["x1g"], "cx1")
    with p.phase():
        g = p.sb("gx", [128, 4 * 8], F32)
        selt = p.sb("selt", [128, 4], F32)
        acc = p.sb("accx", [128, 8], F32)
        p.dma("sync", g[:].rearrange("p (j k) -> p j k", j=4),
              x1g.rearrange("(j p) k -> p j k", p=128), reads=["x1g"],
              writes=["gx"], tag="lgx")
        p.dma("sync", selt[:], io["sel1"], writes=["selt"], tag="lsel1")
        ew(p, "vector", "memset", acc[:], 0.0, writes=["accx"], tag="az")
        for j in range(4):
            ew(p, "vector", "scalar_tensor_tensor", acc[:],
               g[:, j * 8:(j + 1) * 8], selt[:, j:j + 1], acc[:],
               ALU.mult, ALU.add, reads=["gx", "selt", "accx"],
               writes=["accx"], tag=f"asel{j}")
        p.dma("sync", xprev, acc[:], reads=["accx"], writes=["xprev_d"],
              tag="sxp")
    p.barrier()
    with p.phase():
        e = dict(env, x1=x1, xprev=xprev, wrT=io["wrT"], wkT=io["wkT"],
                 wvT=io["wvT"], murc=io["murc"], mukc=io["mukc"], o_x2=x2)
        build_l4(sh=sh, tch=tch, env=e)
    p.barrier()
    with p.phase():
        t = p.sb("hx2", [128, 24], F32)
        p.dma("sync", t[:].rearrange("p (k o) -> p k o", o=3),
              x2.rearrange("(k p) t -> p k t", p=128)[:, :, sh - 3:sh],
              reads=["x2_d"], writes=["hx2"], tag="ghx2",
              allow_slow_non_contiguous=True)
        p.dma("sync", x2l3, t[:], reads=["hx2"], writes=["x2l3"], tag="shx2")
    p.barrier()
    coll("AllGather", ALU.bypass, G2, [x2l3], [x2g], ["x2l3"], ["x2g"], "cx2")
    with p.phase():
        g = p.sb("gx2", [128, 4 * 24], F32)
        selt = p.sb("selt3", [128, 4], F32)
        acc = p.sb("accx2", [128, 24], F32)
        p.dma("sync", g[:].rearrange("p (j k) -> p j k", j=4),
              x2g.rearrange("(j p) k -> p j k", p=128), reads=["x2g"],
              writes=["gx2"], tag="lgx2")
        p.dma("sync", selt[:], io["sel3"], writes=["selt3"], tag="lsel3")
        ew(p, "vector", "memset", acc[:], 0.0, writes=["accx2"], tag="az2")
        for j in range(4):
            ew(p, "vector", "scalar_tensor_tensor", acc[:],
               g[:, j * 24:(j + 1) * 24], selt[:, j:j + 1], acc[:],
               ALU.mult, ALU.add, reads=["gx2", "selt3", "accx2"],
               writes=["accx2"], tag=f"asel2{j}")
        p.dma("sync", x2p3, acc[:], reads=["accx2"], writes=["x2p3_d"],
              tag="sxp2")
    p.barrier()
    with p.phase():
        e = dict(env, x2=x2, x2p3=x2p3, winT=io["winT"], wdtT=io["wdtT"],
                 cwc=io["cwc"], cbc=io["cbc"], bdtc=io["bdtc"],
                 dtbc=io["dtbc"], alogc=io["alogc"], bparc=io["bparc"],
                 o_dt=dtD, o_uu=uuD, o_sg=sgD, o_send=sendl, o_gend=gendl)
        build_l5(sh=sh, tch=tch, env=e)
    p.barrier()
    coll("AllGather", ALU.bypass, G2, [sendl], [sendg], ["sendl"], ["sendg"],
         "cse")
    coll("AllGather", ALU.bypass, G2, [gendl], [gendg], ["gendl"], ["gendg"],
         "cge")
    p.barrier()
    with p.phase():
        e = dict(env, dt=dtD, uu=uuD, sg=sgD, x2=x2,
                 send4=sendg.rearrange("(g p) t -> g p t", p=128),
                 gend4=gendg.rearrange("(g p) t -> g p t", p=128),
                 sel4=io["sel4"], alogc=io["alogc"], bparc=io["bparc"],
                 cparc=io["cparc"], dparc=io["dparc"], woutT=io["woutT"],
                 o_x3=x3)
        build_l6(sh=sh, tch=tch, env=e)
    p.barrier()
    with p.phase():
        e = dict(env, x3=x3, wg=io["wg"], we1f=io["we1f"], we3f=io["we3f"],
                 we2f=io["we2f"], o_y=io["o_y"])
        build_l7(sh=sh, tch=tch, env=e)
    p.barrier()
    p.final_wait("sync")
    p.build()
    p.finish()
    return nc


def _prep_core_inputs(ii):
    x_full = np.asarray(ii["L0_attn_x"])
    base = dict(
        wq=np.ascontiguousarray(ii["L0_attn_Wq"].reshape(D, D)),
        wk=np.ascontiguousarray(ii["L0_attn_Wk"].reshape(D, D)),
        wv=np.ascontiguousarray(ii["L0_attn_Wv"].reshape(D, D)),
        projT=np.ascontiguousarray(np.asarray(ii["L0_attn_proj"]).T),
        wo=np.ascontiguousarray(ii["L0_attn_Wo"].reshape(D, D)),
        w1=np.asarray(ii["L0_attn_W1"]), w2=np.asarray(ii["L0_attn_W2"]),
        b1c=_rsh(ii["L0_attn_b1"], 32), b2c=_rsh(ii["L0_attn_b2"], 8),
        ln1g=_rsh(ii["L0_attn_ln1_g"], 8), ln1b=_rsh(ii["L0_attn_ln1_b"], 8),
        ln2g=_rsh(ii["L0_attn_ln2_g"], 8), ln2b=_rsh(ii["L0_attn_ln2_b"], 8),
        wrT=np.ascontiguousarray(np.asarray(ii["L0_ffn_W_r"]).T),
        wkT=np.ascontiguousarray(np.asarray(ii["L0_ffn_W_k"]).T),
        wvT=np.ascontiguousarray(np.asarray(ii["L0_ffn_W_v"]).T),
        murc=_rsh(ii["L0_ffn_mu_r"], 8), mukc=_rsh(ii["L0_ffn_mu_k"], 8),
        winT=np.ascontiguousarray(np.asarray(ii["L1_attn_W_in"]).T),
        wdtT=np.ascontiguousarray(np.asarray(ii["L1_attn_W_dt"]).T),
        cwc=_rsh2(np.asarray(ii["L1_attn_conv_w"])),
        cbc=_rsh(ii["L1_attn_conv_b"], 16),
        bdtc=_rsh(ii["L1_attn_b_dt"], 16),
        dtbc=_rsh(ii["L1_attn_dt_bias"], 16),
        alogc=_rsh2(np.asarray(ii["L1_attn_A_log"])),
        bparc=_rsh2(np.asarray(ii["L1_attn_B_param"])),
        cparc=_rsh2(np.asarray(ii["L1_attn_C_param"])),
        dparc=_rsh(ii["L1_attn_D_param"], 16),
        woutT=np.ascontiguousarray(np.asarray(ii["L1_attn_W_out"]).T),
        wg=np.asarray(ii["L1_ffn_Wg"]),
        we1f=np.ascontiguousarray(np.concatenate(
            [np.asarray(ii["L1_ffn_We1"]).reshape(8 * D, FF_DIM),
             np.asarray(ii["L1_ffn_Ws1"])], 0)),
        we3f=np.ascontiguousarray(np.concatenate(
            [np.asarray(ii["L1_ffn_We3"]).reshape(8 * D, FF_DIM),
             np.asarray(ii["L1_ffn_Ws3"])], 0)),
        we2f=np.ascontiguousarray(np.concatenate(
            [np.asarray(ii["L1_ffn_We2"]).reshape(8 * FF_DIM, D),
             np.asarray(ii["L1_ffn_Ws2"])], 0)),
    )
    maps = []
    for c in range(8):
        b, q = c // 4, c % 4
        m = dict(base)
        m["x"] = np.ascontiguousarray(x_full[b, q * SH:(q + 1) * SH, :].T)
        for nm in ("sel1", "sel3"):
            s = np.zeros((128, 4), np.float32)
            if q > 0:
                s[:, q - 1] = 1.0
            m[nm] = s
        s4 = np.zeros((128, 4), np.float32)
        s4[:, q] = 1.0
        m["sel4"] = s4
        maps.append(m)
    return maps


def kernel_fused(**inputs):
    maps = _prep_core_inputs(inputs)
    nc = _get_nc("fused", build_fused)
    res = run_bass_kernel_spmd(nc, maps, core_ids=list(range(8))).results
    out = np.empty((2, N, D), np.float32)
    for b in range(2):
        for q in range(4):
            out[b, q * SH:(q + 1) * SH, :] = res[b * 4 + q]["y"].T
    return out



def build_fusedA(sh=SH, tch=512):
    nc = bass.Bass("TRN2", target_bir_lowering=False)
    EIn = lambda nm, shp, dt=F32: nc.dram_tensor(nm, shp, dt, kind="ExternalInput").ap()
    EOut = lambda nm, shp, dt=F32: nc.dram_tensor(nm, shp, dt, kind="ExternalOutput").ap()
    Int = lambda nm, shp, dt=F32: nc.dram_tensor(nm, shp, dt, kind="Internal").ap()
    x = EIn("x", [D, sh])
    wq = EIn("wq", [D, D]); wk = EIn("wk", [D, D]); wv = EIn("wv", [D, D])
    projT = EIn("projT", [DH, M])
    o_qf = EOut("qf", [D, sh], BF16)
    o_kv = EOut("kvp", [128, 512]); o_kfs = EOut("kfsp", [128, 8])
    o_vs = EOut("vsp", [1, 1024]); o_mx = EOut("mxloc", [1, 1])
    ddk = Int("ddk_i", [D, sh]); vv = Int("vv_i", [D, sh], BF16)
    mxall = Int("mxall_i", [8, 1])
    p = Prog(nc)
    env = {"nc": nc, "p": p}
    with p.phase():
        build_l1(sh=sh, tch=tch, env=dict(env, x=x, wq=wq, wk=wk, wv=wv,
                                          projT=projT, o_qf=o_qf, o_ddk=ddk,
                                          o_v=vv, o_mx=o_mx))
    p.barrier()
    with p.phase():
        t = p.sb("mxt", [1, 1], F32)
        p.dma("sync", t[:], o_mx, writes=["mxt"], tag="gmx")
        for j in range(8):
            p.dma("sync", mxall[j:j + 1, :], t[:], reads=["mxt"],
                  writes=["mxall"], tag=f"smx{j}")
    p.barrier()
    with p.phase():
        build_l2(sh=sh, tch=tch, env=dict(env, ddk=ddk, vv=vv, mxall=mxall,
                                          o_kv=o_kv, o_kfs=o_kfs, o_vs=o_vs))
    p.barrier()
    p.final_wait("sync")
    p.build()
    p.finish()
    return nc


def build_fusedD(sh=SH, tch=512):
    nc = bass.Bass("TRN2", target_bir_lowering=False)
    EIn = lambda nm, shp, dt=F32: nc.dram_tensor(nm, shp, dt, kind="ExternalInput").ap()
    Int = lambda nm, shp, dt=F32: nc.dram_tensor(nm, shp, dt, kind="Internal").ap()
    dt_ = EIn("dt", [D_INNER, sh]); uu = EIn("uu", [D_INNER, sh])
    sg = EIn("sg", [D_INNER, sh]); x2 = EIn("x2", [D, sh])
    send4 = EIn("send4", [4, 128, 256]); gend4 = EIn("gend4", [4, 128, 256])
    sel4 = EIn("sel4", [128, 4])
    alogc = EIn("alogc", [128, 256]); bparc = EIn("bparc", [128, 256])
    cparc = EIn("cparc", [128, 256]); dparc = EIn("dparc", [128, 16])
    woutT = EIn("woutT", [D_INNER, D])
    wg = EIn("wg", [D, 8])
    we1f = EIn("we1f", [9 * D, FF_DIM]); we3f = EIn("we3f", [9 * D, FF_DIM])
    we2f = EIn("we2f", [9 * FF_DIM, D])
    o_y = nc.dram_tensor("y", [D, sh], F32, kind="ExternalOutput").ap()
    x3 = Int("x3_i", [D, sh])
    p = Prog(nc)
    env = {"nc": nc, "p": p}
    with p.phase():
        build_l6(sh=sh, tch=tch, env=dict(env, dt=dt_, uu=uu, sg=sg, x2=x2,
                                          send4=send4, gend4=gend4, sel4=sel4,
                                          alogc=alogc, bparc=bparc, cparc=cparc,
                                          dparc=dparc, woutT=woutT, o_x3=x3))
    p.barrier()
    with p.phase():
        build_l7(sh=sh, tch=tch, env=dict(env, x3=x3, wg=wg, we1f=we1f,
                                          we3f=we3f, we2f=we2f, o_y=o_y))
    p.barrier()
    p.final_wait("sync")
    p.build()
    p.finish()
    return nc


def build_l3a(sh=SH, tch=512):
    nc = bass.Bass("TRN2", target_bir_lowering=False)
    EIn = lambda nm, shp, dt=F32: nc.dram_tensor(nm, shp, dt, kind="ExternalInput").ap()
    x = EIn("x", [D, sh]); qf = EIn("qf", [D, sh], BF16)
    kvp4 = EIn("kvp4", [4, 128, 512]); kfsp4 = EIn("kfsp4", [4, 128, 8])
    vsp4 = EIn("vsp4", [4, 1, 1024])
    wo = EIn("wo", [D, D]); w1 = EIn("w1", [D, DFF]); w2 = EIn("w2", [DFF, D])
    b1c = EIn("b1c", [128, 32]); b2c = EIn("b2c", [128, 8])
    lng = {nm: EIn(nm, [128, 8]) for nm in ("ln1g", "ln1b", "ln2g", "ln2b")}
    mxg4 = EIn("mxg4", [1, 4]); mx8 = EIn("mx8", [1, 8])
    o_x1 = nc.dram_tensor("x1", [D, sh], F32, kind="ExternalOutput").ap()
    p = Prog(nc)
    env = {"nc": nc, "p": p}
    with p.phase():
        build_l3(sh=sh, tch=tch, ttok=N, env=dict(
            env, x=x, qf=qf, kvp4=kvp4, kfsp4=kfsp4, vsp4=vsp4, wo=wo, w1=w1,
            w2=w2, b1c=b1c, b2c=b2c, lng=lng, o_x1=o_x1, mxg4=mxg4, mx8=mx8))
    p.barrier()
    p.final_wait("sync")
    p.build()
    p.finish()
    return nc


def kernel_v5(**inputs):
    ii = inputs
    maps = _prep_core_inputs(ii)
    ncA = _get_nc("fA", build_fusedA)
    rA = run_bass_kernel_spmd(ncA, [dict(x=m["x"], wq=m["wq"], wk=m["wk"],
                                         wv=m["wv"], projT=m["projT"])
                                    for m in maps],
                              core_ids=list(range(8))).results
    mx8 = np.concatenate([rA[c]["mxloc"] for c in range(8)], 1).astype(np.float32)
    nc3 = _get_nc("l3a", build_l3a)
    im3 = []
    for c in range(8):
        g0 = (c // 4) * 4
        m = maps[c]
        im3.append(dict(
            x=m["x"], qf=rA[c]["qf"],
            kvp4=np.stack([rA[g0 + j]["kvp"] for j in range(4)]),
            kfsp4=np.stack([rA[g0 + j]["kfsp"] for j in range(4)]),
            vsp4=np.stack([rA[g0 + j]["vsp"] for j in range(4)]),
            wo=m["wo"], w1=m["w1"], w2=m["w2"], b1c=m["b1c"], b2c=m["b2c"],
            ln1g=m["ln1g"], ln1b=m["ln1b"], ln2g=m["ln2g"], ln2b=m["ln2b"],
            mxg4=np.ascontiguousarray(mx8[:, g0:g0 + 4]), mx8=mx8))
    r3 = run_bass_kernel_spmd(nc3, im3, core_ids=list(range(8))).results

    nc4 = _get_nc("l4", build_l4)
    im4 = []
    for c in range(8):
        if c % 4 == 0:
            xprev = np.zeros((128, 8), np.float32)
        else:
            xprev = _fm3(r3[c - 1]["x1"][:, SH - 1:SH])
        m = maps[c]
        im4.append(dict(x1=r3[c]["x1"], xprev=np.ascontiguousarray(xprev),
                        wrT=m["wrT"], wkT=m["wkT"], wvT=m["wvT"],
                        murc=m["murc"], mukc=m["mukc"]))
    r4 = run_bass_kernel_spmd(nc4, im4, core_ids=list(range(8))).results

    nc5 = _get_nc("l5", build_l5)
    im5 = []
    for c in range(8):
        if c % 4 == 0:
            p3 = np.zeros((128, 24), np.float32)
        else:
            p3 = _fm3(r4[c - 1]["x2"][:, SH - 3:SH]).reshape(128, 24)
        m = maps[c]
        im5.append(dict(x2=r4[c]["x2"], x2p3=np.ascontiguousarray(p3),
                        winT=m["winT"], wdtT=m["wdtT"], cwc=m["cwc"],
                        cbc=m["cbc"], bdtc=m["bdtc"], dtbc=m["dtbc"],
                        alogc=m["alogc"], bparc=m["bparc"]))
    r5 = run_bass_kernel_spmd(nc5, im5, core_ids=list(range(8))).results

    ncD = _get_nc("fD", build_fusedD)
    imD = []
    for c in range(8):
        g0 = (c // 4) * 4
        m = maps[c]
        imD.append(dict(dt=r5[c]["dt"], uu=r5[c]["uu"], sg=r5[c]["sg"],
                        x2=r4[c]["x2"],
                        send4=np.stack([r5[g0 + j]["send"] for j in range(4)]),
                        gend4=np.stack([r5[g0 + j]["gend"] for j in range(4)]),
                        sel4=m["sel4"], alogc=m["alogc"], bparc=m["bparc"],
                        cparc=m["cparc"], dparc=m["dparc"], woutT=m["woutT"],
                        wg=m["wg"], we1f=m["we1f"], we3f=m["we3f"],
                        we2f=m["we2f"]))
    rD = run_bass_kernel_spmd(ncD, imD, core_ids=list(range(8))).results
    out = np.empty((2, N, D), np.float32)
    for b in range(2):
        for q in range(4):
            out[b, q * SH:(q + 1) * SH, :] = rD[b * 4 + q]["y"].T
    return out


def build_fusedB(sh=SH, tch=512):
    nc = bass.Bass("TRN2", target_bir_lowering=False)
    EIn = lambda nm, shp, dt=F32: nc.dram_tensor(nm, shp, dt, kind="ExternalInput").ap()
    Int = lambda nm, shp, dt=F32: nc.dram_tensor(nm, shp, dt, kind="Internal").ap()
    x = EIn("x", [D, sh]); qf = EIn("qf", [D, sh], BF16)
    kvp4 = EIn("kvp4", [4, 128, 512]); kfsp4 = EIn("kfsp4", [4, 128, 8])
    vsp4 = EIn("vsp4", [4, 1, 1024])
    wo = EIn("wo", [D, D]); w1 = EIn("w1", [D, DFF]); w2 = EIn("w2", [DFF, D])
    b1c = EIn("b1c", [128, 32]); b2c = EIn("b2c", [128, 8])
    lng = {nm: EIn(nm, [128, 8]) for nm in ("ln1g", "ln1b", "ln2g", "ln2b")}
    mxg4 = EIn("mxg4", [1, 4]); mx8 = EIn("mx8", [1, 8])
    x_h = EIn("x_h", [128, 32]); qf_h = EIn("qf_h", [128, 32], BF16)
    mask1 = EIn("mask1", [128, 1])
    wrT = EIn("wrT", [D, D]); wkT = EIn("wkT", [D, D]); wvT = EIn("wvT", [D, D])
    murc = EIn("murc", [128, 8]); mukc = EIn("mukc", [128, 8])
    winT = EIn("winT", [D, 2 * D_INNER]); wdtT = EIn("wdtT", [D_INNER, D_INNER])
    cwc = EIn("cwc", [128, 64]); cbc = EIn("cbc", [128, 16])
    bdtc = EIn("bdtc", [128, 16]); dtbc = EIn("dtbc", [128, 16])
    alogc = EIn("alogc", [128, 256]); bparc = EIn("bparc", [128, 256])
    o_dt = nc.dram_tensor("dt", [D_INNER, sh], F32, kind="ExternalOutput").ap()
    o_uu = nc.dram_tensor("uu", [D_INNER, sh], F32, kind="ExternalOutput").ap()
    o_sg = nc.dram_tensor("sg", [D_INNER, sh], F32, kind="ExternalOutput").ap()
    o_send = nc.dram_tensor("send", [128, 256], F32, kind="ExternalOutput").ap()
    o_gend = nc.dram_tensor("gend", [128, 256], F32, kind="ExternalOutput").ap()
    o_x2 = nc.dram_tensor("x2", [D, sh], F32, kind="ExternalOutput").ap()
    x1 = Int("x1_i", [D, sh])
    x1h = Int("x1h_i", [128, 32])
    xprev = Int("xprev_i", [128, 8])
    x2p3 = Int("x2p3_i", [128, 24])
    p = Prog(nc)
    env = {"nc": nc, "p": p}
    with p.phase():
        build_l3(sh=sh, tch=tch, ttok=N, env=dict(
            env, x=x, qf=qf, kvp4=kvp4, kfsp4=kfsp4, vsp4=vsp4, wo=wo, w1=w1,
            w2=w2, b1c=b1c, b2c=b2c, lng=lng, o_x1=x1, mxg4=mxg4, mx8=mx8,
            x_h=x_h, qf_h=qf_h, mask1=mask1, o_x1h=x1h))
    p.barrier()
    with p.phase():
        t = p.sb("xpv", [128, 8], F32)
        p.dma("sync", t[:].rearrange("p (k o) -> p k o", o=1),
              x1h.rearrange("p (k t) -> p k t", t=4)[:, :, 3:4],
              writes=["xpv"], tag="gxp", allow_slow_non_contiguous=True)
        p.dma("sync", xprev, t[:], reads=["xpv"], writes=["xprev_d"], tag="sxp")
    p.barrier()
    with p.phase():
        build_l4(sh=sh, tch=tch, env=dict(
            env, x1=x1, xprev=xprev, wrT=wrT, wkT=wkT, wvT=wvT, murc=murc,
            mukc=mukc, o_x2=o_x2, x1h=x1h, mask1=mask1, o_x2p3=x2p3))
    p.barrier()
    with p.phase():
        build_l5(sh=sh, tch=tch, env=dict(
            env, x2=o_x2, x2p3=x2p3, winT=winT, wdtT=wdtT, cwc=cwc, cbc=cbc,
            bdtc=bdtc, dtbc=dtbc, alogc=alogc, bparc=bparc, o_dt=o_dt,
            o_uu=o_uu, o_sg=o_sg, o_send=o_send, o_gend=o_gend))
    p.barrier()
    p.final_wait("sync")
    p.build()
    p.finish()
    return nc


def kernel_v3(**inputs):
    ii = inputs
    maps = _prep_core_inputs(ii)
    ncA = _get_nc("fA", build_fusedA)
    rA = run_bass_kernel_spmd(ncA, [dict(x=m["x"], wq=m["wq"], wk=m["wk"],
                                         wv=m["wv"], projT=m["projT"])
                                    for m in maps],
                              core_ids=list(range(8))).results
    mx8 = np.concatenate([rA[c]["mxloc"] for c in range(8)], 1).astype(np.float32)
    ncB = _get_nc("fB", build_fusedB)
    imB = []
    for c in range(8):
        g0 = (c // 4) * 4
        m = maps[c]
        q = c % 4
        if q == 0:
            x_h = np.zeros((128, 32), np.float32)
            qf_h = np.zeros((128, 32), np.float32).astype(rA[c]["qf"].dtype)
            mk1 = np.zeros((128, 1), np.float32)
        else:
            x_h = _fm3(np.ascontiguousarray(m_prev_x(maps, c)[:, SH - 4:SH]))
            qf_h = _fm3(np.ascontiguousarray(
                np.asarray(rA[c - 1]["qf"])[:, SH - 4:SH]))
            mk1 = np.ones((128, 1), np.float32)
        imB.append(dict(
            x=m["x"], qf=rA[c]["qf"],
            kvp4=np.stack([rA[g0 + j]["kvp"] for j in range(4)]),
            kfsp4=np.stack([rA[g0 + j]["kfsp"] for j in range(4)]),
            vsp4=np.stack([rA[g0 + j]["vsp"] for j in range(4)]),
            wo=m["wo"], w1=m["w1"], w2=m["w2"], b1c=m["b1c"], b2c=m["b2c"],
            ln1g=m["ln1g"], ln1b=m["ln1b"], ln2g=m["ln2g"], ln2b=m["ln2b"],
            mxg4=np.ascontiguousarray(mx8[:, g0:g0 + 4]), mx8=mx8,
            x_h=np.ascontiguousarray(x_h), qf_h=np.ascontiguousarray(qf_h),
            mask1=mk1, wrT=m["wrT"], wkT=m["wkT"], wvT=m["wvT"],
            murc=m["murc"], mukc=m["mukc"], winT=m["winT"], wdtT=m["wdtT"],
            cwc=m["cwc"], cbc=m["cbc"], bdtc=m["bdtc"], dtbc=m["dtbc"],
            alogc=m["alogc"], bparc=m["bparc"]))
    rB = run_bass_kernel_spmd(ncB, imB, core_ids=list(range(8))).results

    ncD = _get_nc("fD2", build_fusedD2)
    imD = []
    for c in range(8):
        g0 = (c // 4) * 4
        m = maps[c]
        imD.append(dict(dt=rB[c]["dt"], uu=rB[c]["uu"], sg=rB[c]["sg"],
                        x2=rB[c]["x2"],
                        send4=np.stack([rB[g0 + j]["send"] for j in range(4)]),
                        gend4=np.stack([rB[g0 + j]["gend"] for j in range(4)]),
                        sel4=m["sel4"], alogc=m["alogc"], bparc=m["bparc"],
                        cparc=m["cparc"], dparc=m["dparc"], woutT=m["woutT"],
                        wg=m["wg"], we1f=m["we1f"], we3f=m["we3f"],
                        we2f=m["we2f"]))
    rD = run_bass_kernel_spmd(ncD, imD, core_ids=list(range(8))).results
    out = np.empty((2, N, D), np.float32)
    for b in range(2):
        for q in range(4):
            out[b, q * SH:(q + 1) * SH, :] = rD[b * 4 + q]["y"].T
    return out


def m_prev_x(maps, c):
    return maps[c - 1]["x"]


def build_fusedD2(sh=SH, tch=512):
    NIC = D_INNER // 128
    NE = 9
    nch = sh // tch
    nc = bass.Bass("TRN2", target_bir_lowering=False)
    EIn = lambda nm, shp, dt=F32: nc.dram_tensor(nm, shp, dt, kind="ExternalInput").ap()
    dt_ = EIn("dt", [D_INNER, sh]); uu = EIn("uu", [D_INNER, sh])
    sg = EIn("sg", [D_INNER, sh]); x2 = EIn("x2", [D, sh])
    send4 = EIn("send4", [4, 128, 256]); gend4 = EIn("gend4", [4, 128, 256])
    sel4 = EIn("sel4", [128, 4])
    alogc = EIn("alogc", [128, 256]); bparc = EIn("bparc", [128, 256])
    cparc = EIn("cparc", [128, 256]); dparc = EIn("dparc", [128, 16])
    woutT = EIn("woutT", [D_INNER, D])
    wg = EIn("wg", [D, 8])
    we1f = EIn("we1f", [9 * D, FF_DIM]); we3f = EIn("we3f", [9 * D, FF_DIM])
    we2f = EIn("we2f", [9 * FF_DIM, D])
    o_y = nc.dram_tensor("y", [D, sh], F32, kind="ExternalOutput").ap()
    we1b = nc.dram_tensor("we1b", [128, NE * 16 * 1024], BF16, kind="Internal").ap()
    we3b = nc.dram_tensor("we3b", [128, NE * 16 * 1024], BF16, kind="Internal").ap()
    we2b = nc.dram_tensor("we2b", [128, NE * 16 * D], BF16, kind="Internal").ap()

    wob_scr = nc.dram_tensor("wob_scr", [128, 8 * NIC * 128], BF16,
                             kind="Internal").ap()
    p = Prog(nc)

    wot = [p.sb(f"wot{i}", [128, NIC * 128], BF16) for i in range(2)]
    negA = p.sb("negA", [128, 256], F32)
    bpar = p.sb("bpar", [128, 256], F32)
    cpar = p.sb("cpar", [128, 256], F32)
    dpar = p.sb("dpar", [128, 16], F32)
    sends = p.sb("sends", [128, 4 * 256], F32)
    gends = p.sb("gends", [128, 4 * 256], F32)
    sel = p.sb("sel", [128, 4], F32)
    carry = p.sb("carry", [128, 256], F32)
    Tst = p.sb("Tst", [128, 256], F32)
    wgb = p.sb("wgb", [128, 64], BF16)
    identf = p.sb("identf", [128, 128], F32)
    esel = p.sb("esel", [8, 8 * 128], F32)
    wtf = p.sb("wtf", [128, 1024], F32)
    wtb = p.sb("wtb", [128, 1024], BF16)
    cdt = [p.sb(f"cdt{i}", [128, tch], F32) for i in range(2)]
    cuu = [p.sb(f"cuu{i}", [128, tch], F32) for i in range(2)]
    cdtu = [p.sb("cdtu0", [128, tch], F32)] * 2
    cdA = [p.sb(f"cdA{i}", [128, tch], F32) for i in range(2)]
    cdBu = [p.sb(f"cdBu{i}", [128, tch], F32) for i in range(2)]
    cS = p.sb("cS", [128, tch], F32)
    cya = p.sb("cya", [128, tch], F32)
    csg = p.sb("csg", [128, tch], F32)
    ygb = [p.sb("ygb0", [128, NIC * tch], BF16)] * 2
    x2c = [p.sb(f"x2c{i}", [128, tch], F32) for i in range(2)]
    x3f = [p.sb(f"x3f{i}", [128, 8 * tch], F32) for i in range(2)]
    x3b = [p.sb("x3b0", [128, 8 * tch], BF16)] * 2
    w1t3 = [p.sb(f"w1t3{i}", [128, 8 * 128], BF16) for i in range(2)]
    w3t3 = [p.sb(f"w3t3{i}", [128, 8 * 128], BF16) for i in range(2)]
    w2t2 = [p.sb(f"w2t2{i}", [128, 16 * 128], BF16) for i in range(2)]
    racc = p.sb("racc", [128, 8 * tch], F32)
    hwb = p.sb("hwb", [128, 16 * tch], BF16)
    stmp2 = [p.sb(f"stmp2{i}", [128, tch], F32) for i in range(2)]
    wbs = p.sb("wbs", [128, tch], F32)
    g_fm = p.sb("g_fm", [8, tch], F32)
    w_fmt = p.sb("w_fmt", [8, tch], F32)
    g_tm = p.sb("g_tm", [128, 8], F32)
    top8 = p.sb("top8", [128, 8], F32)
    rep = p.sb("rep", [128, 8], F32)
    gz = p.sb("gz", [128, 8], F32)
    w_tm = p.sb("w_tm", [128, 8], F32)
    s2c = p.sb("s2c", [128, 1], F32)
    zf = p.sb("zf", [128, 8 * tch], F32)
    aux = make_ln_aux(p, "ln", tch)

    pC = p.ps("pC", [128, tch], F32)
    pD = p.ps("pD", [128, tch], F32)
    pE = p.ps("pE", [128, tch], F32)
    pF = p.ps("pF", [128, tch], F32)
    pA = p.ps("pA", [128, tch], F32)
    pB = p.ps("pB", [128, tch], F32)
    ps1 = p.ps("ps1", [1, tch], F32)
    ps2 = p.ps("ps2", [1, tch], F32)
    aux.update(ps1=ps1, ps2=ps2, pbm=pA, pba=pB, utile=cya)
    aux["keys"].update(ps1="ps1", ps2="ps2", pbm="pA", pba="pB", utile="cya",
                       epscol="epsc")
    ec2, eck2 = p.constcol(EPS_LN_OUTER)
    aux["epscol"] = ec2
    aux["keys"]["epscol"] = eck2

    make_identity(p, identf[:], "identf")
    p.emit("gpsimd", lambda e: e.memset(esel[:], 0.0), writes=["esel"], tag="es0")
    p.emit("gpsimd", lambda e: e.affine_select(
        out=esel[:].rearrange("j (e q) -> j e q", q=128),
        in_=esel[:].rearrange("j (e q) -> j e q", q=128),
        compare_op=ALU.not_equal, fill=1.0, base=0,
        pattern=[[-1, 8], [0, 128]], channel_multiplier=1),
        reads=["esel"], writes=["esel"], tag="es1")
    for g in range(NIC):
        p.dma("sync", wtf[:, 0:D],
              woutT.rearrange("(k p) m -> p k m", p=128)[:, g, :],
              writes=["wtf"], tag=f"wo{g}")
        act(p, wtb[:, 0:D], wtf[:, 0:D], AF.Copy,
            reads=["wtf"], writes=["wtb"], tag=f"woc{g}")
        p.dma("sync",
              wob_scr.rearrange("p (o k f) -> p o k f", k=NIC, f=128)[:, :, g, :],
              wtb[:, 0:D].rearrange("p (o f) -> p o f", f=128),
              reads=["wtb"], writes=["wob_hbm"], tag=f"wos{g}")
    p.dma("sync", wtf[:, 0:64].rearrange("p (k e) -> p k e", e=8),
          wg.rearrange("(k p) e -> p k e", p=128), writes=["wtf"], tag="lwg")
    act(p, wgb[:], wtf[:, 0:64], AF.Copy, reads=["wtf"], writes=["wgb"], tag="cwg")
    p.dma("sync", negA[:], alogc, writes=["negA0"], tag="lda")
    act(p, negA[:], negA[:], AF.Exp, reads=["negA0"], writes=["negA1"], tag="na1")
    ew(p, "vector", "tensor_scalar_mul", negA[:], negA[:], -1.0,
       reads=["negA1"], writes=["negA"], tag="na2")
    p.dma("sync", bpar[:], bparc, writes=["bpar"], tag="ldb")
    p.dma("sync", cpar[:], cparc, writes=["cpar"], tag="ldc")
    p.dma("sync", dpar[:], dparc, writes=["dpar"], tag="ldd")
    p.dma("sync", sends[:].rearrange("p (g t) -> p g t", g=4),
          send4.rearrange("g p t -> p g t"), writes=["sends"], tag="lds")
    p.dma("sync", gends[:].rearrange("p (g t) -> p g t", g=4),
          gend4.rearrange("g p t -> p g t"), writes=["gends"], tag="ldg")
    p.dma("sync", sel[:], sel4, writes=["sel"], tag="ldsel")
    ew(p, "vector", "memset", Tst[:], 0.0, writes=["Tst"], tag="T0")
    ew(p, "vector", "memset", carry[:], 0.0, writes=["carry"], tag="c00")
    for j in range(4):
        ew(p, "vector", "scalar_tensor_tensor", carry[:], Tst[:],
           sel[:, j:j + 1], carry[:], ALU.mult, ALU.add,
           reads=["Tst", "sel", "carry"], writes=["carry"], tag=f"s0a{j}")
        if j < 3:
            ew(p, "vector", "tensor_mul", Tst[:], Tst[:],
               gends[:, j * 256:(j + 1) * 256], reads=["Tst", "gends"],
               writes=["Tst"], tag=f"Tm{j}")
            ew(p, "vector", "tensor_add", Tst[:], Tst[:],
               sends[:, j * 256:(j + 1) * 256], reads=["Tst", "sends"],
               writes=["Tst"], tag=f"Ta{j}")
    for (srcT, dstT, key) in [(we1f, we1b, "we1_hbm"), (we3f, we3b, "we3_hbm")]:
        for e in range(NE):
            for k in range(8):
                for hh in range(2):
                    p.dma("sync", wtf[:],
                          srcT.rearrange("(q p) m -> p q m", p=128)
                          [:, e * 8 + k, hh * 1024:(hh + 1) * 1024],
                          writes=["wtf"], tag=f"pl{key}{e}_{k}_{hh}")
                    act(p, wtb[:], wtf[:], AF.Copy, reads=["wtf"],
                        writes=["wtb"], tag=f"pc{key}{e}_{k}_{hh}")
                    p.dma("sync",
                          dstT.rearrange("p (q g kk f) -> p q g kk f", g=16,
                                         kk=8, f=128)
                          [:, e, hh * 8:(hh + 1) * 8, k, :],
                          wtb[:].rearrange("p (g f) -> p g f", f=128),
                          reads=["wtb"], writes=[key], tag=f"ps{key}{e}_{k}_{hh}")
    for e in range(NE):
        for k in range(16):
            p.dma("sync", wtf[:, 0:D],
                  we2f.rearrange("(q p) m -> p q m", p=128)[:, e * 16 + k, :],
                  writes=["wtf"], tag=f"plw2{e}_{k}")
            act(p, wtb[:, 0:D], wtf[:, 0:D], AF.Copy, reads=["wtf"],
                writes=["wtb"], tag=f"pcw2{e}_{k}")
            p.dma("sync",
                  we2b.rearrange("p (q o kk f) -> p q o kk f", o=8, kk=16,
                                 f=128)[:, e, :, k, :],
                  wtb[:, 0:D].rearrange("p (o f) -> p o f", f=128),
                  reads=["wtb"], writes=["we2_hbm"], tag=f"psw2{e}_{k}")

    dt_r = dt_.rearrange("(i p) t -> p i t", p=128)
    uu_r = uu.rearrange("(i p) t -> p i t", p=128)
    sg_r = sg.rearrange("(i p) t -> p i t", p=128)
    x2_r = x2.rearrange("(k p) t -> p k t", p=128)
    y_r = o_y.rearrange("(k p) t -> p k t", p=128)

    for tc in range(nch):
        bi = tc % 2
        s = slice(tc * tch, (tc + 1) * tch)
        for ci in range(NIC):
            p.dma("sync", cdt[bi][:], dt_r[:, ci, s], writes=[f"cdt{bi}"],
                  tag=f"ldt{tc}_{ci}")
            p.dma("sync", cuu[bi][:], uu_r[:, ci, s], writes=[f"cuu{bi}"],
                  tag=f"luu{tc}_{ci}")
            ew(p, "vector", "tensor_mul", cdtu[bi][:], cdt[bi][:], cuu[bi][:],
               reads=[f"cdt{bi}", f"cuu{bi}"], writes=[f"cdtu{bi}"],
               tag=f"mdu{tc}_{ci}")
            for n in range(D_STATE):
                col = ci * D_STATE + n
                nb = n % 2
                act(p, cdA[nb][:], cdt[bi][:], AF.Exp,
                    scale=negA[:, col:col + 1],
                    reads=[f"cdt{bi}", "negA"], writes=[f"cdA{nb}"],
                    tag=f"bA{tc}_{ci}_{n}")
                act(p, cdBu[nb][:], cdtu[bi][:], AF.Copy,
                    scale=bpar[:, col:col + 1],
                    reads=[f"cdtu{bi}", "bpar"], writes=[f"cdBu{nb}"],
                    tag=f"bB{tc}_{ci}_{n}")
                ew(p, "vector", "tensor_tensor_scan", cS[:], cdA[nb][:],
                   cdBu[nb][:], carry[:, col:col + 1], ALU.mult, ALU.add,
                   reads=[f"cdA{nb}", f"cdBu{nb}", "carry"], writes=["cS"],
                   tag=f"sc{tc}_{ci}_{n}")
                ew(p, "vector", "tensor_copy", carry[:, col:col + 1],
                   cS[:, tch - 1:tch], reads=["cS"], writes=["carry"],
                   tag=f"cc{tc}_{ci}_{n}")
                if n == 0:
                    ew(p, "vector", "tensor_scalar_mul", cya[:], cS[:],
                       cpar[:, col:col + 1], reads=["cS", "cpar"],
                       writes=["cya"], tag=f"y0{tc}_{ci}")
                else:
                    ew(p, "vector", "scalar_tensor_tensor", cya[:], cS[:],
                       cpar[:, col:col + 1], cya[:], ALU.mult, ALU.add,
                       reads=["cS", "cpar", "cya"], writes=["cya"],
                       tag=f"ya{tc}_{ci}_{n}")
            ew(p, "vector", "scalar_tensor_tensor", cya[:], cuu[bi][:],
               dpar[:, ci:ci + 1], cya[:], ALU.mult, ALU.add,
               reads=[f"cuu{bi}", "dpar", "cya"], writes=["cya"],
               tag=f"yd{tc}_{ci}")
            p.dma("gpsimd", csg[:], sg_r[:, ci, s], writes=["csg"],
                  tag=f"lsg{tc}_{ci}")
            ew(p, "vector", "tensor_mul",
               ygb[bi][:, ci * tch:(ci + 1) * tch], cya[:], csg[:],
               reads=["cya", "csg"], writes=[f"ygb{bi}", f"ygb{bi}_{ci}"],
               tag=f"yg{tc}_{ci}")
        for oc in range(8):
            wi = oc % 2
            p.dma("gpsimd", wot[wi][:],
                  wob_scr[:, oc * NIC * 128:(oc + 1) * NIC * 128],
                  reads=["wob_hbm"], writes=[f"wot{wi}"], tag=f"lwo{tc}_{oc}")
            p.dma("sync", x2c[wi][:],
                  x2_r[:, oc, s], writes=[f"x2c{wi}"], tag=f"lx2{tc}_{oc}")
            pb, pbk = (pC, "pC") if oc % 2 == 0 else (pD, "pD")
            mm_group(p, [(pb[:], wot[wi][:, k * 128:(k + 1) * 128],
                          ygb[bi][:, k * tch:(k + 1) * tch], k == 0, k == NIC - 1)
                         for k in range(NIC)],
                     reads=[f"wot{wi}", f"ygb{bi}"], writes=[pbk],
                     tag=f"op{tc}_{oc}")
            ew(p, "vector", "tensor_add", zf[:, oc * tch:(oc + 1) * tch],
               x2c[wi][:], pb[:],
               reads=[f"x2c{wi}", pbk], writes=["zf"], tag=f"z{tc}_{oc}")
        ln_fm(p, f"lno_{tc}", zf, "zf", x3f[bi], f"x3f{bi}", aux, EPS_LN_OUTER)
        for oc in range(8):
            act(p, x3b[bi][:, oc * tch:(oc + 1) * tch],
                x3f[bi][:, oc * tch:(oc + 1) * tch], AF.Copy,
                reads=[f"x3f{bi}"], writes=[f"x3b{bi}"], tag=f"cx{tc}_{oc}")
        trip = [(ps1[0:8, :] if False else pF[0:8, :],
                 wgb[:, k * 8:(k + 1) * 8],
                 x3b[bi][:, k * tch:(k + 1) * tch], k == 0, k == 7)
                for k in range(8)]
        mm_group(p, trip, reads=[f"x3b{bi}", "wgb"], writes=["pF"],
                 tag=f"gm{tc}")
        act(p, g_fm[:], pF[0:8, :], AF.Sigmoid, reads=["pF"], writes=["g_fm"],
            tag=f"gs{tc}")
        for tb in range(tch // 128):
            transpose(p, pF[0:128, 0:8], g_fm[:, tb * 128:(tb + 1) * 128],
                      identf[0:8, 0:8], reads=["g_fm", "identf"], writes=["pF"],
                      tag=f"gt{tc}_{tb}")
            ew(p, "vector", "tensor_copy", g_tm[:], pF[0:128, 0:8],
               reads=["pF"], writes=["g_tm"], tag=f"ge{tc}_{tb}")
            ew(p, "vector", "max", out=top8[:], in_=g_tm[:],
               reads=["g_tm"], writes=["top8"], tag=f"gx{tc}_{tb}")
            ew(p, "vector", "tensor_copy", rep[:, 0:2], top8[:, 0:2],
               reads=["top8"], writes=["rep"], tag=f"gr{tc}_{tb}")
            ew(p, "vector", "memset", rep[:, 2:8], -1e30,
               reads=[], writes=["rep2"], deps=[p.res["rep"]["w"]],
               tag=f"gm2{tc}_{tb}")
            ew(p, "vector", "match_replace", gz[:], rep[:], g_tm[:], 0.0,
               reads=["rep", "rep2", "g_tm"], writes=["gz"], tag=f"gmr{tc}_{tb}")
            ew(p, "vector", "tensor_sub", w_tm[:], g_tm[:], gz[:],
               reads=["g_tm", "gz"], writes=["w_tm"], tag=f"gw{tc}_{tb}")
            ew(p, "vector", "tensor_add", s2c[:], top8[:, 0:1], top8[:, 1:2],
               reads=["top8"], writes=["s2c"], tag=f"g2{tc}_{tb}")
            ew(p, "vector", "reciprocal", s2c[:], s2c[:],
               reads=["s2c"], writes=["s2c"], tag=f"g3{tc}_{tb}")
            ew(p, "vector", "tensor_scalar_mul", w_tm[:], w_tm[:], s2c[:, 0:1],
               reads=["w_tm", "s2c"], writes=["w_tm"], tag=f"g4{tc}_{tb}")
            transpose(p, pF[0:8, 0:128], w_tm[:], identf[:],
                      reads=["w_tm", "identf"], writes=["pF"], tag=f"g5{tc}_{tb}")
            ew(p, "vector", "tensor_copy", w_fmt[:, tb * 128:(tb + 1) * 128],
               pF[0:8, 0:128], reads=["pF"], writes=["w_fmt"], tag=f"g6{tc}_{tb}")
        for e in range(NE):
            if e < 8:
                mm_group(p, [(pB[:], esel[:, e * 128:(e + 1) * 128],
                              w_fmt[:], True, True)],
                         reads=["w_fmt", "esel"], writes=["pB"],
                         tag=f"wb{tc}_{e}")
                ew(p, "vector", "tensor_copy", wbs[:], pB[:], reads=["pB"],
                   writes=["wbs"], tag=f"wbe{tc}_{e}")
            for fcg in range(16):
                wi = fcg % 2
                tb0 = (e * 16 + fcg) * 1024
                p.dma("sync", w1t3[wi][:], we1b[:, tb0:tb0 + 1024],
                      reads=["we1_hbm"], writes=[f"w1t3{wi}"],
                      tag=f"l1{tc}_{e}_{fcg}")
                p.dma("gpsimd", w3t3[wi][:], we3b[:, tb0:tb0 + 1024],
                      reads=["we3_hbm"], writes=[f"w3t3{wi}"],
                      tag=f"l3{tc}_{e}_{fcg}")
                mm_group(p, [(pC[:], w1t3[wi][:, k * 128:(k + 1) * 128],
                              x3b[bi][:, k * tch:(k + 1) * tch], k == 0, k == 7)
                             for k in range(8)],
                         reads=[f"w1t3{wi}", f"x3b{bi}"], writes=["pC"],
                         tag=f"e1{tc}_{e}_{fcg}")
                mm_group(p, [(pD[:], w3t3[wi][:, k * 128:(k + 1) * 128],
                              x3b[bi][:, k * tch:(k + 1) * tch], k == 0, k == 7)
                             for k in range(8)],
                         reads=[f"w3t3{wi}", f"x3b{bi}"], writes=["pD"],
                         tag=f"e3{tc}_{e}_{fcg}")
                st_ = stmp2[fcg % 2]
                stk = f"stmp2{fcg % 2}"
                act(p, st_[:], pC[:], AF.Silu, reads=["pC"], writes=[stk],
                    tag=f"sl{tc}_{e}_{fcg}")
                if e < 8:
                    ew(p, "vector", "tensor_mul", st_[:], st_[:], pD[:],
                       reads=[stk, "pD"], writes=[stk], tag=f"h1{tc}_{e}_{fcg}")
                    ew(p, "vector", "tensor_mul",
                       hwb[:, fcg * tch:(fcg + 1) * tch], st_[:], wbs[:],
                       reads=[stk, "wbs"], writes=["hwb"],
                       tag=f"h2{tc}_{e}_{fcg}")
                else:
                    ew(p, "vector", "tensor_mul",
                       hwb[:, fcg * tch:(fcg + 1) * tch], st_[:], pD[:],
                       reads=[stk, "pD"], writes=["hwb"],
                       tag=f"h2{tc}_{e}_{fcg}")
            for oc in range(8):
                wi = oc % 2
                t20 = (e * 8 + oc) * 16 * 128
                p.dma("sync", w2t2[wi][:], we2b[:, t20:t20 + 16 * 128],
                      reads=["we2_hbm"], writes=[f"w2t2{wi}"],
                      tag=f"lw2{tc}_{e}_{oc}")
                pb, pbk = (pE, "pE") if oc % 2 == 0 else (pF, "pF")
                mm_group(p, [(pb[:], w2t2[wi][:, k * 128:(k + 1) * 128],
                              hwb[:, k * tch:(k + 1) * tch], k == 0, k == 15)
                             for k in range(16)],
                         reads=[f"w2t2{wi}", "hwb"], writes=[pbk],
                         tag=f"e2{tc}_{e}_{oc}")
                rsl = racc[:, oc * tch:(oc + 1) * tch]
                if e == 0:
                    ew(p, "vector", "tensor_copy", rsl, pb[:],
                       reads=[pbk], writes=["racc"], tag=f"r0{tc}_{e}_{oc}")
                else:
                    ew(p, "vector", "tensor_add", rsl, rsl, pb[:],
                       reads=[pbk, "racc"], writes=["racc"],
                       tag=f"ra{tc}_{e}_{oc}")
        for oc in range(8):
            ew(p, "vector", "scalar_tensor_tensor",
               zf[:, oc * tch:(oc + 1) * tch],
               racc[:, oc * tch:(oc + 1) * tch], 1.0 / 3.0,
               x3f[bi][:, oc * tch:(oc + 1) * tch], ALU.mult, ALU.add,
               reads=["racc", f"x3f{bi}"], writes=["zf"], tag=f"fz{tc}_{oc}")
        ln_fm(p, f"lnf_{tc}", zf, "zf", x3f[bi], f"x3f{bi}", aux, EPS_LN_OUTER)
        p.dma("sync", y_r[:, :, s], x3f[bi][:], reads=[f"x3f{bi}"],
              tag=f"sy{tc}")

    p.final_wait("sync")
    p.build()
    p.finish()
    return nc


# version 12
# speedup vs baseline: 1.1541x; 1.1541x over previous
import contextlib
import math
import numpy as np
import concourse.bass as bass
import concourse.mybir as mybir
from concourse.bass_utils import run_bass_kernel_spmd

F32 = mybir.dt.float32
F16 = mybir.dt.float16
BF16 = mybir.dt.bfloat16
AF = mybir.ActivationFunctionType
ALU = mybir.AluOpType
AX = mybir.AxisListType

ENGINES = ("sync", "tensor", "vector", "scalar", "gpsimd")

B = 2
N = 16384
D = 1024
NCORE = 8
SH = N // 4
H, DH = 16, 64
M = 64
DFF = 4096
EPS_KERN = 1e-6
EPS_LN = 1e-6
EPS_LN_OUTER = 1e-5
RATIO = 1.0 / math.sqrt(M)
DN = 1.0 / math.sqrt(math.sqrt(DH))
D_INNER = 2048
D_STATE = 16
D_CONV = 4
FF_DIM = 2048
N_ROUTED = 8
TOP_K = 2


class Prog:

    def __init__(self, nc: bass.Bass):
        self.nc = nc
        self.ctx = contextlib.ExitStack()
        self.sems = {e: self.ctx.enter_context(nc.semaphore(f"q_{e}"))
                     for e in ENGINES}
        self.counts = {e: 0 for e in ENGINES}
        self.schedule = {e: [] for e in ENGINES}
        self.last_wait = {e: {} for e in ENGINES}
        self.res = {}

    def sb(self, name, shape, dtype):
        self._nsb = getattr(self, "_nsb", 0) + 1
        return self.ctx.enter_context(
            self.nc.sbuf_tensor(f"{name}_s{self._nsb}", shape, dtype))

    def constcol(self, value):
        k = f"__const_{value!r}"
        if k not in self.res or self.res[k]["w"] is None:
            t = self.sb(f"c{len([x for x in self.res if x.startswith('__const')])}",
                        [128, 1], F32)
            self.emit("gpsimd", lambda e: e.memset(t[:], value), writes=[k],
                      tag=f"const{value}")
            self._constcache = getattr(self, "_constcache", {})
            self._constcache[k] = t
        return self._constcache[k][:], k

    def ps(self, name, shape, dtype=F32):
        self._nps = getattr(self, "_nps", 0) + 1
        return self.ctx.enter_context(
            self.nc.psum_tensor(f"{name}_p{self._nps}", shape, dtype))

    @contextlib.contextmanager
    def phase(self):
        old = self.ctx
        self.ctx = contextlib.ExitStack()
        try:
            yield
        finally:
            self.ctx.close()
            self.ctx = old
            for k in list(getattr(self, "_constcache", {})):
                del self._constcache[k]
                self.res.pop(k, None)

    def barrier(self):
        snap = {e: self.counts[e] for e in ENGINES if self.counts[e] > 0}
        for e in ENGINES:
            for (e2, v2) in snap.items():
                if v2 <= self.last_wait[e].get(e2, 0):
                    continue
                self.schedule[e].append(([(e2, v2)], lambda _: None, 0,
                                         "barrier"))
                self.last_wait[e][e2] = v2

    def emit(self, engine, fn, reads=(), writes=(), deps=(), inc=1, tag=""):
        alldeps = list(deps)
        for k in reads:
            r = self.res.get(k)
            if r and r["w"]:
                alldeps.append(r["w"])
        for k in writes:
            r = self.res.get(k)
            if r:
                if r["w"]:
                    alldeps.append(r["w"])
                alldeps.extend(r["r"])
        waits = []
        for d in alldeps:
            if d is None:
                continue
            e2, v2 = d
            assert v2 <= self.counts[e2], f"dep on future op {d} tag={tag}"
            if v2 <= self.last_wait[engine].get(e2, 0):
                continue
            waits.append((e2, v2))
            self.last_wait[engine][e2] = v2
        byeng = {}
        for (e2, v2) in waits:
            byeng[e2] = max(byeng.get(e2, 0), v2)
        waits = sorted(byeng.items())
        self.counts[engine] += inc
        tok = (engine, self.counts[engine])
        self.schedule[engine].append((waits, fn, inc, tag))
        for k in reads:
            self.res.setdefault(k, {"w": None, "r": []})["r"].append(tok)
        for k in writes:
            self.res[k] = {"w": tok, "r": []}
        return tok

    def dma(self, queue, out, in_, reads=(), writes=(), deps=(), tag="", **kw):
        return self.emit(queue, lambda e: e.dma_start(out=out, in_=in_, **kw),
                         reads=reads, writes=writes, deps=deps, inc=16, tag=tag)

    def final_wait(self, engine, sem_engine=None, value=None):
        se = sem_engine or engine
        val = value if value is not None else self.counts[se]
        self.schedule[engine].append(([(se, val)], lambda e: None, 0, "final"))

    def build(self):
        nc = self.nc
        sems = self.sems
        with nc.Block() as block:
            for eng_name in ENGINES:
                ops = self.schedule[eng_name]
                if not ops:
                    continue

                def make_body(ops=ops, eng_name=eng_name):
                    def body(eng):
                        for (waits, fn, inc, tag) in ops:
                            for (e2, v2) in waits:
                                eng.wait_ge(sems[e2], v2)
                            inst = fn(eng)
                            if inc and inst is not None:
                                inst.then_inc(sems[eng_name], inc)
                    return body

                getattr(block, eng_name)(make_body())

    def finish(self):
        self.ctx.close()


def ew(prog, engine, fname, *args, reads=(), writes=(), deps=(), tag="", **kw):
    def fn(e):
        return getattr(e, fname)(*args, **kw)
    return prog.emit(engine, fn, reads=reads, writes=writes, deps=deps, tag=tag)


def mm_group(prog, triples, reads=(), writes=(), deps=(), tag="", **mmkw):
    def fn(e):
        inst = None
        for (out, lhsT, rhs, start, stop) in triples:
            inst = e.matmul(out, lhsT, rhs, start=start, stop=stop, **mmkw)
        return inst
    return prog.emit("tensor", fn, reads=reads, writes=writes, deps=deps, tag=tag)


def transpose(prog, out_psum, in_sbuf, ident, reads=(), writes=(), deps=(), tag=""):
    return prog.emit("tensor", lambda e: e.transpose(out_psum, in_sbuf, ident),
                     reads=reads, writes=writes, deps=deps, tag=tag)


def act(prog, out, in_, func, bias=0.0, scale=1.0, reads=(), writes=(), deps=(),
        engine="scalar", tag=""):
    return prog.emit(engine,
                     lambda e: e.activation(out, in_, func, bias=bias, scale=scale),
                     reads=reads, writes=writes, deps=deps, tag=tag)


def make_identity(prog, ident_ap, key, deps=()):
    prog.emit("gpsimd", lambda e: e.memset(ident_ap, 0.0), deps=deps,
              writes=[key], tag="id0")
    n = ident_ap.shape[0]
    return prog.emit("gpsimd", lambda e: e.affine_select(
        out=ident_ap, in_=ident_ap, compare_op=ALU.not_equal, fill=1.0,
        base=0, pattern=[[-1, n]], channel_multiplier=1),
        reads=[key], writes=[key], tag="id1")


def make_range_sel(prog, ap3, key, cm, step_pats, lo, width, deps=(), fill=1.0):
    prog.emit("gpsimd", lambda e: e.memset(ap3, fill), deps=deps,
              writes=[key], tag="rs0")
    prog.emit("gpsimd", lambda e: e.affine_select(
        out=ap3, in_=ap3, compare_op=ALU.is_ge, fill=0.0, base=-lo,
        pattern=step_pats, channel_multiplier=cm),
        reads=[key], writes=[key], tag="rs1")
    neg = [[-s, c] for (s, c) in step_pats]
    return prog.emit("gpsimd", lambda e: e.affine_select(
        out=ap3, in_=ap3, compare_op=ALU.is_ge, fill=0.0, base=lo + width - 1,
        pattern=neg, channel_multiplier=-cm),
        reads=[key], writes=[key], tag="rs2")



def build_l1(sh=SH, tch=512, env=None):
    nch = sh // tch
    if env is None:
        nc = bass.Bass("TRN2", target_bir_lowering=False)
        x = nc.dram_tensor("x", [D, sh], F32, kind="ExternalInput").ap()
        wq = nc.dram_tensor("wq", [D, D], F32, kind="ExternalInput").ap()
        wk = nc.dram_tensor("wk", [D, D], F32, kind="ExternalInput").ap()
        wv = nc.dram_tensor("wv", [D, D], F32, kind="ExternalInput").ap()
        projT = nc.dram_tensor("projT", [DH, M], F32, kind="ExternalInput").ap()
        o_qf = nc.dram_tensor("qf", [D, sh], BF16, kind="ExternalOutput").ap()
        o_ddk = nc.dram_tensor("ddk", [D, sh], F32, kind="ExternalOutput").ap()
        o_v = nc.dram_tensor("vv", [D, sh], BF16, kind="ExternalOutput").ap()
        o_mx = nc.dram_tensor("mxloc", [1, 1], F32, kind="ExternalOutput").ap()

        p = Prog(nc)
    else:
        nc, p = env["nc"], env["p"]
        x = env["x"]
        wq = env["wq"]
        wk = env["wk"]
        wv = env["wv"]
        projT = env["projT"]
        o_qf = env["o_qf"]
        o_ddk = env["o_ddk"]
        o_v = env["o_v"]
        o_mx = env["o_mx"]
    LNR = math.log(RATIO)

    wtmp = p.sb("wtmp", [128, 2 * D], F32)
    wqb = p.sb("wqb", [128, 8 * D], BF16)
    wkb = p.sb("wkb", [128, 8 * D], BF16)
    wvb = p.sb("wvb", [128, 8 * D], BF16)
    pjb = p.sb("pjb", [128, M], BF16)
    pjf = p.sb("pjf", [128, M], F32)
    sel16 = p.sb("sel16", [16, 8 * 128], F32)
    hsel = p.sb("hsel", [128, 8 * 16], F32)
    identf = p.sb("identf", [128, 128], F32)
    runmax = p.sb("runmax", [128, 1], F32)

    xf = [p.sb(f"xf{i}", [128, 8 * tch], F32) for i in range(2)]
    xb = [p.sb(f"xb{i}", [128, 8 * tch], BF16) for i in range(2)]
    qb = p.sb("qb", [128, 8 * tch], BF16)
    kb = p.sb("kb", [128, 8 * tch], BF16)
    ddq_sb = p.sb("ddq_sb", [128, 8 * tch], F32)
    ddk_sb = p.sb("ddk_sb", [128, 8 * tch], F32)
    zsq = [p.sb(f"zsq{i}", [128, tch], F32) for i in range(2)]
    mxtm = p.sb("mxtm", [128, 16], F32)
    mx_fm = p.sb("mx_fm", [16, tch], F32)
    sq_sb = p.sb("sq_sb", [16, tch], F32)
    sk_sb = p.sb("sk_sb", [16, tch], F32)
    qf_sb = p.sb("qf_sb", [128, 8 * tch], BF16)
    vb_sb = p.sb("vb_sb", [128, 8 * tch], BF16)
    ddko = p.sb("ddko", [128, 8 * tch], F32)
    dtmp = [p.sb(f"dtmp{i}", [128, tch], F32) for i in range(2)]
    mx1 = p.sb("mx1", [128, 1], F32)
    mx2 = p.sb("mx2", [1, 1], F32)

    pp = [p.ps(f"pp{i}", [128, tch], F32) for i in range(3)]
    pdiag = [p.ps("pdiag_q", [16, tch], F32), p.ps("pdiag_k", [16, tch], F32)]
    ptr = [pp[0][:, 0:128], pp[1][:, 0:128]]
    pbc = [pp[0], pp[1]]
    pmx = pp[2][0:16, 0:128]

    make_identity(p, identf[:], "identf")
    make_range_sel(p, sel16[:].rearrange("h (m p) -> h m p", p=128), "sel16",
                   -64, [[128, 8], [1, 128]], 0, 64)
    make_range_sel(p, hsel[:].rearrange("p (m j) -> p m j", j=16), "hsel",
                   1, [[128, 8], [-64, 16]], 0, 64, fill=0.5)
    ew(p, "vector", "memset", runmax[:], -3e38, writes=["runmax"], tag="rmi")

    for wnm, src, dst, scale in [("q", wq, wqb, DN), ("k", wk, wkb, DN),
                                 ("v", wv, wvb, 1.0)]:
        for g in range(4):
            p.dma("sync", wtmp[:].rearrange("p (k m) -> p k m", m=D),
                  src.rearrange("(k p) m -> p k m", p=128)[:, 2 * g:2 * g + 2, :],
                  writes=["wtmp"], tag=f"w_{wnm}{g}")
            act(p, dst[:, g * 2 * D:(g + 1) * 2 * D], wtmp[:], AF.Copy,
                scale=scale, reads=["wtmp"], writes=[f"w{wnm}b"],
                tag=f"wc_{wnm}{g}")
    p.dma("sync", pjf[0:64, :], projT, writes=["pjf"], tag="pj0")
    p.dma("sync", pjf[64:128, :], projT, reads=["pjf"], deps=(), tag="pj1",
          writes=["pjf2"])
    act(p, pjb[:], pjf[:], AF.Copy, reads=["pjf", "pjf2"], writes=["pjb"],
        tag="pjc")

    x_r = x.rearrange("(k p) t -> p k t", p=128)
    qf_r = o_qf.rearrange("(k p) t -> p k t", p=128)
    v_r = o_v.rearrange("(k p) t -> p k t", p=128)
    ddk_r = o_ddk.rearrange("(k p) t -> p k t", p=128)

    for c in range(nch):
        bi = c % 2
        s = slice(c * tch, (c + 1) * tch)
        p.dma("sync", xf[bi][:], x_r[:, :, s], writes=[f"xf{bi}"], tag=f"ldx{c}")
        act(p, xb[bi][:], xf[bi][:], AF.Copy, reads=[f"xf{bi}", "wvb"],
            writes=[f"xb{bi}"], tag=f"xc{c}")

        for mc in range(8):
            for wi, (wkey, wb, dkey, dst_bf) in enumerate([
                    ("wqb", wqb, "qb", qb), ("wkb", wkb, "kb", kb),
                    ("wvb", wvb, "vb", vb_sb)]):
                bank = pp[wi]
                mm_group(p, [
                    (bank[:], wb[:, k * D + mc * 128:k * D + mc * 128 + 128],
                     xb[bi][:, k * tch:(k + 1) * tch], k == 0, k == 7)
                    for k in range(8)],
                    reads=[wkey, f"xb{bi}"], writes=[f"pp{wi}"],
                    tag=f"prj{c}_{mc}_{wi}")
                act(p, dst_bf[:, mc * tch:(mc + 1) * tch], bank[:], AF.Copy,
                    reads=[f"pp{wi}"], writes=[dkey, f"{dkey}_{mc}"],
                    tag=f"cast{c}_{mc}_{wi}")
                if wi < 2:
                    act(p, zsq[wi][:], bank[:], AF.Square,
                        reads=[f"pp{wi}"], writes=[f"zsq{wi}"],
                        tag=f"sq{c}_{mc}_{wi}")
                    mm_group(p, [(pdiag[wi][:], hsel[:, mc * 16:mc * 16 + 16],
                                  zsq[wi][:], mc == 0, mc == 7)],
                             reads=[f"zsq{wi}", "hsel"], writes=[f"pdiag{wi}"],
                             tag=f"dg{c}_{mc}_{wi}")

        p.dma("sync", v_r[:, :, s], vb_sb[:], reads=["vb"], tag=f"stv{c}")

        for mc in range(8):
            for wi, (skey, src_bf, ddkey, dd_dst) in enumerate([
                    ("qb", qb, "ddq", ddq_sb), ("kb", kb, "ddk", ddk_sb)]):
                bank = pp[wi]
                mm_group(p, [
                    (bank[0:64, :], pjb[0:64, :],
                     src_bf[0:64, mc * tch:(mc + 1) * tch], True, True),
                    (bank[64:128, :], pjb[64:128, :],
                     src_bf[64:128, mc * tch:(mc + 1) * tch], True, True)],
                    reads=[f"{skey}_{mc}", "pjb"], writes=[f"pp{wi}"],
                    tag=f"dd{c}_{mc}_{wi}")
                ew(p, "vector", "tensor_copy", dd_dst[:, mc * tch:(mc + 1) * tch],
                   bank[:], reads=[f"pp{wi}"], writes=[f"{ddkey}_{mc}", ddkey],
                   tag=f"ddev{c}_{mc}_{wi}")
                if wi == 1:
                    ew(p, "vector", "reduce_max", mx1[:], bank[:], axis=AX.X,
                       reads=[f"pp{wi}"], writes=["mx1"], tag=f"rmx{c}_{mc}")
                    ew(p, "vector", "tensor_max", runmax[:], runmax[:], mx1[:],
                       reads=["mx1", "runmax"], writes=["runmax"],
                       tag=f"rma{c}_{mc}")

        for tb in range(tch // 128):
            for mc in range(8):
                pb = ptr[mc % 2]
                transpose(p, pb,
                          ddq_sb[:, mc * tch + tb * 128:mc * tch + tb * 128 + 128],
                          identf[:], reads=[f"ddq_{mc}", "identf"],
                          writes=[f"pp{mc % 2}"], tag=f"trq{c}_{tb}_{mc}")
                ew(p, "vector", "reduce_max", mxtm[:, mc * 2:mc * 2 + 2],
                   pb.rearrange("t (h m) -> t h m", m=64), axis=AX.X,
                   reads=[f"pp{mc % 2}"], writes=["mxtm"],
                   tag=f"redq{c}_{tb}_{mc}")
            transpose(p, pmx, mxtm[:], identf[:],
                      reads=["mxtm", "identf"], writes=["pp2"],
                      tag=f"trmx{c}_{tb}")
            ew(p, "vector", "tensor_copy", mx_fm[:, tb * 128:(tb + 1) * 128],
               pmx, reads=["pp2"], writes=["mx_fm"], tag=f"mxev{c}_{tb}")

        ew(p, "vector", "tensor_add", sq_sb[:], pdiag[0][:], mx_fm[:],
           reads=["pdiag0", "mx_fm"], writes=["sq_sb"], tag=f"sqc{c}")
        ew(p, "vector", "tensor_copy", sk_sb[:], pdiag[1][:],
           reads=["pdiag1"], writes=["sk_sb"], tag=f"skc{c}")

        for mc in range(8):
            pb = pbc[mc % 2]
            pkey = f"pp{mc % 2}"
            mm_group(p, [(pb[:], sel16[:, mc * 128:(mc + 1) * 128], sq_sb[:],
                          True, True)],
                     reads=["sq_sb", "sel16"], writes=[pkey], tag=f"bcq{c}_{mc}")
            ew(p, "vector", "tensor_sub", dtmp[mc % 2][:],
               ddq_sb[:, mc * tch:(mc + 1) * tch], pb[:],
               reads=[f"ddq_{mc}", pkey], writes=[f"dtmp{mc % 2}"],
               tag=f"qfs{c}_{mc}")
            lnr_ap, lnr_key = p.constcol(LNR)
            act(p, qf_sb[:, mc * tch:(mc + 1) * tch], dtmp[mc % 2][:], AF.Exp,
                bias=lnr_ap, reads=[f"dtmp{mc % 2}", lnr_key], writes=["qf_sb"],
                tag=f"qfe{c}_{mc}")
            mm_group(p, [(pb[:], sel16[:, mc * 128:(mc + 1) * 128], sk_sb[:],
                          True, True)],
                     reads=["sk_sb", "sel16"], writes=[pkey], tag=f"bck{c}_{mc}")
            ew(p, "vector", "tensor_sub", ddko[:, mc * tch:(mc + 1) * tch],
               ddk_sb[:, mc * tch:(mc + 1) * tch], pb[:],
               reads=[f"ddk_{mc}", pkey], writes=["ddko"], tag=f"dks{c}_{mc}")

        p.dma("sync", qf_r[:, :, s], qf_sb[:], reads=["qf_sb"], tag=f"stqf{c}")
        p.dma("sync", ddk_r[:, :, s], ddko[:], reads=["ddko"], tag=f"stddk{c}")

    transpose(p, pmx[0:1, :], runmax[:], identf[:],
              reads=["runmax", "identf"], writes=["pp2"], tag="mxT")
    ew(p, "vector", "reduce_max", mx2[:], pmx[0:1, :], axis=AX.X,
       reads=["pp2"], writes=["mx2"], tag="mxR")
    p.dma("sync", o_mx, mx2[:], reads=["mx2"], tag="stmx")

    if env is None:
        p.final_wait("sync")
        p.build()
        p.finish()
        return nc



def build_l2(sh=SH, tch=512, debug=False, parts=7, env=None):
    nch = sh // tch
    if env is None:
        nc = bass.Bass("TRN2", target_bir_lowering=False)
        ddk = nc.dram_tensor("ddk", [D, sh], F32, kind="ExternalInput").ap()
        vv = nc.dram_tensor("vv", [D, sh], BF16, kind="ExternalInput").ap()
        mxall = nc.dram_tensor("mxall", [8, 1], F32, kind="ExternalInput").ap()
        o_kv = nc.dram_tensor("kvp", [128, 512], F32, kind="ExternalOutput").ap()
        o_kfs = nc.dram_tensor("kfsp", [128, 8], F32, kind="ExternalOutput").ap()
        o_vs = nc.dram_tensor("vsp", [1, 1024], F32, kind="ExternalOutput").ap()

        p = Prog(nc)
    else:
        nc, p = env["nc"], env["p"]
        ddk = env["ddk"]
        vv = env["vv"]
        mxall = env["mxall"]
        o_kv = env["o_kv"]
        o_kfs = env["o_kfs"]
        o_vs = env["o_vs"]
    LNR = math.log(RATIO)

    identb = p.sb("identb", [128, 128], BF16)
    mx8 = p.sb("mx8", [1, 8], F32)
    bias1 = p.sb("bias1", [1, 1], F32)
    bias128 = p.sb("bias128", [128, 1], F32)
    ones1f = p.sb("ones1f", [1, 128], F32)
    onesb = p.sb("onesb", [128, 1], BF16)
    ddf = [p.sb(f"ddf{i}", [128, 8 * tch], F32) for i in range(2)]
    vbf = [p.sb(f"vbf{i}", [128, 8 * tch], BF16) for i in range(2)]
    kfb = [p.sb(f"kfb{i}", [128, 8 * tch], BF16) for i in range(2)]
    ktm = p.sb("ktm", [128, 128], BF16)
    vtm = p.sb("vtm", [128, 128], BF16)
    kvev = p.sb("kvev", [128, 512], F32)
    kfsev = p.sb("kfsev", [128, 8], F32)
    vsev = p.sb("vsev", [1, 1024], F32)

    pkv = p.ps("pkv", [128, 512], F32)
    pkfs = p.ps("pkfs", [128, 8], F32)
    pvs = [p.ps(f"pvs{i}", [1, 512], F32) for i in range(2)]
    ptk = p.ps("ptk", [128, 128], BF16)
    ptv = p.ps("ptv", [128, 128], BF16)
    pb1 = p.ps("pb1", [128, 1], F32)

    make_identity(p, identb[:], "identb")
    ew(p, "gpsimd", "memset", ones1f[:], 1.0, writes=["ones1f"], tag="o1")
    ew(p, "gpsimd", "memset", onesb[:], 1.0, writes=["onesb"], tag="o2")

    p.dma("sync", mx8[:], mxall.rearrange("a b -> b a"), writes=["mx8"], tag="ldmx")
    ew(p, "vector", "reduce_max", bias1[:], mx8[:], axis=AX.X,
       reads=["mx8"], writes=["bias1"], tag="redmx")
    ew(p, "vector", "tensor_scalar", bias1[:], bias1[:], -1.0, LNR,
       ALU.mult, ALU.add, reads=["bias1"], writes=["bias1"], tag="biasc")
    mm_group(p, [(pb1[:], ones1f[:], bias1[:], True, True)],
             reads=["ones1f", "bias1"], writes=["pb1"], tag="biasbc")
    ew(p, "vector", "tensor_copy", bias128[:], pb1[:], reads=["pb1"],
       writes=["bias128"], tag="biasev")

    ddk_r = ddk.rearrange("(k p) t -> p k t", p=128)
    vv_r = vv.rearrange("(k p) t -> p k t", p=128)
    if debug:
        o_kf = nc.dram_tensor("kfdbg", [128, 8 * tch], BF16, kind="ExternalOutput").ap()
        o_ktm = nc.dram_tensor("ktmdbg", [128, 128], BF16, kind="ExternalOutput").ap()

    nmm = 0
    for c in range(nch):
        bi = c % 2
        s = slice(c * tch, (c + 1) * tch)
        p.dma("sync", ddf[bi][:], ddk_r[:, :, s], writes=[f"ddf{bi}"], tag=f"lddd{c}")
        p.dma("sync", vbf[bi][:], vv_r[:, :, s], writes=[f"vbf{bi}"], tag=f"ldv{c}")
        act(p, kfb[bi][:], ddf[bi][:], AF.Exp, bias=bias128[:, 0:1],
            reads=[f"ddf{bi}", "bias128"], writes=[f"kfb{bi}"], tag=f"kfe{c}")
        if debug and c == 0:
            p.dma("sync", o_kf, kfb[bi][:], reads=[f"kfb{bi}"], tag="dbgkf")

        for mc in range(8):
            for tb in range(tch // 128):
                col = mc * tch + tb * 128
                transpose(p, ptk[:], kfb[bi][:, col:col + 128], identb[:],
                          reads=[f"kfb{bi}", "identb"], writes=["ptk"],
                          tag=f"trk{c}_{mc}_{tb}")
                transpose(p, ptv[:], vbf[bi][:, col:col + 128], identb[:],
                          reads=[f"vbf{bi}", "identb"], writes=["ptv"],
                          tag=f"trv{c}_{mc}_{tb}")
                ew(p, "vector", "tensor_copy", ktm[:], ptk[:], reads=["ptk"],
                   writes=["ktm"], tag=f"ektm{c}_{mc}_{tb}")
                if debug and c == 0 and mc == 0 and tb == 0:
                    p.dma("sync", o_ktm, ktm[:], reads=["ktm"], tag="dbgktm")
                ew(p, "vector", "tensor_copy", vtm[:], ptv[:], reads=["ptv"],
                   writes=["vtm"], tag=f"evtm{c}_{mc}_{tb}")
                first = (c == 0 and tb == 0 and mc == 0)
                trip = []
                for par in range(2):
                    if parts & 1:
                        trip.append((pkv[par * 64:par * 64 + 64, mc * 64:mc * 64 + 64],
                                     ktm[:, par * 64:par * 64 + 64],
                                     vtm[:, par * 64:par * 64 + 64], first, False))
                    if parts & 2:
                        trip.append((pkfs[par * 64:par * 64 + 64, mc:mc + 1],
                                     ktm[:, par * 64:par * 64 + 64],
                                     onesb[:], first, False))
                vsb = (mc * 128 // 512) % 2
                if parts & 4:
                    trip.append((pvs[vsb][:, (mc * 128) % 512:(mc * 128) % 512 + 128],
                                 onesb[:], vtm[:],
                                 (c == 0 and tb == 0 and mc % 4 == 0), False))
                mm_group(p, trip, reads=["ktm", "vtm", "onesb"],
                         writes=["pkv_acc"], tag=f"kv{c}_{mc}_{tb}",
                         skip_group_check=True)
                nmm += 1

    ew(p, "vector", "tensor_copy", kvev[:], pkv[:], reads=["pkv_acc"],
       writes=["kvev"], tag="evkv")
    ew(p, "vector", "tensor_copy", kfsev[:], pkfs[:], reads=["pkv_acc"],
       writes=["kfsev"], tag="evkfs")
    ew(p, "vector", "tensor_copy", vsev[:, 0:512], pvs[0][:], reads=["pkv_acc"],
       writes=["vsev"], tag="evvs0")
    ew(p, "vector", "tensor_copy", vsev[:, 512:1024], pvs[1][:], reads=["pkv_acc"],
       writes=["vsev2"], tag="evvs1")
    p.dma("sync", o_kv, kvev[:], reads=["kvev"], tag="stkv")
    p.dma("sync", o_kfs, kfsev[:], reads=["kfsev"], tag="stkfs")
    p.dma("sync", o_vs, vsev[:], reads=["vsev", "vsev2"], tag="stvs")

    if env is None:
        p.final_wait("sync")
        p.build()
        p.finish()
        return nc


def ln_fm(p, pref, z, zkey, dst, dstkey, aux, eps, nsl=8, tch=512,
          gamma=None, beta=None, gkey=None, dst_bf=None, dstbfkey=None,
          wid=None):
    K = aux["keys"]
    nfeat = float(nsl * 128)
    w = tch if wid is None else wid
    for oc in range(nsl):
        sl = slice(oc * tch, oc * tch + w)
        act(p, aux["zsq"][:, 0:w], z[:, sl], AF.Square,
            reads=[zkey], writes=[K["zsq"]], tag=f"{pref}_sq{oc}")
        mm_group(p, [(aux["ps1"][:, 0:w], aux["ones128"][:], z[:, sl],
                      oc == 0, oc == nsl - 1)],
                 reads=[zkey, K["ones128"]], writes=[K["ps1"]],
                 tag=f"{pref}_s1_{oc}", skip_group_check=True)
        mm_group(p, [(aux["ps2"][:, 0:w], aux["ones128"][:], aux["zsq"][:, 0:w],
                      oc == 0, oc == nsl - 1)],
                 reads=[K["zsq"], K["ones128"]], writes=[K["ps2"]],
                 tag=f"{pref}_s2_{oc}", skip_group_check=True)
    ew(p, "vector", "tensor_scalar_mul", aux["mean"][:, 0:w], aux["ps1"][:, 0:w],
       1.0 / nfeat, reads=[K["ps1"]], writes=[K["mean"]], tag=f"{pref}_mean")
    ew(p, "vector", "tensor_mul", aux["msq"][:, 0:w], aux["mean"][:, 0:w],
       aux["mean"][:, 0:w],
       reads=[K["mean"]], writes=[K["msq"]], tag=f"{pref}_msq")
    ew(p, "vector", "scalar_tensor_tensor", aux["var"][:, 0:w], aux["ps2"][:, 0:w],
       1.0 / nfeat, aux["msq"][:, 0:w], ALU.mult, ALU.subtract,
       reads=[K["ps2"], K["msq"]], writes=[K["var"]], tag=f"{pref}_var")
    act(p, aux["lnv"][:, 0:w], aux["var"][:, 0:w], AF.Ln, bias=aux["epscol"][0:1, :],
        reads=[K["var"], K["epscol"]], writes=[K["lnv"]], tag=f"{pref}_ln")
    act(p, aux["rstd"][:, 0:w], aux["lnv"][:, 0:w], AF.Exp, scale=-0.5,
        reads=[K["lnv"]], writes=[K["rstd"]], tag=f"{pref}_rstd")
    mm_group(p, [(aux["pbm"][:, 0:w], aux["ones1r"][:], aux["mean"][:, 0:w],
                  True, True)],
             reads=[K["mean"], K["ones1r"]], writes=[K["pbm"]],
             tag=f"{pref}_bcm")
    mm_group(p, [(aux["pba"][:, 0:w], aux["ones1r"][:], aux["rstd"][:, 0:w],
                  True, True)],
             reads=[K["rstd"], K["ones1r"]], writes=[K["pba"]],
             tag=f"{pref}_bca")
    for oc in range(nsl):
        sl = slice(oc * tch, oc * tch + w)
        ew(p, "vector", "tensor_sub", aux["utile"][:, 0:w], z[:, sl],
           aux["pbm"][:, 0:w],
           reads=[zkey, K["pbm"]], writes=[K["utile"]], tag=f"{pref}_u{oc}")
        if gamma is None:
            ew(p, "vector", "tensor_mul", dst[:, sl], aux["utile"][:, 0:w],
               aux["pba"][:, 0:w], reads=[K["utile"], K["pba"]], writes=[dstkey],
               tag=f"{pref}_w{oc}")
        else:
            ew(p, "vector", "tensor_mul", aux["utile"][:, 0:w],
               aux["utile"][:, 0:w],
               aux["pba"][:, 0:w], reads=[K["utile"], K["pba"]], writes=[K["utile"]],
               tag=f"{pref}_w{oc}")
            ew(p, "vector", "tensor_scalar", dst[:, sl], aux["utile"][:, 0:w],
               gamma[:, oc:oc + 1], beta[:, oc:oc + 1], ALU.mult, ALU.add,
               reads=[K["utile"], gkey], writes=[dstkey], tag=f"{pref}_g{oc}")
        if dst_bf is not None:
            act(p, dst_bf[:, sl], dst[:, sl], AF.Copy,
                reads=[dstkey], writes=[dstbfkey], tag=f"{pref}_bf{oc}")


def make_ln_aux(p, pref, tch=512):
    aux = dict(
        ones128=p.sb(f"{pref}_o128", [128, 1], F32),
        ones1r=p.sb(f"{pref}_o1r", [1, 128], F32),
        zsq=p.sb(f"{pref}_zsq", [128, tch], F32),
        mean=p.sb(f"{pref}_mean", [1, tch], F32),
        msq=p.sb(f"{pref}_msq", [1, tch], F32),
        var=p.sb(f"{pref}_var", [1, tch], F32),
        lnv=p.sb(f"{pref}_lnv", [1, tch], F32),
        rstd=p.sb(f"{pref}_rstd", [1, tch], F32),
    )
    keys = {k: f"{pref}_{k}" for k in
            ("ones128", "ones1r", "zsq", "mean", "msq", "var", "lnv", "rstd",
             "ps1", "ps2", "pbm", "pba", "utile", "epscol")}
    aux["keys"] = keys
    ew(p, "gpsimd", "memset", aux["ones128"][:], 1.0, writes=[keys["ones128"]],
       tag=f"{pref}i1")
    ew(p, "gpsimd", "memset", aux["ones1r"][:], 1.0, writes=[keys["ones1r"]],
       tag=f"{pref}i2")
    return aux



def build_l3(sh=SH, tch=512, ttok=N, env=None):
    nch = sh // tch
    if env is None:
        nc = bass.Bass("TRN2", target_bir_lowering=False)
        x = nc.dram_tensor("x", [D, sh], F32, kind="ExternalInput").ap()
        qf = nc.dram_tensor("qf", [D, sh], BF16, kind="ExternalInput").ap()
        kvp4 = nc.dram_tensor("kvp4", [4, 128, 512], F32, kind="ExternalInput").ap()
        kfsp4 = nc.dram_tensor("kfsp4", [4, 128, 8], F32, kind="ExternalInput").ap()
        vsp4 = nc.dram_tensor("vsp4", [4, 1, 1024], F32, kind="ExternalInput").ap()
        wo = nc.dram_tensor("wo", [D, D], F32, kind="ExternalInput").ap()
        w1 = nc.dram_tensor("w1", [D, DFF], F32, kind="ExternalInput").ap()
        w2 = nc.dram_tensor("w2", [DFF, D], F32, kind="ExternalInput").ap()
        b1c = nc.dram_tensor("b1c", [128, 32], F32, kind="ExternalInput").ap()
        b2c = nc.dram_tensor("b2c", [128, 8], F32, kind="ExternalInput").ap()
        lng = {}
        for nm in ("ln1g", "ln1b", "ln2g", "ln2b"):
            lng[nm] = nc.dram_tensor(nm, [128, 8], F32, kind="ExternalInput").ap()
        o_x1 = nc.dram_tensor("x1", [D, sh], F32, kind="ExternalOutput").ap()

        p = Prog(nc)
    else:
        nc, p = env["nc"], env["p"]
        x = env["x"]
        qf = env["qf"]
        kvp4 = env["kvp4"]
        kfsp4 = env["kfsp4"]
        vsp4 = env["vsp4"]
        wo = env["wo"]
        w1 = env["w1"]
        w2 = env["w2"]
        b1c = env["b1c"]
        b2c = env["b2c"]
        lng = env["lng"]
        o_x1 = env["o_x1"]
        mxg4 = env.get("mxg4")
        mx8 = env.get("mx8")
        x_h = env.get("x_h")
        qf_h = env.get("qf_h")
        mask1 = env.get("mask1")
        o_x1h = env.get("o_x1h")
    if env is None:
        mxg4 = mx8 = x_h = qf_h = mask1 = o_x1h = None
    w1b = nc.dram_tensor("w1b_scr", [128, 8 * DFF], BF16, kind="Internal").ap()
    w2b = nc.dram_tensor("w2b_scr", [128, 32 * D], BF16, kind="Internal").ap()
    CEPS = RATIO * EPS_KERN

    wob = p.sb("wob", [128, 8 * D], BF16)
    sel16 = p.sb("sel16", [16, 8 * 128], F32)
    kvs = p.sb("kvs", [128, 4 * 512], F32)
    kfss = p.sb("kfss", [128, 4 * 8], F32)
    vss = p.sb("vss", [1, 4 * 1024], F32)
    kvb = p.sb("kvb", [128, 512], BF16)
    denv = p.sb("denv", [128, 8], F32)
    zall = p.sb("zall", [128, 8 * 16], BF16)
    zallf = p.sb("zallf", [128, 8 * 16], F32)
    denc = p.sb("denc", [16, 1], F32)
    cepsc = p.sb("cepsc", [128, 1], BF16)
    rden = p.sb("rden", [16, tch], F32)
    denf = p.sb("denf", [16, tch], F32)
    xc = [p.sb("xc0", [128, 8 * tch], F32)] * 2
    qfc = [p.sb("qfc0", [128, 8 * tch], BF16)] * 2
    qsb = p.sb("qsb", [128, 8 * tch], BF16)
    attnb = p.sb("attnb", [128, 8 * tch], BF16)
    z1 = p.sb("z1", [128, 8 * tch], F32)
    out1f = p.sb("out1f", [128, 8 * tch], F32)
    out1b = p.sb("out1b", [128, 8 * tch], BF16)
    hb = p.sb("hb", [128, 32 * tch], BF16)
    xbtmp = p.sb("xbtmp", [128, 2 * tch], F32)
    etmp = p.sb("etmp", [128, 2 * tch], F32)
    htmp = p.sb("htmp", [128, tch], F32)
    w1t = [p.sb(f"w1t{i}", [128, 8 * 128], BF16) for i in range(2)]
    w2t = [p.sb("w2t0", [128, 32 * 128], BF16)] * 2
    aux = make_ln_aux(p, "ln", tch)
    wtmp2 = [z1[:][:, 0:DFF] if False else z1[:], out1f[:]]
    wtmpb2_ = [hb[:, 0:DFF], hb[:, DFF:2 * DFF]]
    wtmp = wtmp2[0]
    wtmpb = wtmpb2_[0]

    pden = p.ps("pden", [16, tch], F32)
    pA = p.ps("pA", [128, tch], F32)
    pB = p.ps("pB", [128, tch], F32)
    pC = p.ps("pC", [128, tch], F32)
    pD = p.ps("pD", [128, tch], F32)
    ps1 = p.ps("ps1", [1, tch], F32)
    ps2 = p.ps("ps2", [1, tch], F32)
    aux.update(ps1=ps1, ps2=ps2, pbm=pA, pba=pB, utile=htmp)
    aux["keys"].update(ps1="ps1", ps2="ps2", pbm="pA", pba="pB", utile="htmp",
                       epscol="epsc")

    make_range_sel(p, sel16[:].rearrange("h (m p) -> h m p", p=128), "sel16",
                   -64, [[128, 8], [1, 128]], 0, 64)
    ec1, eck1 = p.constcol(EPS_LN)
    ec2, eck2 = p.constcol(EPS_LN_OUTER)

    p.dma("sync", kvs[:].rearrange("p (g t) -> p g t", g=4),
          kvp4.rearrange("g p t -> p g t"), writes=["kvs"], tag="ldkv")
    p.dma("sync", kfss[:].rearrange("p (g t) -> p g t", g=4),
          kfsp4.rearrange("g p t -> p g t"), writes=["kfss"], tag="ldkfs")
    p.dma("sync", vss[:].rearrange("p (g t) -> p g t", g=4),
          vsp4.rearrange("g p t -> p g t"), writes=["vss"], tag="ldvs")
    if mxg4 is not None:
        mx8s = p.sb("mx8s", [1, 8], F32)
        mxg4s = p.sb("mxg4s", [1, 4], F32)
        nmx = p.sb("nmx", [1, 1], F32)
        arow = p.sb("arow", [1, 4], F32)
        acol = p.sb("acol", [128, 4], F32)
        p.dma("sync", mx8s[:], mx8, writes=["mx8s"], tag="lmx8")
        p.dma("sync", mxg4s[:], mxg4, writes=["mxg4s"], tag="lmxg4")
        ew(p, "vector", "reduce_max", nmx[:], mx8s[:], axis=AX.X,
           reads=["mx8s"], writes=["nmx"], tag="gmax")
        ew(p, "vector", "tensor_scalar_mul", nmx[:], nmx[:], -1.0,
           reads=["nmx"], writes=["nmx"], tag="negmax")
        act(p, arow[:], mxg4s[:], AF.Exp, bias=nmx[0:1, 0:1],
            reads=["mxg4s", "nmx"], writes=["arow"], tag="alpha")
        mm_group(p, [(pC[0:128, 0:4], aux["ones1r"][:], arow[:], True, True)],
                 reads=["arow", "ln_ones1r"], writes=["pC"], tag="abc")
        ew(p, "vector", "tensor_copy", acol[:], pC[0:128, 0:4], reads=["pC"],
           writes=["acol"], tag="aev")
        ew(p, "vector", "tensor_scalar_mul", kvs[:, 0:512], kvs[:, 0:512],
           acol[:, 0:1], reads=["kvs", "acol"], writes=["kvs"], tag="akv0")
        ew(p, "vector", "tensor_scalar_mul", kfss[:, 0:8], kfss[:, 0:8],
           acol[:, 0:1], reads=["kfss", "acol"], writes=["kfss"], tag="akf0")
        for i in range(1, 4):
            ew(p, "vector", "scalar_tensor_tensor", kvs[:, 0:512],
               kvs[:, i * 512:(i + 1) * 512], acol[:, i:i + 1], kvs[:, 0:512],
               ALU.mult, ALU.add, reads=["kvs", "acol"], writes=["kvs"],
               tag=f"akv{i}")
            ew(p, "vector", "scalar_tensor_tensor", kfss[:, 0:8],
               kfss[:, i * 8:(i + 1) * 8], acol[:, i:i + 1], kfss[:, 0:8],
               ALU.mult, ALU.add, reads=["kfss", "acol"], writes=["kfss"],
               tag=f"akf{i}")
            ew(p, "vector", "tensor_add", vss[:, 0:1024], vss[:, 0:1024],
               vss[:, i * 1024:(i + 1) * 1024], reads=["vss"], writes=["vss"],
               tag=f"svs{i}")
    else:
        for i in range(1, 4):
            ew(p, "vector", "tensor_add", kvs[:, 0:512], kvs[:, 0:512],
               kvs[:, i * 512:(i + 1) * 512], reads=["kvs"], writes=["kvs"],
               tag=f"skv{i}")
            ew(p, "vector", "tensor_add", kfss[:, 0:8], kfss[:, 0:8],
               kfss[:, i * 8:(i + 1) * 8], reads=["kfss"], writes=["kfss"],
               tag=f"skfs{i}")
            ew(p, "vector", "tensor_add", vss[:, 0:1024], vss[:, 0:1024],
               vss[:, i * 1024:(i + 1) * 1024], reads=["vss"], writes=["vss"],
               tag=f"svs{i}")
    trip = []
    for h in range(16):
        trip.append((pC[(h % 2) * 64:(h % 2) * 64 + 64,
                        (h // 2) * 64:(h // 2) * 64 + 64],
                     aux["ones1r"][:, 0:64],
                     vss[0:1, h * 64:(h + 1) * 64], h < 2, False))
    mm_group(p, trip, reads=["vss", "ln_ones1r"], writes=["pC"], tag="kvc",
             skip_group_check=True)
    ew(p, "vector", "scalar_tensor_tensor", kvs[:, 0:512], pC[:], CEPS,
       kvs[:, 0:512], ALU.mult, ALU.add, reads=["pC", "kvs"], writes=["kvs"],
       tag="kvcorr")
    act(p, kvb[:], kvs[:, 0:512], AF.Copy, reads=["kvs"], writes=["kvb"],
        tag="kvcast")
    ew(p, "vector", "tensor_scalar_add", denv[:], kfss[:, 0:8],
       CEPS * float(ttok), reads=["kfss"], writes=["denv"], tag="denv")
    ew(p, "gpsimd", "memset", zallf[:], 0.0, writes=["zallf"], tag="z0")
    for h in range(16):
        ew(p, "vector", "tensor_copy",
           zallf[(h % 2) * 64:(h % 2) * 64 + 64, (h // 2) * 16 + h:(h // 2) * 16 + h + 1],
           denv[(h % 2) * 64:(h % 2) * 64 + 64, h // 2:h // 2 + 1],
           reads=["denv", "zallf"], writes=["zallf"], tag=f"zc{h}")
    act(p, zall[:], zallf[:], AF.Copy, reads=["zallf"], writes=["zall"],
        tag="zcast")
    ew(p, "gpsimd", "memset", cepsc[:], 1.0, writes=["cepsc"], tag="cep")
    trip = [(pden[:, 0:1], zall[:, mc * 16:(mc + 1) * 16], cepsc[:],
             mc == 0, mc == 7) for mc in range(8)]
    mm_group(p, trip, reads=["zall", "cepsc"], writes=["pden"], tag="denc")
    ew(p, "vector", "tensor_scalar_mul", denc[:], pden[:, 0:1], CEPS,
       reads=["pden"], writes=["denc"], tag="dencev")
    for g in range(2):
        pi = g % 2
        p.dma("sync", wtmp2[pi].rearrange("p (k m) -> p k m", m=D),
              wo.rearrange("(k p) m -> p k m", p=128)[:, 4 * g:4 * g + 4, :],
              writes=[f"wtmp{pi}"], tag=f"wo{g}")
        act(p, wob[:, g * 4 * D:(g + 1) * 4 * D], wtmp2[pi], AF.Copy,
            reads=[f"wtmp{pi}"], writes=["wob"], tag=f"woc{g}")
    for g in range(8):
        pi = g % 2
        p.dma("sync", wtmp2[pi],
              w1.rearrange("(k p) m -> p k m", p=128)[:, g, :],
              writes=[f"wtmp{pi}"], tag=f"w1l{g}")
        act(p, wtmpb2_[pi], wtmp2[pi], AF.Copy, reads=[f"wtmp{pi}"],
            writes=[f"wtmpb{pi}"], tag=f"w1c{g}")
        p.dma("sync", w1b[:, g * DFF:(g + 1) * DFF], wtmpb2_[pi],
              reads=[f"wtmpb{pi}"], writes=["w1b_hbm"], tag=f"w1s{g}")
    for g in range(8):
        pi = g % 2
        p.dma("sync", wtmp2[pi].rearrange("p (k m) -> p k m", m=D),
              w2.rearrange("(k p) m -> p k m", p=128)[:, 4 * g:4 * g + 4, :],
              writes=[f"wtmp{pi}"], tag=f"w2l{g}")
        act(p, wtmpb2_[pi], wtmp2[pi], AF.Copy, reads=[f"wtmp{pi}"],
            writes=[f"wtmpb{pi}"], tag=f"w2c{g}")
        p.dma("sync", w2b[:, g * 4 * D:(g + 1) * 4 * D], wtmpb2_[pi],
              reads=[f"wtmpb{pi}"], writes=["w2b_hbm"], tag=f"w2s{g}")

    b1s = p.sb("b1s", [128, 32], F32)
    b2s = p.sb("b2s", [128, 8], F32)
    lngs = {nm: p.sb(nm + "s", [128, 8], F32) for nm in lng}
    p.dma("sync", b1s[:], b1c, writes=["b1s"], tag="ldb1")
    p.dma("sync", b2s[:], b2c, writes=["b2s"], tag="ldb2")
    for nm in lng:
        p.dma("sync", lngs[nm][:], lng[nm], writes=[nm], tag="ld" + nm)

    for pk, dst in (("wtmp0", "z1"), ("wtmp1", "out1f"),
                    ("wtmpb0", "hb"), ("wtmpb1", "hb")):
        if pk in p.res:
            p.res.setdefault(dst, {"w": None, "r": []})["r"].extend(
                p.res[pk]["r"] + ([p.res[pk]["w"]] if p.res[pk]["w"] else []))
    x_r = x.rearrange("(k p) t -> p k t", p=128)
    qf_r = qf.rearrange("(k p) t -> p k t", p=128)
    x1_r = o_x1.rearrange("(k p) t -> p k t", p=128)

    for c in range(nch):
        bi = c % 2
        s = slice(c * tch, (c + 1) * tch)
        p.dma("sync", xc[bi][:], x_r[:, :, s], writes=["xc0"], tag=f"lx{c}")
        p.dma("sync", qfc[bi][:], qf_r[:, :, s], writes=["qfc0"], tag=f"lq{c}")

        trip = [(pden[:], zall[:, mc * 16:(mc + 1) * 16],
                 qfc[bi][:, mc * tch:(mc + 1) * tch], mc == 0, mc == 7)
                for mc in range(8)]
        mm_group(p, trip, reads=["zall", "qfc0"], writes=["pden"],
                 tag=f"den{c}")
        ew(p, "vector", "tensor_scalar", denf[:], pden[:], denc[:, 0:1], None,
           ALU.add, reads=["pden", "denc"], writes=["denf"], tag=f"denf{c}")
        ew(p, "vector", "reciprocal", rden[:], denf[:], reads=["denf"],
           writes=["rden"], tag=f"rden{c}")

        for mc in range(8):
            pb = pA if mc % 2 == 0 else pB
            pbk = "pA" if mc % 2 == 0 else "pB"
            mm_group(p, [(pb[:], sel16[:, mc * 128:(mc + 1) * 128], rden[:],
                          True, True)],
                     reads=["rden", "sel16"], writes=[pbk], tag=f"rb{c}_{mc}")
            ew(p, "vector", "scalar_tensor_tensor",
               qsb[:, mc * tch:(mc + 1) * tch],
               qfc[bi][:, mc * tch:(mc + 1) * tch], CEPS, pb[:],
               ALU.add, ALU.mult, reads=["qfc0", pbk],
               writes=[f"qsb_{mc}", "qsb"], tag=f"qs{c}_{mc}")
        for mc in range(8):
            pb = pC if mc % 2 == 0 else pD
            pbk = "pC" if mc % 2 == 0 else "pD"
            trip = []
            for par in range(2):
                trip.append((pb[par * 64:par * 64 + 64, :],
                             kvb[par * 64:par * 64 + 64, mc * 64:mc * 64 + 64],
                             qsb[par * 64:par * 64 + 64, mc * tch:(mc + 1) * tch],
                             True, True))
            mm_group(p, trip, reads=["kvb", f"qsb_{mc}"], writes=[pbk],
                     tag=f"at{c}_{mc}")
            act(p, attnb[:, mc * tch:(mc + 1) * tch], pb[:], AF.Copy,
                reads=[pbk], writes=["attnb"], tag=f"atc{c}_{mc}")

        for oc in range(8):
            pb = pC if oc % 2 == 0 else pD
            pbk = "pC" if oc % 2 == 0 else "pD"
            trip = [(pb[:], wob[:, k * D + oc * 128:k * D + oc * 128 + 128],
                     attnb[:, k * tch:(k + 1) * tch], k == 0, k == 7)
                    for k in range(8)]
            mm_group(p, trip, reads=["wob", "attnb"], writes=[pbk],
                     tag=f"ao{c}_{oc}")
            ew(p, "vector", "tensor_add", z1[:, oc * tch:(oc + 1) * tch],
               xc[bi][:, oc * tch:(oc + 1) * tch], pb[:],
               reads=["xc0", pbk], writes=["z1"], tag=f"z1{c}_{oc}")

        aux["epscol"] = ec1
        aux["keys"]["epscol"] = eck1
        ln_fm(p, f"ln1_{c}", z1, "z1", out1f, "out1f", aux, EPS_LN,
              gamma=lngs["ln1g"], beta=lngs["ln1b"], gkey="ln1g",
              dst_bf=out1b, dstbfkey="out1b")

        for fc in range(32):
            wi = fc % 2
            p.dma("gpsimd", w1t[wi][:].rearrange("p (k f) -> p k f", f=128),
                  w1b.rearrange("p (k f) -> p k f", f=DFF)[:, :, fc * 128:(fc + 1) * 128],
                  reads=["w1b_hbm"], writes=[f"w1t{wi}"], tag=f"lw1_{c}_{fc}")
            pb = pC if fc % 2 == 0 else pD
            pbk = "pC" if fc % 2 == 0 else "pD"
            trip = [(pb[:], w1t[wi][:, k * 128:(k + 1) * 128],
                     out1b[:, k * tch:(k + 1) * tch], k == 0, k == 7)
                    for k in range(8)]
            mm_group(p, trip, reads=[f"w1t{wi}", "out1b"], writes=[pbk],
                     tag=f"f1{c}_{fc}")
            bsl = slice((fc % 2) * tch, (fc % 2) * tch + tch)
            xk_, ek_ = f"xbt{fc % 2}", f"ebt{fc % 2}"
            ew(p, "vector", "tensor_scalar", xbtmp[:, bsl], pb[:],
               b1s[:, fc:fc + 1], None, ALU.add, reads=[pbk, "b1s"],
               writes=[xk_], tag=f"xb{c}_{fc}")
            ew(p, "vector", "tensor_scalar_min", etmp[:, bsl], xbtmp[:, bsl], 0.0,
               reads=[xk_], writes=[ek_], tag=f"xn{c}_{fc}")
            act(p, etmp[:, bsl], etmp[:, bsl], AF.Exp, reads=[ek_], writes=[ek_],
                tag=f"ex{c}_{fc}")
            ew(p, "vector", "scalar_tensor_tensor", etmp[:, bsl], xbtmp[:, bsl],
               0.0, etmp[:, bsl], ALU.max, ALU.add, reads=[xk_, ek_],
               writes=[ek_], tag=f"hm{c}_{fc}")
            ew(p, "vector", "tensor_scalar_add", hb[:, fc * tch:(fc + 1) * tch],
               etmp[:, bsl], -1.0, reads=[ek_], writes=["hb", f"hb_{fc}"],
               tag=f"hc{c}_{fc}")

        for oc in range(8):
            wi = oc % 2
            p.dma("gpsimd", w2t[wi][:].rearrange("p (k f) -> p k f", f=128),
                  w2b.rearrange("p (k f) -> p k f", f=D)[:, :, oc * 128:(oc + 1) * 128],
                  reads=["w2b_hbm"], writes=["w2t0"], tag=f"lw2_{c}_{oc}")
            pb = pC if oc % 2 == 0 else pD
            pbk = "pC" if oc % 2 == 0 else "pD"
            trip = [(pb[:], w2t[wi][:, k * 128:(k + 1) * 128],
                     hb[:, k * tch:(k + 1) * tch], k == 0, k == 31)
                    for k in range(32)]
            mm_group(p, trip, reads=["w2t0", "hb"], writes=[pbk],
                     tag=f"f2{c}_{oc}")
            ew(p, "vector", "scalar_tensor_tensor",
               z1[:, oc * tch:(oc + 1) * tch], pb[:], b2s[:, oc:oc + 1],
               out1f[:, oc * tch:(oc + 1) * tch], ALU.add, ALU.add,
               reads=[pbk, "b2s", "out1f"], writes=["z1"], tag=f"z2{c}_{oc}")
        ln_fm(p, f"ln2_{c}", z1, "z1", out1f, "out1f", aux, EPS_LN,
              gamma=lngs["ln2g"], beta=lngs["ln2b"], gkey="ln2g")
        for oc in range(8):
            sl = slice(oc * tch, (oc + 1) * tch)
            ew(p, "vector", "tensor_add", z1[:, sl], out1f[:, sl],
               xc[bi][:, sl], reads=["out1f", "xc0"], writes=["z1"],
               tag=f"z3{c}_{oc}")
        aux["epscol"] = ec2
        aux["keys"]["epscol"] = eck2
        ln_fm(p, f"lno_{c}", z1, "z1", out1f, "out1f", aux, EPS_LN_OUTER)
        p.dma("sync", x1_r[:, :, s], out1f[:], reads=["out1f"], tag=f"sx1{c}")

    if x_h is not None:
        W4 = 4
        xh4 = p.sb("xh4", [128, 32], F32)
        qh4 = p.sb("qh4", [128, 32], BF16)
        qs4 = p.sb("qs4", [128, 32], BF16)
        at4 = p.sb("at4", [128, 32], BF16)
        msk = p.sb("msk", [128, 1], F32)
        x1h4 = p.sb("x1h4", [128, 32], F32)
        aux["epscol"] = ec1
        aux["keys"]["epscol"] = eck1
        p.dma("sync", xh4[:], x_h, writes=["xh4"], tag="hlx")
        p.dma("sync", qh4[:], qf_h, writes=["qh4"], tag="hlq")
        p.dma("sync", msk[:], mask1, writes=["msk"], tag="hlm")
        trip = [(pden[:, 0:W4], zall[:, mc * 16:(mc + 1) * 16],
                 qh4[:, mc * W4:(mc + 1) * W4], mc == 0, mc == 7)
                for mc in range(8)]
        mm_group(p, trip, reads=["zall", "qh4"], writes=["pden"], tag="hden")
        ew(p, "vector", "tensor_scalar", denf[:, 0:W4], pden[:, 0:W4],
           denc[:, 0:1], None, ALU.add, reads=["pden", "denc"],
           writes=["denf"], tag="hdenf")
        ew(p, "vector", "reciprocal", rden[:, 0:W4], denf[:, 0:W4],
           reads=["denf"], writes=["rden"], tag="hrden")
        for mc in range(8):
            pb = pA if mc % 2 == 0 else pB
            pbk = "pA" if mc % 2 == 0 else "pB"
            mm_group(p, [(pb[:, 0:W4], sel16[:, mc * 128:(mc + 1) * 128],
                          rden[:, 0:W4], True, True)],
                     reads=["rden", "sel16"], writes=[pbk], tag=f"hrb{mc}")
            ew(p, "vector", "scalar_tensor_tensor", qs4[:, mc * W4:(mc + 1) * W4],
               qh4[:, mc * W4:(mc + 1) * W4], CEPS, pb[:, 0:W4],
               ALU.add, ALU.mult, reads=["qh4", pbk], writes=["qs4"],
               tag=f"hqs{mc}")
        for mc in range(8):
            pb = pC if mc % 2 == 0 else pD
            pbk = "pC" if mc % 2 == 0 else "pD"
            trip = [(pb[par * 64:par * 64 + 64, 0:W4],
                     kvb[par * 64:par * 64 + 64, mc * 64:mc * 64 + 64],
                     qs4[par * 64:par * 64 + 64, mc * W4:(mc + 1) * W4],
                     True, True) for par in range(2)]
            mm_group(p, trip, reads=["kvb", "qs4"], writes=[pbk], tag=f"hat{mc}")
            act(p, at4[:, mc * W4:(mc + 1) * W4], pb[:, 0:W4], AF.Copy,
                reads=[pbk], writes=["at4"], tag=f"hatc{mc}")
        for oc in range(8):
            pb = pC if oc % 2 == 0 else pD
            pbk = "pC" if oc % 2 == 0 else "pD"
            trip = [(pb[:, 0:W4], wob[:, k * D + oc * 128:k * D + oc * 128 + 128],
                     at4[:, k * W4:(k + 1) * W4], k == 0, k == 7)
                    for k in range(8)]
            mm_group(p, trip, reads=["wob", "at4"], writes=[pbk], tag=f"hao{oc}")
            ew(p, "vector", "tensor_add", z1[:, oc * tch:oc * tch + W4],
               xh4[:, oc * W4:(oc + 1) * W4], pb[:, 0:W4],
               reads=["xh4", pbk], writes=["z1"], tag=f"hz1{oc}")
        ln_fm(p, "hln1", z1, "z1", out1f, "out1f", aux, EPS_LN,
              gamma=lngs["ln1g"], beta=lngs["ln1b"], gkey="ln1g",
              dst_bf=out1b, dstbfkey="out1b", wid=W4)
        for fc in range(32):
            wi = fc % 2
            p.dma("gpsimd", w1t[wi][:].rearrange("p (k f) -> p k f", f=128),
                  w1b.rearrange("p (k f) -> p k f", f=DFF)[:, :, fc * 128:(fc + 1) * 128],
                  reads=["w1b_hbm"], writes=[f"w1t{wi}"], tag=f"hlw1_{fc}")
            pb = pC if fc % 2 == 0 else pD
            pbk = "pC" if fc % 2 == 0 else "pD"
            trip = [(pb[:, 0:W4], w1t[wi][:, k * 128:(k + 1) * 128],
                     out1b[:, k * tch:k * tch + W4], k == 0, k == 7)
                    for k in range(8)]
            mm_group(p, trip, reads=[f"w1t{wi}", "out1b"], writes=[pbk],
                     tag=f"hf1{fc}")
            ew(p, "vector", "tensor_scalar", xbtmp[:, 0:W4], pb[:, 0:W4],
               b1s[:, fc:fc + 1], None, ALU.add, reads=[pbk, "b1s"],
               writes=["xbtmp"], tag=f"hxb{fc}")
            ew(p, "vector", "tensor_scalar_min", etmp[:, 0:W4], xbtmp[:, 0:W4],
               0.0, reads=["xbtmp"], writes=["etmp"], tag=f"hxn{fc}")
            act(p, etmp[:, 0:W4], etmp[:, 0:W4], AF.Exp, reads=["etmp"],
                writes=["etmp"], tag=f"hex{fc}")
            ew(p, "vector", "scalar_tensor_tensor", htmp[:, 0:W4],
               xbtmp[:, 0:W4], 0.0, etmp[:, 0:W4], ALU.max, ALU.add,
               reads=["xbtmp", "etmp"], writes=["htmp"], tag=f"hhm{fc}")
            ew(p, "vector", "tensor_scalar_add", hb[:, fc * tch:fc * tch + W4],
               htmp[:, 0:W4], -1.0, reads=["htmp"], writes=["hb"],
               tag=f"hhc{fc}")
        for oc in range(8):
            wi = oc % 2
            p.dma("gpsimd", w2t[wi][:].rearrange("p (k f) -> p k f", f=128),
                  w2b.rearrange("p (k f) -> p k f", f=D)[:, :, oc * 128:(oc + 1) * 128],
                  reads=["w2b_hbm"], writes=["w2t0"], tag=f"hlw2_{oc}")
            pb = pC if oc % 2 == 0 else pD
            pbk = "pC" if oc % 2 == 0 else "pD"
            trip = [(pb[:, 0:W4], w2t[wi][:, k * 128:(k + 1) * 128],
                     hb[:, k * tch:k * tch + W4], k == 0, k == 31)
                    for k in range(32)]
            mm_group(p, trip, reads=["w2t0", "hb"], writes=[pbk], tag=f"hf2{oc}")
            ew(p, "vector", "scalar_tensor_tensor",
               z1[:, oc * tch:oc * tch + W4], pb[:, 0:W4], b2s[:, oc:oc + 1],
               out1f[:, oc * tch:oc * tch + W4], ALU.add, ALU.add,
               reads=[pbk, "b2s", "out1f"], writes=["z1"], tag=f"hz2{oc}")
        ln_fm(p, "hln2", z1, "z1", out1f, "out1f", aux, EPS_LN,
              gamma=lngs["ln2g"], beta=lngs["ln2b"], gkey="ln2g", wid=W4)
        for oc in range(8):
            ew(p, "vector", "tensor_add", z1[:, oc * tch:oc * tch + W4],
               out1f[:, oc * tch:oc * tch + W4], xh4[:, oc * W4:(oc + 1) * W4],
               reads=["out1f", "xh4"], writes=["z1"], tag=f"hz3{oc}")
        aux["epscol"] = ec2
        aux["keys"]["epscol"] = eck2
        ln_fm(p, "hlno", z1, "z1", out1f, "out1f", aux, EPS_LN_OUTER, wid=W4)
        for oc in range(8):
            ew(p, "vector", "tensor_scalar_mul", x1h4[:, oc * W4:(oc + 1) * W4],
               out1f[:, oc * tch:oc * tch + W4], msk[:, 0:1],
               reads=["out1f", "msk"], writes=["x1h4"], tag=f"hmk{oc}")
        p.dma("sync", o_x1h, x1h4[:], reads=["x1h4"], tag="hsx1h")

    if env is None:
        p.final_wait("sync")
        p.build()
        p.finish()
        return nc



def build_l4(sh=SH, tch=512, env=None):
    nch = sh // tch
    if env is None:
        nc = bass.Bass("TRN2", target_bir_lowering=False)
        x1 = nc.dram_tensor("x1", [D, sh], F32, kind="ExternalInput").ap()
        xprev = nc.dram_tensor("xprev", [128, 8], F32, kind="ExternalInput").ap()
        wrT = nc.dram_tensor("wrT", [D, D], F32, kind="ExternalInput").ap()
        wkT = nc.dram_tensor("wkT", [D, D], F32, kind="ExternalInput").ap()
        wvT = nc.dram_tensor("wvT", [D, D], F32, kind="ExternalInput").ap()
        murc = nc.dram_tensor("murc", [128, 8], F32, kind="ExternalInput").ap()
        mukc = nc.dram_tensor("mukc", [128, 8], F32, kind="ExternalInput").ap()
        o_x2 = nc.dram_tensor("x2", [D, sh], F32, kind="ExternalOutput").ap()

        p = Prog(nc)
    else:
        nc, p = env["nc"], env["p"]
        x1 = env["x1"]
        xprev = env["xprev"]
        wrT = env["wrT"]
        wkT = env["wkT"]
        wvT = env["wvT"]
        murc = env["murc"]
        mukc = env["mukc"]
        o_x2 = env["o_x2"]
        x1h = env.get("x1h")
        mask1 = env.get("mask1")
        o_x2p3 = env.get("o_x2p3")
    if env is None:
        x1h = mask1 = o_x2p3 = None
    wrb = p.sb("wrb", [128, 8 * D], BF16)
    wkb = p.sb("wkb", [128, 8 * D], BF16)
    wvb = p.sb("wvb", [128, 8 * D], BF16)
    xbuf = p.sb("xbuf", [128, 8 * (tch + 1)], F32)
    dbuf = p.sb("dbuf", [128, 8 * tch], F32)
    xrb = p.sb("xrb", [128, 8 * tch], BF16)
    xkb = p.sb("xkb", [128, 8 * tch], BF16)
    rbuf = p.sb("rbuf", [128, 8 * tch], F32)
    kab = p.sb("kab", [128, 8 * tch], BF16)
    ktmp = p.sb("ktmp", [128, tch], F32)
    z1 = p.sb("z1", [128, 8 * tch], F32)
    x2o = p.sb("x2o", [128, 8 * tch], F32)
    murs = p.sb("murs", [128, 8], F32)
    muks = p.sb("muks", [128, 8], F32)
    aux = make_ln_aux(p, "ln", tch)
    wtmp = z1

    pC = p.ps("pC", [128, tch], F32)
    pD = p.ps("pD", [128, tch], F32)
    pA = p.ps("pA", [128, tch], F32)
    pB = p.ps("pB", [128, tch], F32)
    ps1 = p.ps("ps1", [1, tch], F32)
    ps2 = p.ps("ps2", [1, tch], F32)
    aux.update(ps1=ps1, ps2=ps2, pbm=pA, pba=pB, utile=ktmp)
    aux["keys"].update(ps1="ps1", ps2="ps2", pbm="pA", pba="pB", utile="ktmp")
    ec2, eck2 = p.constcol(EPS_LN_OUTER)
    aux["epscol"] = ec2
    aux["keys"]["epscol"] = eck2

    for wnm, src, dst in [("wrb", wrT, wrb), ("wkb", wkT, wkb),
                          ("wvb", wvT, wvb)]:
        for g in range(2):
            p.dma("sync", wtmp[:].rearrange("p (k m) -> p k m", m=D),
                  src.rearrange("(k p) m -> p k m", p=128)[:, 4 * g:4 * g + 4, :],
                  writes=["wtmp"], tag=f"wl{wnm}{g}")
            act(p, dst[:, g * 4 * D:(g + 1) * 4 * D], wtmp[:], AF.Copy,
                reads=["wtmp"], writes=[wnm], tag=f"wc{wnm}{g}")
    p.dma("sync", murs[:], murc, writes=["murs"], tag="lmur")
    p.dma("sync", muks[:], mukc, writes=["muks"], tag="lmuk")

    x1_r = x1.rearrange("(k p) t -> p k t", p=128)
    x2_r = o_x2.rearrange("(k p) t -> p k t", p=128)
    xb3 = xbuf[:].rearrange("p (k t) -> p k t", t=tch + 1)

    for c in range(nch):
        s = slice(c * tch, (c + 1) * tch)
        if c == 0:
            p.dma("sync", xb3[:, :, 0:1], xprev.rearrange("p (k o) -> p k o", o=1),
                  writes=["xh"], tag=f"lh{c}", allow_slow_non_contiguous=True)
        else:
            p.dma("sync", xb3[:, :, 0:1], x1_r[:, :, c * tch - 1:c * tch],
                  writes=["xh"], tag=f"lh{c}", allow_slow_non_contiguous=True)
        p.dma("sync", xb3[:, :, 1:tch + 1], x1_r[:, :, s], writes=["xc"],
              tag=f"lx{c}")
        ew(p, "vector", "tensor_sub", dbuf[:].rearrange("p (k t) -> p k t", t=tch),
           xb3[:, :, 1:tch + 1], xb3[:, :, 0:tch],
           reads=["xh", "xc"], writes=["dbuf"], tag=f"d{c}")
        for k in range(8):
            ew(p, "vector", "scalar_tensor_tensor",
               xrb[:, k * tch:(k + 1) * tch], dbuf[:, k * tch:(k + 1) * tch],
               murs[:, k:k + 1], xb3[:, k, 0:tch], ALU.mult, ALU.add,
               reads=["dbuf", "murs", "xc", "xh"], writes=["xrb"],
               tag=f"xr{c}_{k}")
            ew(p, "vector", "scalar_tensor_tensor",
               xkb[:, k * tch:(k + 1) * tch], dbuf[:, k * tch:(k + 1) * tch],
               muks[:, k:k + 1], xb3[:, k, 0:tch], ALU.mult, ALU.add,
               reads=["dbuf", "muks", "xc", "xh"], writes=["xkb"],
               tag=f"xk{c}_{k}")
        for oc in range(8):
            pb, pbk = (pC, "pC") if oc % 2 == 0 else (pD, "pD")
            mm_group(p, [(pb[:], wrb[:, k * D + oc * 128:k * D + oc * 128 + 128],
                          xrb[:, k * tch:(k + 1) * tch], k == 0, k == 7)
                         for k in range(8)],
                     reads=["wrb", "xrb"], writes=[pbk], tag=f"r{c}_{oc}")
            act(p, rbuf[:, oc * tch:(oc + 1) * tch], pb[:], AF.Sigmoid,
                reads=[pbk], writes=["rbuf"], tag=f"rs{c}_{oc}")
        for oc in range(8):
            pb, pbk = (pC, "pC") if oc % 2 == 0 else (pD, "pD")
            mm_group(p, [(pb[:], wkb[:, k * D + oc * 128:k * D + oc * 128 + 128],
                          xkb[:, k * tch:(k + 1) * tch], k == 0, k == 7)
                         for k in range(8)],
                     reads=["wkb", "xkb"], writes=[pbk], tag=f"k{c}_{oc}")
            act(p, ktmp[:], pb[:], AF.Relu, reads=[pbk], writes=["ktmp"],
                tag=f"kr{c}_{oc}")
            act(p, kab[:, oc * tch:(oc + 1) * tch], ktmp[:], AF.Square,
                reads=["ktmp"], writes=["kab"], tag=f"kq{c}_{oc}")
        for oc in range(8):
            pb, pbk = (pC, "pC") if oc % 2 == 0 else (pD, "pD")
            mm_group(p, [(pb[:], wvb[:, k * D + oc * 128:k * D + oc * 128 + 128],
                          kab[:, k * tch:(k + 1) * tch], k == 0, k == 7)
                         for k in range(8)],
                     reads=["wvb", "kab"], writes=[pbk], tag=f"v{c}_{oc}")
            ew(p, "vector", "tensor_mul", z1[:, oc * tch:(oc + 1) * tch],
               rbuf[:, oc * tch:(oc + 1) * tch], pb[:],
               reads=["rbuf", pbk], writes=["z1"], tag=f"zm{c}_{oc}")
            ew(p, "vector", "tensor_add", z1[:, oc * tch:(oc + 1) * tch],
               z1[:, oc * tch:(oc + 1) * tch], xb3[:, oc, 1:tch + 1],
               reads=["z1", "xc"], writes=["z1"], tag=f"za{c}_{oc}")
        ln_fm(p, f"lno_{c}", z1, "z1", x2o, "x2o", aux, EPS_LN_OUTER)
        p.dma("sync", x2_r[:, :, s], x2o[:], reads=["x2o"], tag=f"sx2{c}")

    if x1h is not None:
        W3 = 3
        xh4 = p.sb("xh4", [128, 32], F32)
        msk = p.sb("msk", [128, 1], F32)
        d3 = p.sb("d3", [128, 24], F32)
        xr3 = p.sb("xr3", [128, 24], BF16)
        xk3 = p.sb("xk3", [128, 24], BF16)
        r3 = p.sb("r3", [128, 24], F32)
        ka3 = p.sb("ka3", [128, 24], BF16)
        x23 = p.sb("x23", [128, 24], F32)
        p.dma("sync", xh4[:], x1h, writes=["xh4"], tag="h4lx")
        p.dma("sync", msk[:], mask1, writes=["msk"], tag="h4lm")
        xh3 = xh4[:].rearrange("p (k t) -> p k t", t=4)
        for k in range(8):
            ew(p, "vector", "tensor_sub", d3[:, k * W3:(k + 1) * W3],
               xh3[:, k, 1:4], xh3[:, k, 0:3], reads=["xh4"], writes=["d3"],
               tag=f"h4d{k}")
            ew(p, "vector", "scalar_tensor_tensor", xr3[:, k * W3:(k + 1) * W3],
               d3[:, k * W3:(k + 1) * W3], murs[:, k:k + 1], xh3[:, k, 0:3],
               ALU.mult, ALU.add, reads=["d3", "murs", "xh4"], writes=["xr3"],
               tag=f"h4r{k}")
            ew(p, "vector", "scalar_tensor_tensor", xk3[:, k * W3:(k + 1) * W3],
               d3[:, k * W3:(k + 1) * W3], muks[:, k:k + 1], xh3[:, k, 0:3],
               ALU.mult, ALU.add, reads=["d3", "muks", "xh4"], writes=["xk3"],
               tag=f"h4k{k}")
        for oc in range(8):
            pb, pbk = (pC, "pC") if oc % 2 == 0 else (pD, "pD")
            trip = [(pb[:, 0:W3], wrb[:, k * D + oc * 128:k * D + oc * 128 + 128],
                     xr3[:, k * W3:(k + 1) * W3], k == 0, k == 7)
                    for k in range(8)]
            mm_group(p, trip, reads=["wrb", "xr3"], writes=[pbk], tag=f"h4rm{oc}")
            act(p, r3[:, oc * W3:(oc + 1) * W3], pb[:, 0:W3], AF.Sigmoid,
                reads=[pbk], writes=["r3"], tag=f"h4rs{oc}")
        for oc in range(8):
            pb, pbk = (pC, "pC") if oc % 2 == 0 else (pD, "pD")
            trip = [(pb[:, 0:W3], wkb[:, k * D + oc * 128:k * D + oc * 128 + 128],
                     xk3[:, k * W3:(k + 1) * W3], k == 0, k == 7)
                    for k in range(8)]
            mm_group(p, trip, reads=["wkb", "xk3"], writes=[pbk], tag=f"h4km{oc}")
            act(p, ktmp[:, 0:W3], pb[:, 0:W3], AF.Relu, reads=[pbk],
                writes=["ktmp"], tag=f"h4kr{oc}")
            act(p, ka3[:, oc * W3:(oc + 1) * W3], ktmp[:, 0:W3], AF.Square,
                reads=["ktmp"], writes=["ka3"], tag=f"h4kq{oc}")
        for oc in range(8):
            pb, pbk = (pC, "pC") if oc % 2 == 0 else (pD, "pD")
            trip = [(pb[:, 0:W3], wvb[:, k * D + oc * 128:k * D + oc * 128 + 128],
                     ka3[:, k * W3:(k + 1) * W3], k == 0, k == 7)
                    for k in range(8)]
            mm_group(p, trip, reads=["wvb", "ka3"], writes=[pbk], tag=f"h4vm{oc}")
            ew(p, "vector", "tensor_mul", z1[:, oc * tch:oc * tch + W3],
               r3[:, oc * W3:(oc + 1) * W3], pb[:, 0:W3],
               reads=["r3", pbk], writes=["z1"], tag=f"h4zm{oc}")
            ew(p, "vector", "tensor_add", z1[:, oc * tch:oc * tch + W3],
               z1[:, oc * tch:oc * tch + W3], xh3[:, oc, 1:4],
               reads=["z1", "xh4"], writes=["z1"], tag=f"h4za{oc}")
        ln_fm(p, "h4lno", z1, "z1", x2o, "x2o", aux, EPS_LN_OUTER, wid=W3)
        for oc in range(8):
            ew(p, "vector", "tensor_scalar_mul", x23[:, oc * W3:(oc + 1) * W3],
               x2o[:, oc * tch:oc * tch + W3], msk[:, 0:1],
               reads=["x2o", "msk"], writes=["x23"], tag=f"h4mk{oc}")
        p.dma("sync", o_x2p3, x23[:], reads=["x23"], tag="h4sx2")

    if env is None:
        p.final_wait("sync")
        p.build()
        p.finish()
        return nc



def build_l5(sh=SH, tch=512, env=None):
    nch = sh // tch
    NIC = D_INNER // 128
    if env is None:
        nc = bass.Bass("TRN2", target_bir_lowering=False)
        x2 = nc.dram_tensor("x2", [D, sh], F32, kind="ExternalInput").ap()
        x2p3 = nc.dram_tensor("x2p3", [128, 24], F32, kind="ExternalInput").ap()
        winT = nc.dram_tensor("winT", [D, 2 * D_INNER], F32, kind="ExternalInput").ap()
        wdtT = nc.dram_tensor("wdtT", [D_INNER, D_INNER], F32, kind="ExternalInput").ap()
        cwc = nc.dram_tensor("cwc", [128, 4 * NIC], F32, kind="ExternalInput").ap()
        cbc = nc.dram_tensor("cbc", [128, NIC], F32, kind="ExternalInput").ap()
        bdtc = nc.dram_tensor("bdtc", [128, NIC], F32, kind="ExternalInput").ap()
        dtbc = nc.dram_tensor("dtbc", [128, NIC], F32, kind="ExternalInput").ap()
        alogc = nc.dram_tensor("alogc", [128, NIC * D_STATE], F32, kind="ExternalInput").ap()
        bparc = nc.dram_tensor("bparc", [128, NIC * D_STATE], F32, kind="ExternalInput").ap()
        o_dt = nc.dram_tensor("dt", [D_INNER, sh], F32, kind="ExternalOutput").ap()
        o_uu = nc.dram_tensor("uu", [D_INNER, sh], F32, kind="ExternalOutput").ap()
        o_sg = nc.dram_tensor("sg", [D_INNER, sh], F32, kind="ExternalOutput").ap()
        o_send = nc.dram_tensor("send", [128, NIC * D_STATE], F32, kind="ExternalOutput").ap()
        o_gend = nc.dram_tensor("gend", [128, NIC * D_STATE], F32, kind="ExternalOutput").ap()

        p = Prog(nc)
    else:
        nc, p = env["nc"], env["p"]
        x2 = env["x2"]
        x2p3 = env["x2p3"]
        winT = env["winT"]
        wdtT = env["wdtT"]
        cwc = env["cwc"]
        cbc = env["cbc"]
        bdtc = env["bdtc"]
        dtbc = env["dtbc"]
        alogc = env["alogc"]
        bparc = env["bparc"]
        o_dt = env["o_dt"]
        o_uu = env["o_uu"]
        o_sg = env["o_sg"]
        o_send = env["o_send"]
        o_gend = env["o_gend"]
    wdtb_scr = nc.dram_tensor("wdtb_scr", [128, 16 * D_INNER], BF16, kind="Internal").ap()
    winb_scr = nc.dram_tensor("winb_scr", [128, 8 * 2 * D_INNER], BF16, kind="Internal").ap()
    win_t = [p.sb(f"win_t{i}", [128, 8 * 128], BF16) for i in range(2)]
    xbld = p.sb("xbld", [128, NIC * (tch + 3)], F32)
    carry = p.sb("carry", [128, NIC * 3], F32)
    x2c = p.sb("x2c", [128, 8 * tch], F32)
    x2cb = p.sb("x2cb", [128, 8 * tch], BF16)
    x2p3b = p.sb("x2p3b", [128, 24], BF16)
    ubf = p.sb("ubf", [128, NIC * tch], BF16)
    usb = p.sb("usb", [128, tch], F32)
    sgsb = p.sb("sgsb", [128, tch], F32)
    dtsb = p.sb("dtsb", [128, tch], F32)
    xbt = p.sb("xbt", [128, tch], F32)
    ebt = p.sb("ebt", [128, tch], F32)
    wdt_t = [p.sb(f"wdt_t{i}", [128, 16 * 128], BF16) for i in range(2)]
    wtmpf = p.sb("wtmpf", [128, 2048], F32)
    wtmpb = p.sb("wtmpb", [128, 2048], BF16)
    cws = p.sb("cws", [128, 4 * NIC], F32)
    cbs = p.sb("cbs", [128, NIC], F32)
    bsum = p.sb("bsum", [128, NIC], F32)
    negA = p.sb("negA", [128, NIC * D_STATE], F32)
    bpar = p.sb("bpar", [128, NIC * D_STATE], F32)
    send = p.sb("send_sb", [128, NIC * D_STATE], F32)
    gend = p.sb("gend_sb", [128, NIC * D_STATE], F32)
    bdt = p.sb("bdt", [128, sh], F32)
    buu = p.sb("buu", [128, sh], F32)
    bdtu = p.sb("bdtu", [128, sh], F32)
    bdA = p.sb("bdA", [128, sh], F32)
    bdBu = p.sb("bdBu", [128, sh], F32)
    bS = p.sb("bS", [128, sh], F32)
    dtsum = p.sb("dtsum", [128, 1], F32)

    pC = p.ps("pC", [128, tch], F32)
    pD = p.ps("pD", [128, tch], F32)
    pH = p.ps("pH", [128, 3], F32)

    wtf2 = [bdt[:][:, 0:2048], bdt[:][:, 2048:4096]]
    wtb2 = [buu[:].bitcast(BF16)[:, 0:2048], buu[:].bitcast(BF16)[:, 2048:4096]]
    pcnt5 = 0
    for g in range(8):
        for hh in range(2):
            pi = pcnt5 % 2
            pcnt5 += 1
            p.dma("sync", wtf2[pi],
                  winT.rearrange("(k p) m -> p k m", p=128)[:, g, hh * 2048:(hh + 1) * 2048],
                  writes=[f"wtf5{pi}"], tag=f"wi{g}_{hh}")
            act(p, wtb2[pi], wtf2[pi], AF.Copy,
                reads=[f"wtf5{pi}"], writes=[f"wtb5{pi}"], tag=f"wic{g}_{hh}")
            p.dma("sync", winb_scr[:, g * 4096 + hh * 2048:g * 4096 + (hh + 1) * 2048],
                  wtb2[pi], reads=[f"wtb5{pi}"], writes=["winb_hbm"], tag=f"wis{g}_{hh}")
    for g in range(16):
        pi = pcnt5 % 2
        pcnt5 += 1
        p.dma("sync", wtf2[pi],
              wdtT.rearrange("(k p) m -> p k m", p=128)[:, g, :],
              writes=[f"wtf5{pi}"], tag=f"wd{g}")
        act(p, wtb2[pi], wtf2[pi], AF.Copy, reads=[f"wtf5{pi}"],
            writes=[f"wtb5{pi}"], tag=f"wdc{g}")
        p.dma("sync", wdtb_scr[:, g * D_INNER:(g + 1) * D_INNER], wtb2[pi],
              reads=[f"wtb5{pi}"], writes=["wdtb_hbm"], tag=f"wds{g}")
    for pk in ("wtf50", "wtf51"):
        if pk in p.res:
            p.res.setdefault("bdt", {"w": None, "r": []})["r"].extend(
                p.res[pk]["r"] + ([p.res[pk]["w"]] if p.res[pk]["w"] else []))
    for pk in ("wtb50", "wtb51"):
        if pk in p.res:
            p.res.setdefault("buu", {"w": None, "r": []})["r"].extend(
                p.res[pk]["r"] + ([p.res[pk]["w"]] if p.res[pk]["w"] else []))
    for nm, src, dst, key in [("cw", cwc, cws, "cws"), ("cb", cbc, cbs, "cbs"),
                              ("alog", alogc, negA, "negA0"),
                              ("bpar", bparc, bpar, "bpar")]:
        p.dma("sync", dst[:], src, writes=[key], tag=f"ld{nm}")
    p.dma("sync", bsum[:], bdtc, writes=["bsum"], tag="ldbdt")
    p.dma("sync", wtmpf[:, 0:NIC], dtbc, writes=["dtb_tmp"], tag="lddtb")
    ew(p, "vector", "tensor_add", bsum[:], bsum[:], wtmpf[:, 0:NIC],
       reads=["bsum", "dtb_tmp"], writes=["bsum"], tag="bsum")
    act(p, negA[:], negA[:], AF.Exp, reads=["negA0"], writes=["negA1"], tag="nega1")
    ew(p, "vector", "tensor_scalar_mul", negA[:], negA[:], -1.0,
       reads=["negA1"], writes=["negA"], tag="nega2")
    x2p3f = p.sb("x2p3f", [128, 24], F32)
    p.dma("sync", x2p3f[:], x2p3, writes=["x2p3f"], tag="ldp3")

    x2_r = x2.rearrange("(k p) t -> p k t", p=128)
    xbld3 = xbld[:].rearrange("p (i t) -> p i t", t=tch + 3)
    dt_r = o_dt.rearrange("(i p) t -> p i t", p=128)
    uu_r = o_uu.rearrange("(i p) t -> p i t", p=128)
    sg_r = o_sg.rearrange("(i p) t -> p i t", p=128)

    for c in range(nch):
        s = slice(c * tch, (c + 1) * tch)
        p.dma("sync", x2c[:], x2_r[:, :, s], writes=["x2c"], tag=f"lx{c}")
        act(p, x2cb[:], x2c[:], AF.Copy, reads=["x2c"], writes=["x2cb"],
            tag=f"xc{c}")
        if c == 0:
            act(p, x2p3b[:], x2p3f[:], AF.Copy,
                reads=["x2p3f"], writes=["x2p3bb"], tag="p3c")
            for ic in range(NIC):
                wi = ic % 2
                p.dma("gpsimd", win_t[wi][:].rearrange("p (k f) -> p k f", f=128),
                      winb_scr.rearrange("p (k f) -> p k f", f=4096)[:, :, ic * 128:(ic + 1) * 128],
                      reads=["winb_hbm"], writes=[f"win_t{wi}"], tag=f"hl{ic}")
                mm_group(p, [(pH[:], win_t[wi][:, k * 128:(k + 1) * 128],
                              x2p3b[:, k * 3:(k + 1) * 3], k == 0, k == 7)
                             for k in range(8)],
                         reads=[f"win_t{wi}", "x2p3bb"], writes=["pH"],
                         tag=f"hx{ic}")
                ew(p, "vector", "tensor_copy", carry[:, ic * 3:(ic + 1) * 3],
                   pH[:], reads=["pH"], writes=["carry"], tag=f"hc{ic}")
        ew(p, "vector", "tensor_copy",
           xbld3[:, :, 0:3], carry[:].rearrange("p (i t) -> p i t", t=3),
           reads=["carry"], writes=["xbh"], tag=f"pc{c}")
        for ic in range(NIC):
            wi = ic % 2
            p.dma("gpsimd", win_t[wi][:].rearrange("p (k f) -> p k f", f=128),
                  winb_scr.rearrange("p (k f) -> p k f", f=4096)[:, :, ic * 128:(ic + 1) * 128],
                  reads=["winb_hbm"], writes=[f"win_t{wi}"], tag=f"xzl{c}_{ic}")
            pb, pbk = (pC, "pC") if ic % 2 == 0 else (pD, "pD")
            mm_group(p, [(pb[:], win_t[wi][:, k * 128:(k + 1) * 128],
                          x2cb[:, k * tch:(k + 1) * tch], k == 0, k == 7)
                         for k in range(8)],
                     reads=[f"win_t{wi}", "x2cb"], writes=[pbk], tag=f"xz{c}_{ic}")
            ew(p, "vector", "tensor_copy", xbld3[:, ic, 3:tch + 3], pb[:],
               reads=[pbk], writes=[f"xb_{ic}", "xbB"], tag=f"xze{c}_{ic}")
        for ic in range(NIC):
            wi = ic % 2
            j = 2048 + ic * 128
            p.dma("gpsimd", win_t[wi][:].rearrange("p (k f) -> p k f", f=128),
                  winb_scr.rearrange("p (k f) -> p k f", f=4096)[:, :, j:j + 128],
                  reads=["winb_hbm"], writes=[f"win_t{wi}"], tag=f"gl{c}_{ic}")
            pb, pbk = (pC, "pC") if ic % 2 == 0 else (pD, "pD")
            mm_group(p, [(pb[:], win_t[wi][:, k * 128:(k + 1) * 128],
                          x2cb[:, k * tch:(k + 1) * tch], k == 0, k == 7)
                         for k in range(8)],
                     reads=[f"win_t{wi}", "x2cb"], writes=[pbk], tag=f"g{c}_{ic}")
            act(p, sgsb[:], pb[:], AF.Silu, reads=[pbk], writes=["sgsb"],
                tag=f"gs{c}_{ic}")
            p.dma("sync", sg_r[:, ic, s], sgsb[:], reads=["sgsb"],
                  tag=f"ssg{c}_{ic}")
        for ic in range(NIC):
            ew(p, "vector", "tensor_scalar", usb[:], xbld3[:, ic, 0:tch],
               cws[:, ic * 4:ic * 4 + 1], None, ALU.mult,
               reads=[f"xb_{ic}", "xbh", "cws"], writes=["usb"], tag=f"cv{c}_{ic}_0")
            for kk in range(1, 4):
                ew(p, "vector", "scalar_tensor_tensor", usb[:],
                   xbld3[:, ic, kk:kk + tch], cws[:, ic * 4 + kk:ic * 4 + kk + 1],
                   usb[:], ALU.mult, ALU.add,
                   reads=[f"xb_{ic}", "xbh", "cws", "usb"], writes=["usb"],
                   tag=f"cv{c}_{ic}_{kk}")
            act(p, usb[:], usb[:], AF.Silu, bias=cbs[:, ic:ic + 1],
                reads=["usb", "cbs"], writes=["usb"], tag=f"cs{c}_{ic}")
            p.dma("sync", uu_r[:, ic, s], usb[:], reads=["usb"], tag=f"su{c}_{ic}")
            act(p, ubf[:, ic * tch:(ic + 1) * tch], usb[:], AF.Copy,
                reads=["usb"], writes=["ubf"], tag=f"ub{c}_{ic}")
        ew(p, "vector", "tensor_copy", carry[:].rearrange("p (i t) -> p i t", t=3),
           xbld3[:, :, tch:tch + 3], reads=["xbB", "xbh"], writes=["carry"],
           tag=f"sc{c}")
        for io in range(NIC):
            wi = io % 2
            p.dma("gpsimd", wdt_t[wi][:].rearrange("p (k f) -> p k f", f=128),
                  wdtb_scr.rearrange("p (k f) -> p k f", f=D_INNER)[:, :, io * 128:(io + 1) * 128],
                  reads=["wdtb_hbm"], writes=[f"wdt_t{wi}"], tag=f"lwd{c}_{io}")
            pb, pbk = (pC, "pC") if io % 2 == 0 else (pD, "pD")
            mm_group(p, [(pb[:], wdt_t[wi][:, k * 128:(k + 1) * 128],
                          ubf[:, k * tch:(k + 1) * tch], k == 0, k == 15)
                         for k in range(16)],
                     reads=[f"wdt_t{wi}", "ubf"], writes=[pbk], tag=f"dl{c}_{io}")
            ew(p, "vector", "tensor_scalar", xbt[:], pb[:], bsum[:, io:io + 1],
               None, ALU.add, reads=[pbk, "bsum"], writes=["xbt"],
               tag=f"da{c}_{io}")
            act(p, ebt[:], xbt[:], AF.Abs, reads=["xbt"], writes=["ebt"],
                tag=f"db{c}_{io}")
            act(p, ebt[:], ebt[:], AF.Exp, scale=-1.0, reads=["ebt"],
                writes=["ebt"], tag=f"dc{c}_{io}")
            act(p, ebt[:], ebt[:], AF.Ln, bias=1.0, reads=["ebt"],
                writes=["ebt"], tag=f"dd{c}_{io}")
            ew(p, "vector", "scalar_tensor_tensor", dtsb[:], xbt[:], 0.0,
               ebt[:], ALU.max, ALU.add, reads=["xbt", "ebt"], writes=["dtsb"],
               tag=f"de{c}_{io}")
            p.dma("sync", dt_r[:, io, s], dtsb[:], reads=["dtsb"],
                  tag=f"sdt{c}_{io}")

    p.barrier()
    bdA2 = [bdA[:], xbld[:].bitcast(F32)[:, 0:sh]]
    bdBu2 = [bdBu[:], xbld[:].bitcast(F32)[:, sh:2 * sh]]
    bS2 = [bS[:], ubf[:].bitcast(F32)[:, 0:sh]]
    for ci in range(NIC):
        p.dma("sync", bdt[:], dt_r[:, ci, :], writes=["bdt"], tag=f"bl{ci}")
        p.dma("sync", buu[:], uu_r[:, ci, :], writes=["buu"], tag=f"bu{ci}")
        ew(p, "vector", "tensor_mul", bdtu[:], bdt[:], buu[:],
           reads=["bdt", "buu"], writes=["bdtu"], tag=f"bm{ci}")
        ew(p, "vector", "reduce_sum", dtsum[:], bdt[:], axis=AX.X,
           reads=["bdt"], writes=["dtsum"], tag=f"bs{ci}")
        for n in range(D_STATE):
            col = ci * D_STATE + n
            nb = n % 2
            act(p, bdA2[nb], bdt[:], AF.Exp, scale=negA[:, col:col + 1],
                reads=["bdt", "negA"], writes=[f"bdA{nb}"], tag=f"bA{ci}_{n}")
            act(p, bdBu2[nb], bdtu[:], AF.Copy, scale=bpar[:, col:col + 1],
                reads=["bdtu", "bpar"], writes=[f"bdBu{nb}"], tag=f"bB{ci}_{n}")
            ew(p, "vector", "tensor_tensor_scan", bS2[nb], bdA2[nb], bdBu2[nb],
               0.0, ALU.mult, ALU.add,
               reads=[f"bdA{nb}", f"bdBu{nb}"], writes=[f"bS{nb}"],
               tag=f"bscan{ci}_{n}")
            ew(p, "vector", "tensor_copy", send[:, col:col + 1],
               bS2[nb][:, sh - 1:sh], reads=[f"bS{nb}"], writes=["send"],
               tag=f"bse{ci}_{n}")
            act(p, gend[:, col:col + 1], dtsum[:], AF.Exp,
                scale=negA[:, col:col + 1], reads=["dtsum", "negA"],
                writes=["gend"], tag=f"bg{ci}_{n}")
    p.dma("sync", o_send, send[:], reads=["send"], tag="stsend")
    p.dma("sync", o_gend, gend[:], reads=["gend"], tag="stgend")

    if env is None:
        p.final_wait("sync")
        p.build()
        p.finish()
        return nc



def build_l6(sh=SH, tch=512, env=None):
    nch = sh // tch
    NIC = D_INNER // 128
    if env is None:
        nc = bass.Bass("TRN2", target_bir_lowering=False)
        dt = nc.dram_tensor("dt", [D_INNER, sh], F32, kind="ExternalInput").ap()
        uu = nc.dram_tensor("uu", [D_INNER, sh], F32, kind="ExternalInput").ap()
        sg = nc.dram_tensor("sg", [D_INNER, sh], F32, kind="ExternalInput").ap()
        x2 = nc.dram_tensor("x2", [D, sh], F32, kind="ExternalInput").ap()
        send4 = nc.dram_tensor("send4", [4, 128, 256], F32, kind="ExternalInput").ap()
        gend4 = nc.dram_tensor("gend4", [4, 128, 256], F32, kind="ExternalInput").ap()
        sel4 = nc.dram_tensor("sel4", [128, 4], F32, kind="ExternalInput").ap()
        alogc = nc.dram_tensor("alogc", [128, 256], F32, kind="ExternalInput").ap()
        bparc = nc.dram_tensor("bparc", [128, 256], F32, kind="ExternalInput").ap()
        cparc = nc.dram_tensor("cparc", [128, 256], F32, kind="ExternalInput").ap()
        dparc = nc.dram_tensor("dparc", [128, 16], F32, kind="ExternalInput").ap()
        woutT = nc.dram_tensor("woutT", [D_INNER, D], F32, kind="ExternalInput").ap()
        o_x3 = nc.dram_tensor("x3", [D, sh], F32, kind="ExternalOutput").ap()

        p = Prog(nc)
    else:
        nc, p = env["nc"], env["p"]
        dt = env["dt"]
        uu = env["uu"]
        sg = env["sg"]
        x2 = env["x2"]
        send4 = env["send4"]
        gend4 = env["gend4"]
        sel4 = env["sel4"]
        alogc = env["alogc"]
        bparc = env["bparc"]
        cparc = env["cparc"]
        dparc = env["dparc"]
        woutT = env["woutT"]
        o_x3 = env["o_x3"]
    ygb_scr = nc.dram_tensor("ygb_scr", [D_INNER, sh], BF16, kind="Internal").ap()
    wob_scr = nc.dram_tensor("wob_scr6", [128, 8 * NIC * 128], BF16,
                             kind="Internal").ap()
    wot = [p.sb(f"wot6{i}", [128, NIC * 128], BF16) for i in range(2)]
    negA = p.sb("negA", [128, 256], F32)
    bpar = p.sb("bpar", [128, 256], F32)
    cpar = p.sb("cpar", [128, 256], F32)
    dpar = p.sb("dpar", [128, 16], F32)
    sgend = p.sb("sgend", [128, 2048], F32)
    sends = sgend[:][:, 0:1024]
    gends = sgend[:][:, 1024:2048]
    sel = p.sb("sel", [128, 4], F32)
    s0 = p.sb("s0", [128, 256], F32)
    Tst = p.sb("Tst", [128, 256], F32)
    bdt = p.sb("bdt", [128, sh], F32)
    buu = p.sb("buu", [128, sh], F32)
    bdtu = p.sb("bdtu", [128, sh], F32)
    bdA = p.sb("bdA", [128, sh], F32)
    bdBu = p.sb("bdBu", [128, sh], F32)
    bS = p.sb("bS", [128, sh], F32)
    yacc = p.sb("yacc", [128, sh], F32)
    ybt = sgend[:].bitcast(BF16)[:, 0:sh]
    bdA2_t = p.sb("bdA2t", [128, sh], F32)
    bdBu2_t = p.sb("bdBu2t", [128, sh], F32)
    wtmpf = p.sb("wtmpf", [128, D], F32)
    aux = make_ln_aux(p, "ln", tch)
    if sh >= 8 * tch:
        x2c = bdt[:][:, 0:8 * tch]
        ygc = buu[:].bitcast(BF16)[:, 0:NIC * tch]
        z1 = bdtu[:][:, 0:8 * tch]
        x3o = bdBu[:][:, 0:8 * tch]
    else:
        x2c = p.sb("x2c", [128, 8 * tch], F32)[:]
        ygc = p.sb("ygc", [128, NIC * tch], BF16)[:]
        z1 = p.sb("z1x", [128, 8 * tch], F32)[:]
        x3o = p.sb("x3ox", [128, 8 * tch], F32)[:]

    pC = p.ps("pC", [128, tch], F32)
    pD = p.ps("pD", [128, tch], F32)
    pA = p.ps("pA", [128, tch], F32)
    pB = p.ps("pB", [128, tch], F32)
    ps1 = p.ps("ps1", [1, tch], F32)
    ps2 = p.ps("ps2", [1, tch], F32)
    aux.update(ps1=ps1, ps2=ps2, pbm=pA, pba=pB, utile=bdA[:, 0:tch])
    aux["keys"].update(ps1="ps1", ps2="ps2", pbm="pA", pba="pB", utile="bdA")
    ec2, eck2 = p.constcol(EPS_LN_OUTER)
    aux["epscol"] = ec2
    aux["keys"]["epscol"] = eck2

    wtmpb6 = p.sb("wtmpb6", [128, D], BF16)
    for g in range(NIC):
        p.dma("sync", wtmpf[:],
              woutT.rearrange("(k p) m -> p k m", p=128)[:, g, :],
              writes=["wtmpf"], tag=f"wo{g}")
        act(p, wtmpb6[:], wtmpf[:], AF.Copy,
            reads=["wtmpf"], writes=["wtmpb6"], tag=f"woc{g}")
        p.dma("sync",
              wob_scr.rearrange("p (o k f) -> p o k f", k=NIC, f=128)[:, :, g, :],
              wtmpb6[:].rearrange("p (o f) -> p o f", f=128),
              reads=["wtmpb6"], writes=["wob_hbm"], tag=f"wos{g}")
    p.dma("sync", negA[:], alogc, writes=["negA0"], tag="lda")
    act(p, negA[:], negA[:], AF.Exp, reads=["negA0"], writes=["negA1"], tag="na1")
    ew(p, "vector", "tensor_scalar_mul", negA[:], negA[:], -1.0,
       reads=["negA1"], writes=["negA"], tag="na2")
    p.dma("sync", bpar[:], bparc, writes=["bpar"], tag="ldb")
    p.dma("sync", cpar[:], cparc, writes=["cpar"], tag="ldc")
    p.dma("sync", dpar[:], dparc, writes=["dpar"], tag="ldd")
    p.dma("sync", sends.rearrange("p (g t) -> p g t", g=4),
          send4.rearrange("g p t -> p g t"), writes=["sends"], tag="lds")
    p.dma("sync", gends.rearrange("p (g t) -> p g t", g=4),
          gend4.rearrange("g p t -> p g t"), writes=["gends"], tag="ldg")
    p.dma("sync", sel[:], sel4, writes=["sel"], tag="ldsel")
    ew(p, "vector", "memset", Tst[:], 0.0, writes=["Tst"], tag="T0")
    ew(p, "vector", "memset", s0[:], 0.0, writes=["s0"], tag="s00")
    for j in range(4):
        ew(p, "vector", "scalar_tensor_tensor", s0[:], Tst[:], sel[:, j:j + 1],
           s0[:], ALU.mult, ALU.add, reads=["Tst", "sel", "s0"], writes=["s0"],
           tag=f"s0a{j}")
        if j < 3:
            ew(p, "vector", "tensor_mul", Tst[:], Tst[:],
               gends[:, j * 256:(j + 1) * 256],
               reads=["Tst", "gends"], writes=["Tst"], tag=f"Tm{j}")
            ew(p, "vector", "tensor_add", Tst[:], Tst[:],
               sends[:, j * 256:(j + 1) * 256],
               reads=["Tst", "sends"], writes=["Tst"], tag=f"Ta{j}")
    p.barrier()
    dt_r = dt.rearrange("(i p) t -> p i t", p=128)
    uu_r = uu.rearrange("(i p) t -> p i t", p=128)
    sg_r = sg.rearrange("(i p) t -> p i t", p=128)
    yg_r = ygb_scr.rearrange("(i p) t -> p i t", p=128)
    x2_r = x2.rearrange("(k p) t -> p k t", p=128)
    x3_r = o_x3.rearrange("(k p) t -> p k t", p=128)

    bdA6 = [bdA[:], bdA2_t[:]]
    bdBu6 = [bdBu[:], bdBu2_t[:]]
    for ci in range(NIC):
        p.dma("sync", bdt[:], dt_r[:, ci, :], writes=["bdt"], tag=f"bl{ci}")
        p.dma("sync", buu[:], uu_r[:, ci, :], writes=["buu"], tag=f"bu{ci}")
        ew(p, "vector", "tensor_mul", bdtu[:], bdt[:], buu[:],
           reads=["bdt", "buu"], writes=["bdtu"], tag=f"bm{ci}")
        for n in range(D_STATE):
            col = ci * D_STATE + n
            nb = n % 2
            act(p, bdA6[nb], bdt[:], AF.Exp, scale=negA[:, col:col + 1],
                reads=["bdt", "negA"], writes=[f"bdA{nb}"], tag=f"bA{ci}_{n}")
            act(p, bdBu6[nb], bdtu[:], AF.Copy, scale=bpar[:, col:col + 1],
                reads=["bdtu", "bpar"], writes=[f"bdBu{nb}"], tag=f"bB{ci}_{n}")
            ew(p, "vector", "tensor_tensor_scan", bS[:], bdA6[nb], bdBu6[nb],
               s0[:, col:col + 1], ALU.mult, ALU.add,
               reads=[f"bdA{nb}", f"bdBu{nb}", "s0"], writes=["bS"],
               tag=f"bsc{ci}_{n}")
            if n == 0:
                ew(p, "vector", "tensor_scalar_mul", yacc[:], bS[:],
                   cpar[:, col:col + 1], reads=["bS", "cpar"], writes=["yacc"],
                   tag=f"by{ci}_{n}")
            else:
                ew(p, "vector", "scalar_tensor_tensor", yacc[:], bS[:],
                   cpar[:, col:col + 1], yacc[:], ALU.mult, ALU.add,
                   reads=["bS", "cpar", "yacc"], writes=["yacc"],
                   tag=f"by{ci}_{n}")
        ew(p, "vector", "scalar_tensor_tensor", yacc[:], buu[:],
           dpar[:, ci:ci + 1],
           yacc[:], ALU.mult, ALU.add, reads=["buu", "dpar", "yacc"],
           writes=["yacc"], tag=f"bd{ci}")
        p.dma("sync", bdA[:], sg_r[:, ci, :], writes=["bdA0"], tag=f"bg{ci}")
        ew(p, "vector", "tensor_mul", ybt, yacc[:], bdA[:],
           reads=["yacc", "bdA0"], writes=["ybt"], tag=f"byg{ci}")
        p.dma("sync", yg_r[:, ci, :], ybt, reads=["ybt"], writes=["yg_hbm"],
              tag=f"sy{ci}")

    for c in range(nch):
        s = slice(c * tch, (c + 1) * tch)
        p.dma("sync", x2c, x2_r[:, :, s], writes=["bdt"], tag=f"lx{c}")
        p.dma("sync", ygc.rearrange("p (i t) -> p i t", t=tch),
              yg_r[:, :, s], reads=["yg_hbm"], writes=["buu"], tag=f"ly{c}")
        for oc in range(8):
            wi = oc % 2
            p.dma("gpsimd", wot[wi][:],
                  wob_scr[:, oc * NIC * 128:(oc + 1) * NIC * 128],
                  reads=["wob_hbm"], writes=[f"wot{wi}"], tag=f"lwo{c}_{oc}")
            pb, pbk = (pC, "pC") if oc % 2 == 0 else (pD, "pD")
            mm_group(p, [(pb[:], wot[wi][:, k * 128:(k + 1) * 128],
                          ygc[:, k * tch:(k + 1) * tch], k == 0, k == NIC - 1)
                         for k in range(NIC)],
                     reads=[f"wot{wi}", "buu"], writes=[pbk], tag=f"op{c}_{oc}")
            ew(p, "vector", "tensor_add", z1[:, oc * tch:(oc + 1) * tch],
               x2c[:, oc * tch:(oc + 1) * tch], pb[:],
               reads=["bdt", pbk], writes=["bdtu"], tag=f"z{c}_{oc}")
        ln_fm(p, f"lno_{c}", z1, "bdtu", x3o, "bdBu", aux, EPS_LN_OUTER)
        p.dma("sync", x3_r[:, :, s], x3o, reads=["bdBu"], tag=f"sx3{c}")

    if env is None:
        p.final_wait("sync")
        p.build()
        p.finish()
        return nc




def build_l7(sh=SH, tch=512, env=None):
    SC = 1024
    nsc = sh // SC
    nc5 = SC // tch
    NE = 9
    if env is None:
        nc = bass.Bass("TRN2", target_bir_lowering=False)
        x3 = nc.dram_tensor("x3", [D, sh], F32, kind="ExternalInput").ap()
        wg = nc.dram_tensor("wg", [D, 8], F32, kind="ExternalInput").ap()
        we1f = nc.dram_tensor("we1f", [NE * D, FF_DIM], F32, kind="ExternalInput").ap()
        we3f = nc.dram_tensor("we3f", [NE * D, FF_DIM], F32, kind="ExternalInput").ap()
        we2f = nc.dram_tensor("we2f", [NE * FF_DIM, D], F32, kind="ExternalInput").ap()
        o_y = nc.dram_tensor("y", [D, sh], F32, kind="ExternalOutput").ap()

        p = Prog(nc)
    else:
        nc, p = env["nc"], env["p"]
        x3 = env["x3"]
        wg = env["wg"]
        we1f = env["we1f"]
        we3f = env["we3f"]
        we2f = env["we2f"]
        o_y = env["o_y"]
    we1b = nc.dram_tensor("we1b", [128, NE * 8 * FF_DIM], BF16, kind="Internal").ap()
    we3b = nc.dram_tensor("we3b", [128, NE * 8 * FF_DIM], BF16, kind="Internal").ap()
    we2b = nc.dram_tensor("we2b", [128, NE * 16 * D], BF16, kind="Internal").ap()
    wgb = p.sb("wgb", [128, 8 * 8], BF16)
    identf = p.sb("identf", [128, 128], F32)
    w1t3 = [p.sb(f"w1t3{i}", [128, 8 * 128], BF16) for i in range(2)]
    w3t3 = [p.sb(f"w3t3{i}", [128, 8 * 128], BF16) for i in range(2)]
    w2t2 = [p.sb(f"w2t2{i}", [128, 16 * 128], BF16) for i in range(2)]
    racc = p.sb("racc", [128, 8 * SC], F32)
    x3b = p.sb("x3b", [128, 8 * SC], BF16)
    x3f = p.sb("x3f", [128, 8 * tch], F32)
    hwb = p.sb("hwb", [128, 16 * tch], BF16)
    stmp = p.sb("stmp", [128, tch], F32)
    stmp2 = [p.sb(f"stmp2{i}", [128, tch], F32) for i in range(2)]
    g_fm = p.sb("g_fm", [8, SC], F32)
    w_fmt = p.sb("w_fmt", [8, SC], F32)
    g_tm = p.sb("g_tm", [128, 8], F32)
    top8 = p.sb("top8", [128, 8], F32)
    rep = p.sb("rep", [128, 8], F32)
    gz = p.sb("gz", [128, 8], F32)
    w_tm = p.sb("w_tm", [128, 8], F32)
    s2c = p.sb("s2c", [128, 1], F32)
    wbs = p.sb("wbs", [128, tch], F32)
    aux = make_ln_aux(p, "ln", tch)
    z1 = hwb[:].bitcast(F32)
    wtmpf2 = [racc[:].bitcast(F32)[:, 0:FF_DIM],
              racc[:].bitcast(F32)[:, FF_DIM:2 * FF_DIM]]
    wtmpb2 = [x3b[:][:, 0:FF_DIM], x3b[:][:, FF_DIM:2 * FF_DIM]]
    wtmpf = wtmpf2[0]
    wtmpb = wtmpb2[0]
    yo = x3f

    pC = p.ps("pC", [128, tch], F32)
    pD = p.ps("pD", [128, tch], F32)
    pE = p.ps("pE", [128, tch], F32)
    pA = p.ps("pA", [128, tch], F32)
    pB = p.ps("pB", [128, tch], F32)
    ps1 = p.ps("ps1", [1, tch], F32)
    ps2 = p.ps("ps2", [1, tch], F32)
    pg = p.ps("pg", [16, tch], F32)
    aux.update(ps1=ps1, ps2=ps2, pbm=pA, pba=pB, utile=stmp)
    aux["keys"].update(ps1="ps1", ps2="ps2", pbm="pA", pba="pB", utile="stmp")
    ec2, eck2 = p.constcol(EPS_LN_OUTER)
    aux["epscol"] = ec2
    aux["keys"]["epscol"] = eck2

    make_identity(p, identf[:], "identf")
    esel = p.sb("esel", [8, 8 * 128], F32)
    p.emit("gpsimd", lambda e: e.memset(esel[:], 0.0), writes=["esel"], tag="es0")
    p.emit("gpsimd", lambda e: e.affine_select(
        out=esel[:].rearrange("j (e q) -> j e q", q=128),
        in_=esel[:].rearrange("j (e q) -> j e q", q=128),
        compare_op=ALU.not_equal, fill=1.0, base=0,
        pattern=[[-1, 8], [0, 128]], channel_multiplier=1),
        reads=["esel"], writes=["esel"], tag="es1")
    p.dma("sync", wtmpf[:, 0:64].rearrange("p (k e) -> p k e", e=8),
          wg.rearrange("(k p) e -> p k e", p=128), writes=["wtmpf0"], tag="lwg")
    act(p, wgb[:], wtmpf[:, 0:64], AF.Copy, reads=["wtmpf0"], writes=["wgb"],
        tag="cwg")
    pcnt = 0
    for (srcT, dstT, key) in [(we1f, we1b, "we1_hbm"), (we3f, we3b, "we3_hbm")]:
        for e in range(NE):
            for k in range(8):
                pi = pcnt % 2
                pcnt += 1
                p.dma("sync", wtmpf2[pi],
                      srcT.rearrange("(q p) m -> p q m", p=128)[:, e * 8 + k, :],
                      writes=[f"wtmpf{pi}"], tag=f"pl{key}{e}_{k}")
                act(p, wtmpb2[pi], wtmpf2[pi], AF.Copy,
                    reads=[f"wtmpf{pi}"], writes=[f"wtmpb{pi}"],
                    tag=f"pc{key}{e}_{k}")
                p.dma("sync",
                      dstT.rearrange("p (q g kk f) -> p q g kk f", g=16, kk=8,
                                     f=128)[:, e, :, k, :],
                      wtmpb2[pi].rearrange("p (g f) -> p g f", f=128),
                      reads=[f"wtmpb{pi}"], writes=[key], tag=f"ps{key}{e}_{k}")
    for e in range(NE):
        for k in range(16):
            pi = pcnt % 2
            pcnt += 1
            p.dma("sync", wtmpf2[pi][:, 0:D],
                  we2f.rearrange("(q p) m -> p q m", p=128)[:, e * 16 + k, :],
                  writes=[f"wtmpf{pi}"], tag=f"plw2{e}_{k}")
            act(p, wtmpb2[pi][:, 0:D], wtmpf2[pi][:, 0:D], AF.Copy,
                reads=[f"wtmpf{pi}"], writes=[f"wtmpb{pi}"], tag=f"pcw2{e}_{k}")
            p.dma("sync",
                  we2b.rearrange("p (q o kk f) -> p q o kk f", o=8, kk=16,
                                 f=128)[:, e, :, k, :],
                  wtmpb2[pi][:, 0:D].rearrange("p (o f) -> p o f", f=128),
                  reads=[f"wtmpb{pi}"], writes=["we2_hbm"], tag=f"psw2{e}_{k}")
    for pk in ("wtmpf0", "wtmpf1"):
        if pk in p.res:
            p.res.setdefault("racc", {"w": None, "r": []})["r"].extend(
                p.res[pk]["r"] + ([p.res[pk]["w"]] if p.res[pk]["w"] else []))
    for pk in ("wtmpb0", "wtmpb1"):
        if pk in p.res:
            p.res.setdefault("x3b", {"w": None, "r": []})["r"].extend(
                p.res[pk]["r"] + ([p.res[pk]["w"]] if p.res[pk]["w"] else []))

    x3_r = x3.rearrange("(k p) t -> p k t", p=128)
    y_r = o_y.rearrange("(k p) t -> p k t", p=128)

    for sc in range(nsc):
        scs = slice(sc * SC, (sc + 1) * SC)
        for c5 in range(nc5):
            s5 = slice(sc * SC + c5 * tch, sc * SC + (c5 + 1) * tch)
            p.dma("sync", x3f[:], x3_r[:, :, s5], writes=["x3f"], tag=f"lx{sc}_{c5}")
            act(p, x3b[:].rearrange("p (k t) -> p k t", t=SC)[:, :, c5 * tch:(c5 + 1) * tch],
                x3f[:].rearrange("p (k t) -> p k t", t=tch), AF.Copy,
                reads=["x3f"], writes=["x3b"], tag=f"cx{sc}_{c5}")
        for c5 in range(nc5):
            trip = [(pg[0:8, :], wgb[:, k * 8:(k + 1) * 8],
                     x3b[:].rearrange("p (k t) -> p k t", t=SC)[:, k, c5 * tch:(c5 + 1) * tch],
                     k == 0, k == 7) for k in range(8)]
            mm_group(p, trip, reads=["x3b", "wgb"], writes=["pg"],
                     tag=f"gm{sc}_{c5}")
            act(p, g_fm[:, c5 * tch:(c5 + 1) * tch], pg[0:8, :], AF.Sigmoid,
                reads=["pg"], writes=["g_fm"], tag=f"gs{sc}_{c5}")
        for tb in range(SC // 128):
            transpose(p, pC[0:128, 0:8], g_fm[:, tb * 128:(tb + 1) * 128],
                      identf[0:8, 0:8], reads=["g_fm", "identf"], writes=["pC"],
                      tag=f"gt{sc}_{tb}")
            ew(p, "vector", "tensor_copy", g_tm[:], pC[0:128, 0:8],
               reads=["pC"], writes=["g_tm"], tag=f"ge{sc}_{tb}")
            ew(p, "vector", "max", out=top8[:], in_=g_tm[:],
               reads=["g_tm"], writes=["top8"], tag=f"gx{sc}_{tb}")
            ew(p, "vector", "tensor_copy", rep[:, 0:2], top8[:, 0:2],
               reads=["top8"], writes=["rep"], tag=f"gr{sc}_{tb}")
            ew(p, "vector", "memset", rep[:, 2:8], -1e30,
               reads=[], writes=["rep2"], deps=[p.res["rep"]["w"]],
               tag=f"gm2{sc}_{tb}")
            ew(p, "vector", "match_replace", gz[:], rep[:], g_tm[:], 0.0,
               reads=["rep", "rep2", "g_tm"], writes=["gz"], tag=f"gmr{sc}_{tb}")
            ew(p, "vector", "tensor_sub", w_tm[:], g_tm[:], gz[:],
               reads=["g_tm", "gz"], writes=["w_tm"], tag=f"gw{sc}_{tb}")
            ew(p, "vector", "tensor_add", s2c[:], top8[:, 0:1], top8[:, 1:2],
               reads=["top8"], writes=["s2c"], tag=f"g2{sc}_{tb}")
            ew(p, "vector", "reciprocal", s2c[:], s2c[:],
               reads=["s2c"], writes=["s2c"], tag=f"g3{sc}_{tb}")
            ew(p, "vector", "tensor_scalar_mul", w_tm[:], w_tm[:], s2c[:, 0:1],
               reads=["w_tm", "s2c"], writes=["w_tm"], tag=f"g4{sc}_{tb}")
            transpose(p, pC[0:8, 0:128], w_tm[:], identf[:],
                      reads=["w_tm", "identf"], writes=["pC"], tag=f"g5{sc}_{tb}")
            ew(p, "vector", "tensor_copy", w_fmt[:, tb * 128:(tb + 1) * 128],
               pC[0:8, 0:128], reads=["pC"], writes=["w_fmt"], tag=f"g6{sc}_{tb}")
        for e in range(NE):
            for c5 in range(nc5):
                if e < 8:
                    mm_group(p, [(pB[:], esel[:, e * 128:(e + 1) * 128],
                                  w_fmt[:, c5 * tch:(c5 + 1) * tch],
                                  True, True)],
                             reads=["w_fmt", "esel"], writes=["pB"],
                             tag=f"wb{sc}_{e}_{c5}")
                    ew(p, "vector", "tensor_copy", wbs[:], pB[:], reads=["pB"],
                       writes=["wbs"], tag=f"wbe{sc}_{e}_{c5}")
                for fcg in range(16):
                    wi = fcg % 2
                    tb0 = (e * 16 + fcg) * 1024
                    p.dma("sync", w1t3[wi][:], we1b[:, tb0:tb0 + 1024],
                          reads=["we1_hbm"], writes=[f"w1t3{wi}"],
                          tag=f"l1{sc}_{e}_{c5}_{fcg}")
                    p.dma("gpsimd", w3t3[wi][:], we3b[:, tb0:tb0 + 1024],
                          reads=["we3_hbm"], writes=[f"w3t3{wi}"],
                          tag=f"l3{sc}_{e}_{c5}_{fcg}")
                    pb1, pk1 = (pC, "pC")
                    pb3, pk3 = (pD, "pD")
                    mm_group(p, [(pb1[:], w1t3[wi][:, k * 128:(k + 1) * 128],
                                  x3b[:].rearrange("p (k t) -> p k t", t=SC)
                                  [:, k, c5 * tch:(c5 + 1) * tch], k == 0, k == 7)
                                 for k in range(8)],
                             reads=[f"w1t3{wi}", "x3b"], writes=[pk1],
                             tag=f"e1{sc}_{e}_{c5}_{fcg}")
                    mm_group(p, [(pb3[:], w3t3[wi][:, k * 128:(k + 1) * 128],
                                  x3b[:].rearrange("p (k t) -> p k t", t=SC)
                                  [:, k, c5 * tch:(c5 + 1) * tch], k == 0, k == 7)
                                 for k in range(8)],
                             reads=[f"w3t3{wi}", "x3b"], writes=[pk3],
                             tag=f"e3{sc}_{e}_{c5}_{fcg}")
                    st_ = stmp2[fcg % 2]
                    stk = f"stmp2{fcg % 2}"
                    act(p, st_[:], pb1[:], AF.Silu, reads=[pk1],
                        writes=[stk], tag=f"sl{sc}_{e}_{c5}_{fcg}")
                    if e < 8:
                        ew(p, "vector", "tensor_mul", st_[:], st_[:], pb3[:],
                           reads=[stk, pk3], writes=[stk],
                           tag=f"h1{sc}_{e}_{c5}_{fcg}")
                        ew(p, "vector", "tensor_mul",
                           hwb[:, fcg * tch:(fcg + 1) * tch], st_[:], wbs[:],
                           reads=[stk, "wbs"], writes=["hwb", f"hwb_{fcg}"],
                           tag=f"h2{sc}_{e}_{c5}_{fcg}")
                    else:
                        ew(p, "vector", "tensor_mul",
                           hwb[:, fcg * tch:(fcg + 1) * tch], st_[:], pb3[:],
                           reads=[stk, pk3], writes=["hwb", f"hwb_{fcg}"],
                           tag=f"h2{sc}_{e}_{c5}_{fcg}")
                for oc in range(8):
                    wi2 = oc % 2
                    t20 = (e * 8 + oc) * 16 * 128
                    p.dma("sync", w2t2[wi2][:], we2b[:, t20:t20 + 16 * 128],
                          reads=["we2_hbm"], writes=[f"w2t2{wi2}"],
                          tag=f"lw2{sc}_{e}_{c5}_{oc}")
                    pb, pbk = (pE, "pE") if oc % 2 == 0 else (pA, "pA")
                    mm_group(p, [(pb[:], w2t2[wi2][:, k * 128:(k + 1) * 128],
                                  hwb[:, k * tch:(k + 1) * tch], k == 0, k == 15)
                                 for k in range(16)],
                             reads=[f"w2t2{wi2}", "hwb"], writes=[pbk],
                             tag=f"e2{sc}_{e}_{c5}_{oc}")
                    rsl = racc[:, oc * SC + c5 * tch:oc * SC + (c5 + 1) * tch]
                    if e == 0:
                        ew(p, "vector", "tensor_copy", rsl, pb[:],
                           reads=[pbk], writes=["racc"], tag=f"r0{sc}_{e}_{c5}_{oc}")
                    else:
                        ew(p, "vector", "tensor_add", rsl, rsl, pb[:],
                           reads=[pbk, "racc"], writes=["racc"],
                           tag=f"ra{sc}_{e}_{c5}_{oc}")
        for c5 in range(nc5):
            s5 = slice(sc * SC + c5 * tch, sc * SC + (c5 + 1) * tch)
            p.dma("sync", x3f[:], x3_r[:, :, s5], writes=["x3f"], tag=f"fx{sc}_{c5}")
            for oc in range(8):
                ew(p, "vector", "scalar_tensor_tensor",
                   z1[:, oc * tch:(oc + 1) * tch],
                   racc[:, oc * SC + c5 * tch:oc * SC + (c5 + 1) * tch],
                   1.0 / 3.0, x3f[:, oc * tch:(oc + 1) * tch],
                   ALU.mult, ALU.add, reads=["racc", "x3f"], writes=["hwb"],
                   tag=f"fz{sc}_{c5}_{oc}")
            ln_fm(p, f"lno_{sc}_{c5}", z1, "hwb", yo, "x3f", aux, EPS_LN_OUTER)
            p.dma("sync", y_r[:, :, s5], yo[:], reads=["x3f"], tag=f"sy{sc}_{c5}")

    if env is None:
        p.final_wait("sync")
        p.build()
        p.finish()
        return nc



_NC_CACHE = {}


def _get_nc(name, builder):
    if name not in _NC_CACHE:
        _NC_CACHE[name] = builder()
    return _NC_CACHE[name]


def _rsh(v, n):
    return np.ascontiguousarray(np.asarray(v).reshape(n, 128).T)


def _rsh2(m):
    k = m.shape[1]
    return np.ascontiguousarray(
        m.reshape(16, 128, k).transpose(1, 0, 2).reshape(128, 16 * k))


def _fm3(cols):
    n = cols.shape[1]
    return np.ascontiguousarray(
        cols.reshape(8, 128, n).transpose(1, 0, 2).reshape(128, 8 * n))


def kernel(**inputs):
    return kernel_v3(**inputs)


def kernel_unfused(**inputs):
    ii = inputs
    x_full = np.asarray(ii["L0_attn_x"])
    shards = []
    for b in range(2):
        for q in range(4):
            shards.append(np.ascontiguousarray(
                x_full[b, q * SH:(q + 1) * SH, :].T))

    wq = np.ascontiguousarray(ii["L0_attn_Wq"].reshape(D, D))
    wk = np.ascontiguousarray(ii["L0_attn_Wk"].reshape(D, D))
    wv = np.ascontiguousarray(ii["L0_attn_Wv"].reshape(D, D))
    projT = np.ascontiguousarray(ii["L0_attn_proj"].T)
    nc1 = _get_nc("l1", build_l1)
    r1 = run_bass_kernel_spmd(nc1, [dict(x=shards[c], wq=wq, wk=wk, wv=wv,
                                         projT=projT) for c in range(8)],
                              core_ids=list(range(8))).results

    mxall = np.stack([r1[c]["mxloc"][0] for c in range(8)]).astype(np.float32)
    nc2 = _get_nc("l2", build_l2)
    r2 = run_bass_kernel_spmd(nc2, [dict(ddk=r1[c]["ddk"], vv=r1[c]["vv"],
                                         mxall=mxall) for c in range(8)],
                              core_ids=list(range(8))).results

    wo = np.ascontiguousarray(ii["L0_attn_Wo"].reshape(D, D))
    nc3 = _get_nc("l3", build_l3)
    im3 = []
    for c in range(8):
        g0 = (c // 4) * 4
        im3.append(dict(
            x=shards[c], qf=r1[c]["qf"],
            kvp4=np.stack([r2[g0 + j]["kvp"] for j in range(4)]),
            kfsp4=np.stack([r2[g0 + j]["kfsp"] for j in range(4)]),
            vsp4=np.stack([r2[g0 + j]["vsp"] for j in range(4)]),
            wo=wo, w1=np.asarray(ii["L0_attn_W1"]), w2=np.asarray(ii["L0_attn_W2"]),
            b1c=_rsh(ii["L0_attn_b1"], 32), b2c=_rsh(ii["L0_attn_b2"], 8),
            ln1g=_rsh(ii["L0_attn_ln1_g"], 8), ln1b=_rsh(ii["L0_attn_ln1_b"], 8),
            ln2g=_rsh(ii["L0_attn_ln2_g"], 8), ln2b=_rsh(ii["L0_attn_ln2_b"], 8)))
    r3 = run_bass_kernel_spmd(nc3, im3, core_ids=list(range(8))).results

    nc4 = _get_nc("l4", build_l4)
    wrT = np.ascontiguousarray(np.asarray(ii["L0_ffn_W_r"]).T)
    wkT4 = np.ascontiguousarray(np.asarray(ii["L0_ffn_W_k"]).T)
    wvT4 = np.ascontiguousarray(np.asarray(ii["L0_ffn_W_v"]).T)
    murc = _rsh(ii["L0_ffn_mu_r"], 8)
    mukc = _rsh(ii["L0_ffn_mu_k"], 8)
    im4 = []
    for c in range(8):
        if c % 4 == 0:
            xprev = np.zeros((128, 8), np.float32)
        else:
            xprev = _fm3(r3[c - 1]["x1"][:, SH - 1:SH])[:, :]
        im4.append(dict(x1=r3[c]["x1"], xprev=np.ascontiguousarray(xprev),
                        wrT=wrT, wkT=wkT4, wvT=wvT4, murc=murc, mukc=mukc))
    r4 = run_bass_kernel_spmd(nc4, im4, core_ids=list(range(8))).results

    nc5 = _get_nc("l5", build_l5)
    winT = np.ascontiguousarray(np.asarray(ii["L1_attn_W_in"]).T)
    wdtT = np.ascontiguousarray(np.asarray(ii["L1_attn_W_dt"]).T)
    cwc = _rsh2(np.asarray(ii["L1_attn_conv_w"]))
    cbc = _rsh(ii["L1_attn_conv_b"], 16)
    bdtc = _rsh(ii["L1_attn_b_dt"], 16)
    dtbc = _rsh(ii["L1_attn_dt_bias"], 16)
    alogc = _rsh2(np.asarray(ii["L1_attn_A_log"]))
    bparc = _rsh2(np.asarray(ii["L1_attn_B_param"]))
    im5 = []
    for c in range(8):
        if c % 4 == 0:
            p3 = np.zeros((128, 24), np.float32)
        else:
            p3 = _fm3(r4[c - 1]["x2"][:, SH - 3:SH]).reshape(128, 24)
        im5.append(dict(x2=r4[c]["x2"], x2p3=np.ascontiguousarray(p3),
                        winT=winT, wdtT=wdtT, cwc=cwc, cbc=cbc, bdtc=bdtc,
                        dtbc=dtbc, alogc=alogc, bparc=bparc))
    r5 = run_bass_kernel_spmd(nc5, im5, core_ids=list(range(8))).results

    nc6 = _get_nc("l6", build_l6)
    cparc = _rsh2(np.asarray(ii["L1_attn_C_param"]))
    dparc = _rsh(ii["L1_attn_D_param"], 16)
    woutT = np.ascontiguousarray(np.asarray(ii["L1_attn_W_out"]).T)
    im6 = []
    for c in range(8):
        g0 = (c // 4) * 4
        send4 = np.stack([r5[g0 + j]["send"] for j in range(4)])
        gend4 = np.stack([r5[g0 + j]["gend"] for j in range(4)])
        sel = np.zeros((128, 4), np.float32)
        sel[:, c % 4] = 1.0
        im6.append(dict(dt=r5[c]["dt"], uu=r5[c]["uu"], sg=r5[c]["sg"],
                        x2=r4[c]["x2"], send4=send4, gend4=gend4, sel4=sel,
                        alogc=alogc, bparc=bparc, cparc=cparc, dparc=dparc,
                        woutT=woutT))
    r6 = run_bass_kernel_spmd(nc6, im6, core_ids=list(range(8))).results

    nc7 = _get_nc("l7", build_l7)
    we1f = np.ascontiguousarray(np.concatenate(
        [np.asarray(ii["L1_ffn_We1"]).reshape(8 * D, FF_DIM),
         np.asarray(ii["L1_ffn_Ws1"])], 0))
    we3f = np.ascontiguousarray(np.concatenate(
        [np.asarray(ii["L1_ffn_We3"]).reshape(8 * D, FF_DIM),
         np.asarray(ii["L1_ffn_Ws3"])], 0))
    we2f = np.ascontiguousarray(np.concatenate(
        [np.asarray(ii["L1_ffn_We2"]).reshape(8 * FF_DIM, D),
         np.asarray(ii["L1_ffn_Ws2"])], 0))
    wg7 = np.asarray(ii["L1_ffn_Wg"])
    im7 = [dict(x3=r6[c]["x3"], wg=wg7, we1f=we1f, we3f=we3f, we2f=we2f)
           for c in range(8)]
    r7 = run_bass_kernel_spmd(nc7, im7, core_ids=list(range(8))).results

    out = np.empty((2, N, D), np.float32)
    for b in range(2):
        for q in range(4):
            out[b, q * SH:(q + 1) * SH, :] = r7[b * 4 + q]["y"].T
    return out



def build_fused(sh=SH, tch=512):
    nc = bass.Bass("TRN2", target_bir_lowering=False)
    EIn = lambda nm, shp, dt=F32: nc.dram_tensor(nm, shp, dt, kind="ExternalInput").ap()
    Int = lambda nm, shp, dt=F32: nc.dram_tensor(nm, shp, dt, kind="Internal").ap()
    io = {}
    io["x"] = EIn("x", [D, sh])
    io["wq"] = EIn("wq", [D, D]); io["wk"] = EIn("wk", [D, D])
    io["wv"] = EIn("wv", [D, D]); io["projT"] = EIn("projT", [DH, M])
    io["wo"] = EIn("wo", [D, D]); io["w1"] = EIn("w1", [D, DFF])
    io["w2"] = EIn("w2", [DFF, D]); io["b1c"] = EIn("b1c", [128, 32])
    io["b2c"] = EIn("b2c", [128, 8])
    io["lng"] = {nm: EIn(nm, [128, 8]) for nm in ("ln1g", "ln1b", "ln2g", "ln2b")}
    io["wrT"] = EIn("wrT", [D, D]); io["wkT"] = EIn("wkT", [D, D])
    io["wvT"] = EIn("wvT", [D, D]); io["murc"] = EIn("murc", [128, 8])
    io["mukc"] = EIn("mukc", [128, 8])
    io["winT"] = EIn("winT", [D, 2 * D_INNER])
    io["wdtT"] = EIn("wdtT", [D_INNER, D_INNER])
    io["cwc"] = EIn("cwc", [128, 64]); io["cbc"] = EIn("cbc", [128, 16])
    io["bdtc"] = EIn("bdtc", [128, 16]); io["dtbc"] = EIn("dtbc", [128, 16])
    io["alogc"] = EIn("alogc", [128, 256]); io["bparc"] = EIn("bparc", [128, 256])
    io["cparc"] = EIn("cparc", [128, 256]); io["dparc"] = EIn("dparc", [128, 16])
    io["woutT"] = EIn("woutT", [D_INNER, D])
    io["wg"] = EIn("wg", [D, 8])
    io["we1f"] = EIn("we1f", [9 * D, FF_DIM])
    io["we3f"] = EIn("we3f", [9 * D, FF_DIM])
    io["we2f"] = EIn("we2f", [9 * FF_DIM, D])
    io["sel1"] = EIn("sel1", [128, 4])
    io["sel3"] = EIn("sel3", [128, 4])
    io["sel4"] = EIn("sel4", [128, 4])
    io["o_y"] = nc.dram_tensor("y", [D, sh], F32, kind="ExternalOutput").ap()

    io["qf"] = Int("qf_i", [D, sh], BF16)
    io["ddk"] = Int("ddk_i", [D, sh])
    io["vv"] = Int("vv_i", [D, sh], BF16)
    mxl = Int("mxl_i", [1, 1])
    mxg = Int("mxg_i", [1, 1])
    mxall = Int("mxall_i", [8, 1])
    kvp = Int("kvp_i", [128, 512]); kvg = Int("kvg_i", [128, 512])
    kfsp = Int("kfsp_i", [128, 8]); kfsg = Int("kfsg_i", [128, 8])
    vsp = Int("vsp_i", [1, 1024]); vsg = Int("vsg_i", [1, 1024])
    kvp4 = Int("kvp4_i", [4, 128, 512])
    kfsp4 = Int("kfsp4_i", [4, 128, 8])
    vsp4 = Int("vsp4_i", [4, 1, 1024])
    x1 = Int("x1_i", [D, sh])
    x1last = Int("x1last_i", [128, 8]); x1g = Int("x1g_i", [4 * 128, 8])
    xprev = Int("xprev_i", [128, 8])
    x2 = Int("x2_i", [D, sh])
    x2l3 = Int("x2l3_i", [128, 24]); x2g = Int("x2g_i", [4 * 128, 24])
    x2p3 = Int("x2p3_i", [128, 24])
    dtD = Int("dt_i", [D_INNER, sh]); uuD = Int("uu_i", [D_INNER, sh])
    sgD = Int("sg_i", [D_INNER, sh])
    sendl = Int("sendl_i", [128, 256]); gendl = Int("gendl_i", [128, 256])
    sendg = Int("sendg_i", [4 * 128, 256]); gendg = Int("gendg_i", [4 * 128, 256])
    x3 = Int("x3_i", [D, sh])

    p = Prog(nc)
    env = {"nc": nc, "p": p}

    def coll(kind, op, groups, ins, outs, rkeys, wkeys, tag):
        return p.emit("gpsimd", lambda e: e.collective_compute(
            kind, op, replica_groups=groups, ins=[a.opt() for a in ins],
            outs=[a.opt() for a in outs]),
            reads=rkeys, writes=wkeys, inc=16, tag=tag)

    G2 = [[0, 1, 2, 3], [4, 5, 6, 7]]
    G1 = [list(range(8))]

    with p.phase():
        e = dict(env, x=io["x"], wq=io["wq"], wk=io["wk"], wv=io["wv"],
                 projT=io["projT"], o_qf=io["qf"], o_ddk=io["ddk"],
                 o_v=io["vv"], o_mx=mxl)
        build_l1(sh=sh, tch=tch, env=e)
    p.barrier()
    coll("AllReduce", ALU.max, G1, [mxl], [mxg], ["mxl"], ["mxg"], "cmx")
    with p.phase():
        t = p.sb("mxt", [1, 1], F32)
        p.dma("sync", t[:], mxg, reads=["mxg"], writes=["mxt"], tag="gmx")
        for j in range(8):
            p.dma("sync", mxall[j:j + 1, :], t[:], reads=["mxt"],
                  writes=["mxall"], tag=f"smx{j}")
    p.barrier()
    with p.phase():
        e = dict(env, ddk=io["ddk"], vv=io["vv"], mxall=mxall,
                 o_kv=kvp, o_kfs=kfsp, o_vs=vsp)
        build_l2(sh=sh, tch=tch, env=e)
    p.barrier()
    coll("AllReduce", ALU.add, G2, [kvp], [kvg], ["kvp_d"], ["kvg"], "ckv")
    coll("AllReduce", ALU.add, G2, [kfsp], [kfsg], ["kfsp_d"], ["kfsg"], "ckf")
    coll("AllReduce", ALU.add, G2, [vsp], [vsg], ["vsp_d"], ["vsg"], "cvs")
    with p.phase():
        zt = p.sb("zt", [128, 512], F32)
        tt = p.sb("tt", [128, 512], F32)
        ew(p, "vector", "memset", zt[:], 0.0, writes=["zt"], tag="z")
        p.dma("sync", tt[:], kvg, reads=["kvg"], writes=["tt"], tag="gkv")
        p.dma("sync", kvp4[0], tt[:], reads=["tt"], writes=["kvp4"], tag="skv0")
        for j in range(1, 4):
            p.dma("sync", kvp4[j], zt[:], reads=["zt"], writes=["kvp4"],
                  tag=f"skvz{j}")
        p.dma("sync", tt[:, 0:8], kfsg, reads=["kfsg"], writes=["tt2"], tag="gkf")
        p.dma("sync", kfsp4[0], tt[:, 0:8], reads=["tt2"], writes=["kfsp4"],
              tag="skf0")
        for j in range(1, 4):
            p.dma("sync", kfsp4[j], zt[:, 0:8], reads=["zt"], writes=["kfsp4"],
                  tag=f"skfz{j}")
        p.dma("sync", tt[0:1, 0:512], vsg[:, 0:512], reads=["vsg"],
              writes=["tt3"], tag="gvsa")
        p.dma("sync", tt[1:2, 0:512], vsg[:, 512:1024], reads=["vsg"],
              writes=["tt3"], tag="gvsb")
        p.dma("sync", vsp4[0][:, 0:512], tt[0:1, 0:512], reads=["tt3"],
              writes=["vsp4"], tag="svs0")
        p.dma("sync", vsp4[0][:, 512:1024], tt[1:2, 0:512], reads=["tt3"],
              writes=["vsp4"], tag="svs1")
        for j in range(1, 4):
            p.dma("sync", vsp4[j][:, 0:512], zt[0:1, :], reads=["zt"],
                  writes=["vsp4"], tag=f"svsz{j}a")
            p.dma("sync", vsp4[j][:, 512:1024], zt[0:1, :], reads=["zt"],
                  writes=["vsp4"], tag=f"svsz{j}b")
    p.barrier()
    with p.phase():
        e = dict(env, x=io["x"], qf=io["qf"], kvp4=kvp4, kfsp4=kfsp4,
                 vsp4=vsp4, wo=io["wo"], w1=io["w1"], w2=io["w2"],
                 b1c=io["b1c"], b2c=io["b2c"], lng=io["lng"], o_x1=x1)
        build_l3(sh=sh, tch=tch, ttok=N, env=e)
    p.barrier()
    with p.phase():
        t = p.sb("hx", [128, 8], F32)
        p.dma("sync", t[:].rearrange("p (k o) -> p k o", o=1),
              x1.rearrange("(k p) t -> p k t", p=128)[:, :, sh - 1:sh],
              reads=["x1_d"], writes=["hx"], tag="ghx",
              allow_slow_non_contiguous=True)
        p.dma("sync", x1last, t[:], reads=["hx"], writes=["x1last"], tag="shx")
    p.barrier()
    coll("AllGather", ALU.bypass, G2, [x1last], [x1g], ["x1last"], ["x1g"], "cx1")
    with p.phase():
        g = p.sb("gx", [128, 4 * 8], F32)
        selt = p.sb("selt", [128, 4], F32)
        acc = p.sb("accx", [128, 8], F32)
        p.dma("sync", g[:].rearrange("p (j k) -> p j k", j=4),
              x1g.rearrange("(j p) k -> p j k", p=128), reads=["x1g"],
              writes=["gx"], tag="lgx")
        p.dma("sync", selt[:], io["sel1"], writes=["selt"], tag="lsel1")
        ew(p, "vector", "memset", acc[:], 0.0, writes=["accx"], tag="az")
        for j in range(4):
            ew(p, "vector", "scalar_tensor_tensor", acc[:],
               g[:, j * 8:(j + 1) * 8], selt[:, j:j + 1], acc[:],
               ALU.mult, ALU.add, reads=["gx", "selt", "accx"],
               writes=["accx"], tag=f"asel{j}")
        p.dma("sync", xprev, acc[:], reads=["accx"], writes=["xprev_d"],
              tag="sxp")
    p.barrier()
    with p.phase():
        e = dict(env, x1=x1, xprev=xprev, wrT=io["wrT"], wkT=io["wkT"],
                 wvT=io["wvT"], murc=io["murc"], mukc=io["mukc"], o_x2=x2)
        build_l4(sh=sh, tch=tch, env=e)
    p.barrier()
    with p.phase():
        t = p.sb("hx2", [128, 24], F32)
        p.dma("sync", t[:].rearrange("p (k o) -> p k o", o=3),
              x2.rearrange("(k p) t -> p k t", p=128)[:, :, sh - 3:sh],
              reads=["x2_d"], writes=["hx2"], tag="ghx2",
              allow_slow_non_contiguous=True)
        p.dma("sync", x2l3, t[:], reads=["hx2"], writes=["x2l3"], tag="shx2")
    p.barrier()
    coll("AllGather", ALU.bypass, G2, [x2l3], [x2g], ["x2l3"], ["x2g"], "cx2")
    with p.phase():
        g = p.sb("gx2", [128, 4 * 24], F32)
        selt = p.sb("selt3", [128, 4], F32)
        acc = p.sb("accx2", [128, 24], F32)
        p.dma("sync", g[:].rearrange("p (j k) -> p j k", j=4),
              x2g.rearrange("(j p) k -> p j k", p=128), reads=["x2g"],
              writes=["gx2"], tag="lgx2")
        p.dma("sync", selt[:], io["sel3"], writes=["selt3"], tag="lsel3")
        ew(p, "vector", "memset", acc[:], 0.0, writes=["accx2"], tag="az2")
        for j in range(4):
            ew(p, "vector", "scalar_tensor_tensor", acc[:],
               g[:, j * 24:(j + 1) * 24], selt[:, j:j + 1], acc[:],
               ALU.mult, ALU.add, reads=["gx2", "selt3", "accx2"],
               writes=["accx2"], tag=f"asel2{j}")
        p.dma("sync", x2p3, acc[:], reads=["accx2"], writes=["x2p3_d"],
              tag="sxp2")
    p.barrier()
    with p.phase():
        e = dict(env, x2=x2, x2p3=x2p3, winT=io["winT"], wdtT=io["wdtT"],
                 cwc=io["cwc"], cbc=io["cbc"], bdtc=io["bdtc"],
                 dtbc=io["dtbc"], alogc=io["alogc"], bparc=io["bparc"],
                 o_dt=dtD, o_uu=uuD, o_sg=sgD, o_send=sendl, o_gend=gendl)
        build_l5(sh=sh, tch=tch, env=e)
    p.barrier()
    coll("AllGather", ALU.bypass, G2, [sendl], [sendg], ["sendl"], ["sendg"],
         "cse")
    coll("AllGather", ALU.bypass, G2, [gendl], [gendg], ["gendl"], ["gendg"],
         "cge")
    p.barrier()
    with p.phase():
        e = dict(env, dt=dtD, uu=uuD, sg=sgD, x2=x2,
                 send4=sendg.rearrange("(g p) t -> g p t", p=128),
                 gend4=gendg.rearrange("(g p) t -> g p t", p=128),
                 sel4=io["sel4"], alogc=io["alogc"], bparc=io["bparc"],
                 cparc=io["cparc"], dparc=io["dparc"], woutT=io["woutT"],
                 o_x3=x3)
        build_l6(sh=sh, tch=tch, env=e)
    p.barrier()
    with p.phase():
        e = dict(env, x3=x3, wg=io["wg"], we1f=io["we1f"], we3f=io["we3f"],
                 we2f=io["we2f"], o_y=io["o_y"])
        build_l7(sh=sh, tch=tch, env=e)
    p.barrier()
    p.final_wait("sync")
    p.build()
    p.finish()
    return nc


def _prep_core_inputs(ii):
    x_full = np.asarray(ii["L0_attn_x"])
    base = dict(
        wq=np.ascontiguousarray(ii["L0_attn_Wq"].reshape(D, D)),
        wk=np.ascontiguousarray(ii["L0_attn_Wk"].reshape(D, D)),
        wv=np.ascontiguousarray(ii["L0_attn_Wv"].reshape(D, D)),
        projT=np.ascontiguousarray(np.asarray(ii["L0_attn_proj"]).T),
        wo=np.ascontiguousarray(ii["L0_attn_Wo"].reshape(D, D)),
        w1=np.asarray(ii["L0_attn_W1"]), w2=np.asarray(ii["L0_attn_W2"]),
        b1c=_rsh(ii["L0_attn_b1"], 32), b2c=_rsh(ii["L0_attn_b2"], 8),
        ln1g=_rsh(ii["L0_attn_ln1_g"], 8), ln1b=_rsh(ii["L0_attn_ln1_b"], 8),
        ln2g=_rsh(ii["L0_attn_ln2_g"], 8), ln2b=_rsh(ii["L0_attn_ln2_b"], 8),
        wrT=np.ascontiguousarray(np.asarray(ii["L0_ffn_W_r"]).T),
        wkT=np.ascontiguousarray(np.asarray(ii["L0_ffn_W_k"]).T),
        wvT=np.ascontiguousarray(np.asarray(ii["L0_ffn_W_v"]).T),
        murc=_rsh(ii["L0_ffn_mu_r"], 8), mukc=_rsh(ii["L0_ffn_mu_k"], 8),
        winT=np.ascontiguousarray(np.asarray(ii["L1_attn_W_in"]).T),
        wdtT=np.ascontiguousarray(np.asarray(ii["L1_attn_W_dt"]).T),
        cwc=_rsh2(np.asarray(ii["L1_attn_conv_w"])),
        cbc=_rsh(ii["L1_attn_conv_b"], 16),
        bdtc=_rsh(ii["L1_attn_b_dt"], 16),
        dtbc=_rsh(ii["L1_attn_dt_bias"], 16),
        alogc=_rsh2(np.asarray(ii["L1_attn_A_log"])),
        bparc=_rsh2(np.asarray(ii["L1_attn_B_param"])),
        cparc=_rsh2(np.asarray(ii["L1_attn_C_param"])),
        dparc=_rsh(ii["L1_attn_D_param"], 16),
        woutT=np.ascontiguousarray(np.asarray(ii["L1_attn_W_out"]).T),
        wg=np.asarray(ii["L1_ffn_Wg"]),
        we1f=np.ascontiguousarray(np.concatenate(
            [np.asarray(ii["L1_ffn_We1"]).reshape(8 * D, FF_DIM),
             np.asarray(ii["L1_ffn_Ws1"])], 0)),
        we3f=np.ascontiguousarray(np.concatenate(
            [np.asarray(ii["L1_ffn_We3"]).reshape(8 * D, FF_DIM),
             np.asarray(ii["L1_ffn_Ws3"])], 0)),
        we2f=np.ascontiguousarray(np.concatenate(
            [np.asarray(ii["L1_ffn_We2"]).reshape(8 * FF_DIM, D),
             np.asarray(ii["L1_ffn_Ws2"])], 0)),
    )
    maps = []
    for c in range(8):
        b, q = c // 4, c % 4
        m = dict(base)
        m["x"] = np.ascontiguousarray(x_full[b, q * SH:(q + 1) * SH, :].T)
        for nm in ("sel1", "sel3"):
            s = np.zeros((128, 4), np.float32)
            if q > 0:
                s[:, q - 1] = 1.0
            m[nm] = s
        s4 = np.zeros((128, 4), np.float32)
        s4[:, q] = 1.0
        m["sel4"] = s4
        maps.append(m)
    return maps


def kernel_fused(**inputs):
    maps = _prep_core_inputs(inputs)
    nc = _get_nc("fused", build_fused)
    res = run_bass_kernel_spmd(nc, maps, core_ids=list(range(8))).results
    out = np.empty((2, N, D), np.float32)
    for b in range(2):
        for q in range(4):
            out[b, q * SH:(q + 1) * SH, :] = res[b * 4 + q]["y"].T
    return out



def build_fusedA(sh=SH, tch=512):
    nc = bass.Bass("TRN2", target_bir_lowering=False)
    EIn = lambda nm, shp, dt=F32: nc.dram_tensor(nm, shp, dt, kind="ExternalInput").ap()
    EOut = lambda nm, shp, dt=F32: nc.dram_tensor(nm, shp, dt, kind="ExternalOutput").ap()
    Int = lambda nm, shp, dt=F32: nc.dram_tensor(nm, shp, dt, kind="Internal").ap()
    x = EIn("x", [D, sh])
    wq = EIn("wq", [D, D]); wk = EIn("wk", [D, D]); wv = EIn("wv", [D, D])
    projT = EIn("projT", [DH, M])
    o_qf = EOut("qf", [D, sh], BF16)
    o_kv = EOut("kvp", [128, 512]); o_kfs = EOut("kfsp", [128, 8])
    o_vs = EOut("vsp", [1, 1024]); o_mx = EOut("mxloc", [1, 1])
    ddk = Int("ddk_i", [D, sh]); vv = Int("vv_i", [D, sh], BF16)
    mxall = Int("mxall_i", [8, 1])
    p = Prog(nc)
    env = {"nc": nc, "p": p}
    with p.phase():
        build_l1(sh=sh, tch=tch, env=dict(env, x=x, wq=wq, wk=wk, wv=wv,
                                          projT=projT, o_qf=o_qf, o_ddk=ddk,
                                          o_v=vv, o_mx=o_mx))
    p.barrier()
    with p.phase():
        t = p.sb("mxt", [1, 1], F32)
        p.dma("sync", t[:], o_mx, writes=["mxt"], tag="gmx")
        for j in range(8):
            p.dma("sync", mxall[j:j + 1, :], t[:], reads=["mxt"],
                  writes=["mxall"], tag=f"smx{j}")
    p.barrier()
    with p.phase():
        build_l2(sh=sh, tch=tch, env=dict(env, ddk=ddk, vv=vv, mxall=mxall,
                                          o_kv=o_kv, o_kfs=o_kfs, o_vs=o_vs))
    p.barrier()
    p.final_wait("sync")
    p.build()
    p.finish()
    return nc


def build_fusedD(sh=SH, tch=512):
    nc = bass.Bass("TRN2", target_bir_lowering=False)
    EIn = lambda nm, shp, dt=F32: nc.dram_tensor(nm, shp, dt, kind="ExternalInput").ap()
    Int = lambda nm, shp, dt=F32: nc.dram_tensor(nm, shp, dt, kind="Internal").ap()
    dt_ = EIn("dt", [D_INNER, sh]); uu = EIn("uu", [D_INNER, sh])
    sg = EIn("sg", [D_INNER, sh]); x2 = EIn("x2", [D, sh])
    send4 = EIn("send4", [4, 128, 256]); gend4 = EIn("gend4", [4, 128, 256])
    sel4 = EIn("sel4", [128, 4])
    alogc = EIn("alogc", [128, 256]); bparc = EIn("bparc", [128, 256])
    cparc = EIn("cparc", [128, 256]); dparc = EIn("dparc", [128, 16])
    woutT = EIn("woutT", [D_INNER, D])
    wg = EIn("wg", [D, 8])
    we1f = EIn("we1f", [9 * D, FF_DIM]); we3f = EIn("we3f", [9 * D, FF_DIM])
    we2f = EIn("we2f", [9 * FF_DIM, D])
    o_y = nc.dram_tensor("y", [D, sh], F32, kind="ExternalOutput").ap()
    x3 = Int("x3_i", [D, sh])
    p = Prog(nc)
    env = {"nc": nc, "p": p}
    with p.phase():
        build_l6(sh=sh, tch=tch, env=dict(env, dt=dt_, uu=uu, sg=sg, x2=x2,
                                          send4=send4, gend4=gend4, sel4=sel4,
                                          alogc=alogc, bparc=bparc, cparc=cparc,
                                          dparc=dparc, woutT=woutT, o_x3=x3))
    p.barrier()
    with p.phase():
        build_l7(sh=sh, tch=tch, env=dict(env, x3=x3, wg=wg, we1f=we1f,
                                          we3f=we3f, we2f=we2f, o_y=o_y))
    p.barrier()
    p.final_wait("sync")
    p.build()
    p.finish()
    return nc


def build_l3a(sh=SH, tch=512):
    nc = bass.Bass("TRN2", target_bir_lowering=False)
    EIn = lambda nm, shp, dt=F32: nc.dram_tensor(nm, shp, dt, kind="ExternalInput").ap()
    x = EIn("x", [D, sh]); qf = EIn("qf", [D, sh], BF16)
    kvp4 = EIn("kvp4", [4, 128, 512]); kfsp4 = EIn("kfsp4", [4, 128, 8])
    vsp4 = EIn("vsp4", [4, 1, 1024])
    wo = EIn("wo", [D, D]); w1 = EIn("w1", [D, DFF]); w2 = EIn("w2", [DFF, D])
    b1c = EIn("b1c", [128, 32]); b2c = EIn("b2c", [128, 8])
    lng = {nm: EIn(nm, [128, 8]) for nm in ("ln1g", "ln1b", "ln2g", "ln2b")}
    mxg4 = EIn("mxg4", [1, 4]); mx8 = EIn("mx8", [1, 8])
    o_x1 = nc.dram_tensor("x1", [D, sh], F32, kind="ExternalOutput").ap()
    p = Prog(nc)
    env = {"nc": nc, "p": p}
    with p.phase():
        build_l3(sh=sh, tch=tch, ttok=N, env=dict(
            env, x=x, qf=qf, kvp4=kvp4, kfsp4=kfsp4, vsp4=vsp4, wo=wo, w1=w1,
            w2=w2, b1c=b1c, b2c=b2c, lng=lng, o_x1=o_x1, mxg4=mxg4, mx8=mx8))
    p.barrier()
    p.final_wait("sync")
    p.build()
    p.finish()
    return nc


def kernel_v5(**inputs):
    ii = inputs
    maps = _prep_core_inputs(ii)
    ncA = _get_nc("fA", build_fusedA)
    rA = run_bass_kernel_spmd(ncA, [dict(x=m["x"], wq=m["wq"], wk=m["wk"],
                                         wv=m["wv"], projT=m["projT"])
                                    for m in maps],
                              core_ids=list(range(8))).results
    mx8 = np.concatenate([rA[c]["mxloc"] for c in range(8)], 1).astype(np.float32)
    nc3 = _get_nc("l3a", build_l3a)
    im3 = []
    for c in range(8):
        g0 = (c // 4) * 4
        m = maps[c]
        im3.append(dict(
            x=m["x"], qf=rA[c]["qf"],
            kvp4=np.stack([rA[g0 + j]["kvp"] for j in range(4)]),
            kfsp4=np.stack([rA[g0 + j]["kfsp"] for j in range(4)]),
            vsp4=np.stack([rA[g0 + j]["vsp"] for j in range(4)]),
            wo=m["wo"], w1=m["w1"], w2=m["w2"], b1c=m["b1c"], b2c=m["b2c"],
            ln1g=m["ln1g"], ln1b=m["ln1b"], ln2g=m["ln2g"], ln2b=m["ln2b"],
            mxg4=np.ascontiguousarray(mx8[:, g0:g0 + 4]), mx8=mx8))
    r3 = run_bass_kernel_spmd(nc3, im3, core_ids=list(range(8))).results

    nc4 = _get_nc("l4", build_l4)
    im4 = []
    for c in range(8):
        if c % 4 == 0:
            xprev = np.zeros((128, 8), np.float32)
        else:
            xprev = _fm3(r3[c - 1]["x1"][:, SH - 1:SH])
        m = maps[c]
        im4.append(dict(x1=r3[c]["x1"], xprev=np.ascontiguousarray(xprev),
                        wrT=m["wrT"], wkT=m["wkT"], wvT=m["wvT"],
                        murc=m["murc"], mukc=m["mukc"]))
    r4 = run_bass_kernel_spmd(nc4, im4, core_ids=list(range(8))).results

    nc5 = _get_nc("l5", build_l5)
    im5 = []
    for c in range(8):
        if c % 4 == 0:
            p3 = np.zeros((128, 24), np.float32)
        else:
            p3 = _fm3(r4[c - 1]["x2"][:, SH - 3:SH]).reshape(128, 24)
        m = maps[c]
        im5.append(dict(x2=r4[c]["x2"], x2p3=np.ascontiguousarray(p3),
                        winT=m["winT"], wdtT=m["wdtT"], cwc=m["cwc"],
                        cbc=m["cbc"], bdtc=m["bdtc"], dtbc=m["dtbc"],
                        alogc=m["alogc"], bparc=m["bparc"]))
    r5 = run_bass_kernel_spmd(nc5, im5, core_ids=list(range(8))).results

    ncD = _get_nc("fD", build_fusedD)
    imD = []
    for c in range(8):
        g0 = (c // 4) * 4
        m = maps[c]
        imD.append(dict(dt=r5[c]["dt"], uu=r5[c]["uu"], sg=r5[c]["sg"],
                        x2=r4[c]["x2"],
                        send4=np.stack([r5[g0 + j]["send"] for j in range(4)]),
                        gend4=np.stack([r5[g0 + j]["gend"] for j in range(4)]),
                        sel4=m["sel4"], alogc=m["alogc"], bparc=m["bparc"],
                        cparc=m["cparc"], dparc=m["dparc"], woutT=m["woutT"],
                        wg=m["wg"], we1f=m["we1f"], we3f=m["we3f"],
                        we2f=m["we2f"]))
    rD = run_bass_kernel_spmd(ncD, imD, core_ids=list(range(8))).results
    out = np.empty((2, N, D), np.float32)
    for b in range(2):
        for q in range(4):
            out[b, q * SH:(q + 1) * SH, :] = rD[b * 4 + q]["y"].T
    return out


def build_fusedB(sh=SH, tch=512):
    nc = bass.Bass("TRN2", target_bir_lowering=False)
    EIn = lambda nm, shp, dt=F32: nc.dram_tensor(nm, shp, dt, kind="ExternalInput").ap()
    Int = lambda nm, shp, dt=F32: nc.dram_tensor(nm, shp, dt, kind="Internal").ap()
    x = EIn("x", [D, sh]); qf = EIn("qf", [D, sh], BF16)
    kvp4 = EIn("kvp4", [4, 128, 512]); kfsp4 = EIn("kfsp4", [4, 128, 8])
    vsp4 = EIn("vsp4", [4, 1, 1024])
    wo = EIn("wo", [D, D]); w1 = EIn("w1", [D, DFF]); w2 = EIn("w2", [DFF, D])
    b1c = EIn("b1c", [128, 32]); b2c = EIn("b2c", [128, 8])
    lng = {nm: EIn(nm, [128, 8]) for nm in ("ln1g", "ln1b", "ln2g", "ln2b")}
    mxg4 = EIn("mxg4", [1, 4]); mx8 = EIn("mx8", [1, 8])
    x_h = EIn("x_h", [128, 32]); qf_h = EIn("qf_h", [128, 32], BF16)
    mask1 = EIn("mask1", [128, 1])
    wrT = EIn("wrT", [D, D]); wkT = EIn("wkT", [D, D]); wvT = EIn("wvT", [D, D])
    murc = EIn("murc", [128, 8]); mukc = EIn("mukc", [128, 8])
    winT = EIn("winT", [D, 2 * D_INNER]); wdtT = EIn("wdtT", [D_INNER, D_INNER])
    cwc = EIn("cwc", [128, 64]); cbc = EIn("cbc", [128, 16])
    bdtc = EIn("bdtc", [128, 16]); dtbc = EIn("dtbc", [128, 16])
    alogc = EIn("alogc", [128, 256]); bparc = EIn("bparc", [128, 256])
    o_dt = nc.dram_tensor("dt", [D_INNER, sh], F32, kind="ExternalOutput").ap()
    o_uu = nc.dram_tensor("uu", [D_INNER, sh], F32, kind="ExternalOutput").ap()
    o_sg = nc.dram_tensor("sg", [D_INNER, sh], F32, kind="ExternalOutput").ap()
    o_send = nc.dram_tensor("send", [128, 256], F32, kind="ExternalOutput").ap()
    o_gend = nc.dram_tensor("gend", [128, 256], F32, kind="ExternalOutput").ap()
    o_x2 = nc.dram_tensor("x2", [D, sh], F32, kind="ExternalOutput").ap()
    x1 = Int("x1_i", [D, sh])
    x1h = Int("x1h_i", [128, 32])
    xprev = Int("xprev_i", [128, 8])
    x2p3 = Int("x2p3_i", [128, 24])
    p = Prog(nc)
    env = {"nc": nc, "p": p}
    with p.phase():
        build_l3(sh=sh, tch=tch, ttok=N, env=dict(
            env, x=x, qf=qf, kvp4=kvp4, kfsp4=kfsp4, vsp4=vsp4, wo=wo, w1=w1,
            w2=w2, b1c=b1c, b2c=b2c, lng=lng, o_x1=x1, mxg4=mxg4, mx8=mx8,
            x_h=x_h, qf_h=qf_h, mask1=mask1, o_x1h=x1h))
    p.barrier()
    with p.phase():
        t = p.sb("xpv", [128, 8], F32)
        p.dma("sync", t[:].rearrange("p (k o) -> p k o", o=1),
              x1h.rearrange("p (k t) -> p k t", t=4)[:, :, 3:4],
              writes=["xpv"], tag="gxp", allow_slow_non_contiguous=True)
        p.dma("sync", xprev, t[:], reads=["xpv"], writes=["xprev_d"], tag="sxp")
    p.barrier()
    with p.phase():
        build_l4(sh=sh, tch=tch, env=dict(
            env, x1=x1, xprev=xprev, wrT=wrT, wkT=wkT, wvT=wvT, murc=murc,
            mukc=mukc, o_x2=o_x2, x1h=x1h, mask1=mask1, o_x2p3=x2p3))
    p.barrier()
    with p.phase():
        build_l5(sh=sh, tch=tch, env=dict(
            env, x2=o_x2, x2p3=x2p3, winT=winT, wdtT=wdtT, cwc=cwc, cbc=cbc,
            bdtc=bdtc, dtbc=dtbc, alogc=alogc, bparc=bparc, o_dt=o_dt,
            o_uu=o_uu, o_sg=o_sg, o_send=o_send, o_gend=o_gend))
    p.barrier()
    p.final_wait("sync")
    p.build()
    p.finish()
    return nc


def kernel_v3(**inputs):
    ii = inputs
    maps = _prep_core_inputs(ii)
    ncA = _get_nc("fA", build_fusedA)
    rA = run_bass_kernel_spmd(ncA, [dict(x=m["x"], wq=m["wq"], wk=m["wk"],
                                         wv=m["wv"], projT=m["projT"])
                                    for m in maps],
                              core_ids=list(range(8))).results
    mx8 = np.concatenate([rA[c]["mxloc"] for c in range(8)], 1).astype(np.float32)
    ncB = _get_nc("fB", build_fusedB)
    imB = []
    for c in range(8):
        g0 = (c // 4) * 4
        m = maps[c]
        q = c % 4
        if q == 0:
            x_h = np.zeros((128, 32), np.float32)
            qf_h = np.zeros((128, 32), np.float32).astype(rA[c]["qf"].dtype)
            mk1 = np.zeros((128, 1), np.float32)
        else:
            x_h = _fm3(np.ascontiguousarray(m_prev_x(maps, c)[:, SH - 4:SH]))
            qf_h = _fm3(np.ascontiguousarray(
                np.asarray(rA[c - 1]["qf"])[:, SH - 4:SH]))
            mk1 = np.ones((128, 1), np.float32)
        imB.append(dict(
            x=m["x"], qf=rA[c]["qf"],
            kvp4=np.stack([rA[g0 + j]["kvp"] for j in range(4)]),
            kfsp4=np.stack([rA[g0 + j]["kfsp"] for j in range(4)]),
            vsp4=np.stack([rA[g0 + j]["vsp"] for j in range(4)]),
            wo=m["wo"], w1=m["w1"], w2=m["w2"], b1c=m["b1c"], b2c=m["b2c"],
            ln1g=m["ln1g"], ln1b=m["ln1b"], ln2g=m["ln2g"], ln2b=m["ln2b"],
            mxg4=np.ascontiguousarray(mx8[:, g0:g0 + 4]), mx8=mx8,
            x_h=np.ascontiguousarray(x_h), qf_h=np.ascontiguousarray(qf_h),
            mask1=mk1, wrT=m["wrT"], wkT=m["wkT"], wvT=m["wvT"],
            murc=m["murc"], mukc=m["mukc"], winT=m["winT"], wdtT=m["wdtT"],
            cwc=m["cwc"], cbc=m["cbc"], bdtc=m["bdtc"], dtbc=m["dtbc"],
            alogc=m["alogc"], bparc=m["bparc"]))
    rB = run_bass_kernel_spmd(ncB, imB, core_ids=list(range(8))).results

    ncD = _get_nc("fD", build_fusedD)
    imD = []
    for c in range(8):
        g0 = (c // 4) * 4
        m = maps[c]
        imD.append(dict(dt=rB[c]["dt"], uu=rB[c]["uu"], sg=rB[c]["sg"],
                        x2=rB[c]["x2"],
                        send4=np.stack([rB[g0 + j]["send"] for j in range(4)]),
                        gend4=np.stack([rB[g0 + j]["gend"] for j in range(4)]),
                        sel4=m["sel4"], alogc=m["alogc"], bparc=m["bparc"],
                        cparc=m["cparc"], dparc=m["dparc"], woutT=m["woutT"],
                        wg=m["wg"], we1f=m["we1f"], we3f=m["we3f"],
                        we2f=m["we2f"]))
    rD = run_bass_kernel_spmd(ncD, imD, core_ids=list(range(8))).results
    out = np.empty((2, N, D), np.float32)
    for b in range(2):
        for q in range(4):
            out[b, q * SH:(q + 1) * SH, :] = rD[b * 4 + q]["y"].T
    return out


def m_prev_x(maps, c):
    return maps[c - 1]["x"]


def build_fusedD2(sh=SH, tch=512):
    NIC = D_INNER // 128
    NE = 9
    nch = sh // tch
    nc = bass.Bass("TRN2", target_bir_lowering=False)
    EIn = lambda nm, shp, dt=F32: nc.dram_tensor(nm, shp, dt, kind="ExternalInput").ap()
    dt_ = EIn("dt", [D_INNER, sh]); uu = EIn("uu", [D_INNER, sh])
    sg = EIn("sg", [D_INNER, sh]); x2 = EIn("x2", [D, sh])
    send4 = EIn("send4", [4, 128, 256]); gend4 = EIn("gend4", [4, 128, 256])
    sel4 = EIn("sel4", [128, 4])
    alogc = EIn("alogc", [128, 256]); bparc = EIn("bparc", [128, 256])
    cparc = EIn("cparc", [128, 256]); dparc = EIn("dparc", [128, 16])
    woutT = EIn("woutT", [D_INNER, D])
    wg = EIn("wg", [D, 8])
    we1f = EIn("we1f", [9 * D, FF_DIM]); we3f = EIn("we3f", [9 * D, FF_DIM])
    we2f = EIn("we2f", [9 * FF_DIM, D])
    o_y = nc.dram_tensor("y", [D, sh], F32, kind="ExternalOutput").ap()
    we1b = nc.dram_tensor("we1b", [128, NE * 16 * 1024], BF16, kind="Internal").ap()
    we3b = nc.dram_tensor("we3b", [128, NE * 16 * 1024], BF16, kind="Internal").ap()
    we2b = nc.dram_tensor("we2b", [128, NE * 16 * D], BF16, kind="Internal").ap()

    wob_scr = nc.dram_tensor("wob_scr", [128, 8 * NIC * 128], BF16,
                             kind="Internal").ap()
    p = Prog(nc)

    wot = [p.sb(f"wot{i}", [128, NIC * 128], BF16) for i in range(2)]
    negA = p.sb("negA", [128, 256], F32)
    bpar = p.sb("bpar", [128, 256], F32)
    cpar = p.sb("cpar", [128, 256], F32)
    dpar = p.sb("dpar", [128, 16], F32)
    sends = p.sb("sends", [128, 4 * 256], F32)
    gends = p.sb("gends", [128, 4 * 256], F32)
    sel = p.sb("sel", [128, 4], F32)
    carry = p.sb("carry", [128, 256], F32)
    Tst = p.sb("Tst", [128, 256], F32)
    wgb = p.sb("wgb", [128, 64], BF16)
    identf = p.sb("identf", [128, 128], F32)
    esel = p.sb("esel", [8, 8 * 128], F32)
    wtf = p.sb("wtf", [128, 1024], F32)
    wtb = p.sb("wtb", [128, 1024], BF16)
    cdt = [p.sb(f"cdt{i}", [128, tch], F32) for i in range(2)]
    cuu = [p.sb(f"cuu{i}", [128, tch], F32) for i in range(2)]
    cdtu = [p.sb("cdtu0", [128, tch], F32)] * 2
    cdA = [p.sb(f"cdA{i}", [128, tch], F32) for i in range(2)]
    cdBu = [p.sb(f"cdBu{i}", [128, tch], F32) for i in range(2)]
    cS = p.sb("cS", [128, tch], F32)
    cya = p.sb("cya", [128, tch], F32)
    csg = p.sb("csg", [128, tch], F32)
    ygb = [p.sb("ygb0", [128, NIC * tch], BF16)] * 2
    x2c = [p.sb(f"x2c{i}", [128, tch], F32) for i in range(2)]
    x3f = [p.sb(f"x3f{i}", [128, 8 * tch], F32) for i in range(2)]
    x3b = [p.sb("x3b0", [128, 8 * tch], BF16)] * 2
    w1t3 = [p.sb(f"w1t3{i}", [128, 8 * 128], BF16) for i in range(2)]
    w3t3 = [p.sb(f"w3t3{i}", [128, 8 * 128], BF16) for i in range(2)]
    w2t2 = [p.sb(f"w2t2{i}", [128, 16 * 128], BF16) for i in range(2)]
    racc = p.sb("racc", [128, 8 * tch], F32)
    hwb = p.sb("hwb", [128, 16 * tch], BF16)
    stmp2 = [p.sb(f"stmp2{i}", [128, tch], F32) for i in range(2)]
    wbs = p.sb("wbs", [128, tch], F32)
    g_fm = p.sb("g_fm", [8, tch], F32)
    w_fmt = p.sb("w_fmt", [8, tch], F32)
    g_tm = p.sb("g_tm", [128, 8], F32)
    top8 = p.sb("top8", [128, 8], F32)
    rep = p.sb("rep", [128, 8], F32)
    gz = p.sb("gz", [128, 8], F32)
    w_tm = p.sb("w_tm", [128, 8], F32)
    s2c = p.sb("s2c", [128, 1], F32)
    zf = p.sb("zf", [128, 8 * tch], F32)
    aux = make_ln_aux(p, "ln", tch)

    pC = p.ps("pC", [128, tch], F32)
    pD = p.ps("pD", [128, tch], F32)
    pE = p.ps("pE", [128, tch], F32)
    pF = p.ps("pF", [128, tch], F32)
    pA = p.ps("pA", [128, tch], F32)
    pB = p.ps("pB", [128, tch], F32)
    ps1 = p.ps("ps1", [1, tch], F32)
    ps2 = p.ps("ps2", [1, tch], F32)
    aux.update(ps1=ps1, ps2=ps2, pbm=pA, pba=pB, utile=cya)
    aux["keys"].update(ps1="ps1", ps2="ps2", pbm="pA", pba="pB", utile="cya",
                       epscol="epsc")
    ec2, eck2 = p.constcol(EPS_LN_OUTER)
    aux["epscol"] = ec2
    aux["keys"]["epscol"] = eck2

    make_identity(p, identf[:], "identf")
    p.emit("gpsimd", lambda e: e.memset(esel[:], 0.0), writes=["esel"], tag="es0")
    p.emit("gpsimd", lambda e: e.affine_select(
        out=esel[:].rearrange("j (e q) -> j e q", q=128),
        in_=esel[:].rearrange("j (e q) -> j e q", q=128),
        compare_op=ALU.not_equal, fill=1.0, base=0,
        pattern=[[-1, 8], [0, 128]], channel_multiplier=1),
        reads=["esel"], writes=["esel"], tag="es1")
    for g in range(NIC):
        p.dma("sync", wtf[:, 0:D],
              woutT.rearrange("(k p) m -> p k m", p=128)[:, g, :],
              writes=["wtf"], tag=f"wo{g}")
        act(p, wtb[:, 0:D], wtf[:, 0:D], AF.Copy,
            reads=["wtf"], writes=["wtb"], tag=f"woc{g}")
        p.dma("sync",
              wob_scr.rearrange("p (o k f) -> p o k f", k=NIC, f=128)[:, :, g, :],
              wtb[:, 0:D].rearrange("p (o f) -> p o f", f=128),
              reads=["wtb"], writes=["wob_hbm"], tag=f"wos{g}")
    p.dma("sync", wtf[:, 0:64].rearrange("p (k e) -> p k e", e=8),
          wg.rearrange("(k p) e -> p k e", p=128), writes=["wtf"], tag="lwg")
    act(p, wgb[:], wtf[:, 0:64], AF.Copy, reads=["wtf"], writes=["wgb"], tag="cwg")
    p.dma("sync", negA[:], alogc, writes=["negA0"], tag="lda")
    act(p, negA[:], negA[:], AF.Exp, reads=["negA0"], writes=["negA1"], tag="na1")
    ew(p, "vector", "tensor_scalar_mul", negA[:], negA[:], -1.0,
       reads=["negA1"], writes=["negA"], tag="na2")
    p.dma("sync", bpar[:], bparc, writes=["bpar"], tag="ldb")
    p.dma("sync", cpar[:], cparc, writes=["cpar"], tag="ldc")
    p.dma("sync", dpar[:], dparc, writes=["dpar"], tag="ldd")
    p.dma("sync", sends.rearrange("p (g t) -> p g t", g=4),
          send4.rearrange("g p t -> p g t"), writes=["sends"], tag="lds")
    p.dma("sync", gends.rearrange("p (g t) -> p g t", g=4),
          gend4.rearrange("g p t -> p g t"), writes=["gends"], tag="ldg")
    p.dma("sync", sel[:], sel4, writes=["sel"], tag="ldsel")
    ew(p, "vector", "memset", Tst[:], 0.0, writes=["Tst"], tag="T0")
    ew(p, "vector", "memset", carry[:], 0.0, writes=["carry"], tag="c00")
    for j in range(4):
        ew(p, "vector", "scalar_tensor_tensor", carry[:], Tst[:],
           sel[:, j:j + 1], carry[:], ALU.mult, ALU.add,
           reads=["Tst", "sel", "carry"], writes=["carry"], tag=f"s0a{j}")
        if j < 3:
            ew(p, "vector", "tensor_mul", Tst[:], Tst[:],
               gends[:, j * 256:(j + 1) * 256], reads=["Tst", "gends"],
               writes=["Tst"], tag=f"Tm{j}")
            ew(p, "vector", "tensor_add", Tst[:], Tst[:],
               sends[:, j * 256:(j + 1) * 256], reads=["Tst", "sends"],
               writes=["Tst"], tag=f"Ta{j}")
    for (srcT, dstT, key) in [(we1f, we1b, "we1_hbm"), (we3f, we3b, "we3_hbm")]:
        for e in range(NE):
            for k in range(8):
                for hh in range(2):
                    p.dma("sync", wtf[:],
                          srcT.rearrange("(q p) m -> p q m", p=128)
                          [:, e * 8 + k, hh * 1024:(hh + 1) * 1024],
                          writes=["wtf"], tag=f"pl{key}{e}_{k}_{hh}")
                    act(p, wtb[:], wtf[:], AF.Copy, reads=["wtf"],
                        writes=["wtb"], tag=f"pc{key}{e}_{k}_{hh}")
                    p.dma("sync",
                          dstT.rearrange("p (q g kk f) -> p q g kk f", g=16,
                                         kk=8, f=128)
                          [:, e, hh * 8:(hh + 1) * 8, k, :],
                          wtb[:].rearrange("p (g f) -> p g f", f=128),
                          reads=["wtb"], writes=[key], tag=f"ps{key}{e}_{k}_{hh}")
    for e in range(NE):
        for k in range(16):
            p.dma("sync", wtf[:, 0:D],
                  we2f.rearrange("(q p) m -> p q m", p=128)[:, e * 16 + k, :],
                  writes=["wtf"], tag=f"plw2{e}_{k}")
            act(p, wtb[:, 0:D], wtf[:, 0:D], AF.Copy, reads=["wtf"],
                writes=["wtb"], tag=f"pcw2{e}_{k}")
            p.dma("sync",
                  we2b.rearrange("p (q o kk f) -> p q o kk f", o=8, kk=16,
                                 f=128)[:, e, :, k, :],
                  wtb[:, 0:D].rearrange("p (o f) -> p o f", f=128),
                  reads=["wtb"], writes=["we2_hbm"], tag=f"psw2{e}_{k}")

    dt_r = dt_.rearrange("(i p) t -> p i t", p=128)
    uu_r = uu.rearrange("(i p) t -> p i t", p=128)
    sg_r = sg.rearrange("(i p) t -> p i t", p=128)
    x2_r = x2.rearrange("(k p) t -> p k t", p=128)
    y_r = o_y.rearrange("(k p) t -> p k t", p=128)

    for tc in range(nch):
        bi = tc % 2
        s = slice(tc * tch, (tc + 1) * tch)
        for ci in range(NIC):
            p.dma("sync", cdt[bi][:], dt_r[:, ci, s], writes=[f"cdt{bi}"],
                  tag=f"ldt{tc}_{ci}")
            p.dma("sync", cuu[bi][:], uu_r[:, ci, s], writes=[f"cuu{bi}"],
                  tag=f"luu{tc}_{ci}")
            ew(p, "vector", "tensor_mul", cdtu[bi][:], cdt[bi][:], cuu[bi][:],
               reads=[f"cdt{bi}", f"cuu{bi}"], writes=[f"cdtu{bi}"],
               tag=f"mdu{tc}_{ci}")
            for n in range(D_STATE):
                col = ci * D_STATE + n
                nb = n % 2
                act(p, cdA[nb][:], cdt[bi][:], AF.Exp,
                    scale=negA[:, col:col + 1],
                    reads=[f"cdt{bi}", "negA"], writes=[f"cdA{nb}"],
                    tag=f"bA{tc}_{ci}_{n}")
                act(p, cdBu[nb][:], cdtu[bi][:], AF.Copy,
                    scale=bpar[:, col:col + 1],
                    reads=[f"cdtu{bi}", "bpar"], writes=[f"cdBu{nb}"],
                    tag=f"bB{tc}_{ci}_{n}")
                ew(p, "vector", "tensor_tensor_scan", cS[:], cdA[nb][:],
                   cdBu[nb][:], carry[:, col:col + 1], ALU.mult, ALU.add,
                   reads=[f"cdA{nb}", f"cdBu{nb}", "carry"], writes=["cS"],
                   tag=f"sc{tc}_{ci}_{n}")
                ew(p, "vector", "tensor_copy", carry[:, col:col + 1],
                   cS[:, tch - 1:tch], reads=["cS"], writes=["carry"],
                   tag=f"cc{tc}_{ci}_{n}")
                if n == 0:
                    ew(p, "vector", "tensor_scalar_mul", cya[:], cS[:],
                       cpar[:, col:col + 1], reads=["cS", "cpar"],
                       writes=["cya"], tag=f"y0{tc}_{ci}")
                else:
                    ew(p, "vector", "scalar_tensor_tensor", cya[:], cS[:],
                       cpar[:, col:col + 1], cya[:], ALU.mult, ALU.add,
                       reads=["cS", "cpar", "cya"], writes=["cya"],
                       tag=f"ya{tc}_{ci}_{n}")
            ew(p, "vector", "scalar_tensor_tensor", cya[:], cuu[bi][:],
               dpar[:, ci:ci + 1], cya[:], ALU.mult, ALU.add,
               reads=[f"cuu{bi}", "dpar", "cya"], writes=["cya"],
               tag=f"yd{tc}_{ci}")
            p.dma("gpsimd", csg[:], sg_r[:, ci, s], writes=["csg"],
                  tag=f"lsg{tc}_{ci}")
            ew(p, "vector", "tensor_mul",
               ygb[bi][:, ci * tch:(ci + 1) * tch], cya[:], csg[:],
               reads=["cya", "csg"], writes=[f"ygb{bi}", f"ygb{bi}_{ci}"],
               tag=f"yg{tc}_{ci}")
        for oc in range(8):
            wi = oc % 2
            p.dma("gpsimd", wot[wi][:],
                  wob_scr[:, oc * NIC * 128:(oc + 1) * NIC * 128],
                  reads=["wob_hbm"], writes=[f"wot{wi}"], tag=f"lwo{tc}_{oc}")
            p.dma("sync", x2c[wi][:],
                  x2_r[:, oc, s], writes=[f"x2c{wi}"], tag=f"lx2{tc}_{oc}")
            pb, pbk = (pC, "pC") if oc % 2 == 0 else (pD, "pD")
            mm_group(p, [(pb[:], wot[wi][:, k * 128:(k + 1) * 128],
                          ygb[bi][:, k * tch:(k + 1) * tch], k == 0, k == NIC - 1)
                         for k in range(NIC)],
                     reads=[f"wot{wi}", f"ygb{bi}"], writes=[pbk],
                     tag=f"op{tc}_{oc}")
            ew(p, "vector", "tensor_add", zf[:, oc * tch:(oc + 1) * tch],
               x2c[wi][:], pb[:],
               reads=[f"x2c{wi}", pbk], writes=["zf"], tag=f"z{tc}_{oc}")
        ln_fm(p, f"lno_{tc}", zf, "zf", x3f[bi], f"x3f{bi}", aux, EPS_LN_OUTER)
        for oc in range(8):
            act(p, x3b[bi][:, oc * tch:(oc + 1) * tch],
                x3f[bi][:, oc * tch:(oc + 1) * tch], AF.Copy,
                reads=[f"x3f{bi}"], writes=[f"x3b{bi}"], tag=f"cx{tc}_{oc}")
        trip = [(ps1[0:8, :] if False else pF[0:8, :],
                 wgb[:, k * 8:(k + 1) * 8],
                 x3b[bi][:, k * tch:(k + 1) * tch], k == 0, k == 7)
                for k in range(8)]
        mm_group(p, trip, reads=[f"x3b{bi}", "wgb"], writes=["pF"],
                 tag=f"gm{tc}")
        act(p, g_fm[:], pF[0:8, :], AF.Sigmoid, reads=["pF"], writes=["g_fm"],
            tag=f"gs{tc}")
        for tb in range(tch // 128):
            transpose(p, pF[0:128, 0:8], g_fm[:, tb * 128:(tb + 1) * 128],
                      identf[0:8, 0:8], reads=["g_fm", "identf"], writes=["pF"],
                      tag=f"gt{tc}_{tb}")
            ew(p, "vector", "tensor_copy", g_tm[:], pF[0:128, 0:8],
               reads=["pF"], writes=["g_tm"], tag=f"ge{tc}_{tb}")
            ew(p, "vector", "max", out=top8[:], in_=g_tm[:],
               reads=["g_tm"], writes=["top8"], tag=f"gx{tc}_{tb}")
            ew(p, "vector", "tensor_copy", rep[:, 0:2], top8[:, 0:2],
               reads=["top8"], writes=["rep"], tag=f"gr{tc}_{tb}")
            ew(p, "vector", "memset", rep[:, 2:8], -1e30,
               reads=[], writes=["rep2"], deps=[p.res["rep"]["w"]],
               tag=f"gm2{tc}_{tb}")
            ew(p, "vector", "match_replace", gz[:], rep[:], g_tm[:], 0.0,
               reads=["rep", "rep2", "g_tm"], writes=["gz"], tag=f"gmr{tc}_{tb}")
            ew(p, "vector", "tensor_sub", w_tm[:], g_tm[:], gz[:],
               reads=["g_tm", "gz"], writes=["w_tm"], tag=f"gw{tc}_{tb}")
            ew(p, "vector", "tensor_add", s2c[:], top8[:, 0:1], top8[:, 1:2],
               reads=["top8"], writes=["s2c"], tag=f"g2{tc}_{tb}")
            ew(p, "vector", "reciprocal", s2c[:], s2c[:],
               reads=["s2c"], writes=["s2c"], tag=f"g3{tc}_{tb}")
            ew(p, "vector", "tensor_scalar_mul", w_tm[:], w_tm[:], s2c[:, 0:1],
               reads=["w_tm", "s2c"], writes=["w_tm"], tag=f"g4{tc}_{tb}")
            transpose(p, pF[0:8, 0:128], w_tm[:], identf[:],
                      reads=["w_tm", "identf"], writes=["pF"], tag=f"g5{tc}_{tb}")
            ew(p, "vector", "tensor_copy", w_fmt[:, tb * 128:(tb + 1) * 128],
               pF[0:8, 0:128], reads=["pF"], writes=["w_fmt"], tag=f"g6{tc}_{tb}")
        for e in range(NE):
            if e < 8:
                mm_group(p, [(pB[:], esel[:, e * 128:(e + 1) * 128],
                              w_fmt[:], True, True)],
                         reads=["w_fmt", "esel"], writes=["pB"],
                         tag=f"wb{tc}_{e}")
                ew(p, "vector", "tensor_copy", wbs[:], pB[:], reads=["pB"],
                   writes=["wbs"], tag=f"wbe{tc}_{e}")
            for fcg in range(16):
                wi = fcg % 2
                tb0 = (e * 16 + fcg) * 1024
                p.dma("sync", w1t3[wi][:], we1b[:, tb0:tb0 + 1024],
                      reads=["we1_hbm"], writes=[f"w1t3{wi}"],
                      tag=f"l1{tc}_{e}_{fcg}")
                p.dma("gpsimd", w3t3[wi][:], we3b[:, tb0:tb0 + 1024],
                      reads=["we3_hbm"], writes=[f"w3t3{wi}"],
                      tag=f"l3{tc}_{e}_{fcg}")
                mm_group(p, [(pC[:], w1t3[wi][:, k * 128:(k + 1) * 128],
                              x3b[bi][:, k * tch:(k + 1) * tch], k == 0, k == 7)
                             for k in range(8)],
                         reads=[f"w1t3{wi}", f"x3b{bi}"], writes=["pC"],
                         tag=f"e1{tc}_{e}_{fcg}")
                mm_group(p, [(pD[:], w3t3[wi][:, k * 128:(k + 1) * 128],
                              x3b[bi][:, k * tch:(k + 1) * tch], k == 0, k == 7)
                             for k in range(8)],
                         reads=[f"w3t3{wi}", f"x3b{bi}"], writes=["pD"],
                         tag=f"e3{tc}_{e}_{fcg}")
                st_ = stmp2[fcg % 2]
                stk = f"stmp2{fcg % 2}"
                act(p, st_[:], pC[:], AF.Silu, reads=["pC"], writes=[stk],
                    tag=f"sl{tc}_{e}_{fcg}")
                if e < 8:
                    ew(p, "vector", "tensor_mul", st_[:], st_[:], pD[:],
                       reads=[stk, "pD"], writes=[stk], tag=f"h1{tc}_{e}_{fcg}")
                    ew(p, "vector", "tensor_mul",
                       hwb[:, fcg * tch:(fcg + 1) * tch], st_[:], wbs[:],
                       reads=[stk, "wbs"], writes=["hwb"],
                       tag=f"h2{tc}_{e}_{fcg}")
                else:
                    ew(p, "vector", "tensor_mul",
                       hwb[:, fcg * tch:(fcg + 1) * tch], st_[:], pD[:],
                       reads=[stk, "pD"], writes=["hwb"],
                       tag=f"h2{tc}_{e}_{fcg}")
            for oc in range(8):
                wi = oc % 2
                t20 = (e * 8 + oc) * 16 * 128
                p.dma("sync", w2t2[wi][:], we2b[:, t20:t20 + 16 * 128],
                      reads=["we2_hbm"], writes=[f"w2t2{wi}"],
                      tag=f"lw2{tc}_{e}_{oc}")
                pb, pbk = (pE, "pE") if oc % 2 == 0 else (pF, "pF")
                mm_group(p, [(pb[:], w2t2[wi][:, k * 128:(k + 1) * 128],
                              hwb[:, k * tch:(k + 1) * tch], k == 0, k == 15)
                             for k in range(16)],
                         reads=[f"w2t2{wi}", "hwb"], writes=[pbk],
                         tag=f"e2{tc}_{e}_{oc}")
                rsl = racc[:, oc * tch:(oc + 1) * tch]
                if e == 0:
                    ew(p, "vector", "tensor_copy", rsl, pb[:],
                       reads=[pbk], writes=["racc"], tag=f"r0{tc}_{e}_{oc}")
                else:
                    ew(p, "vector", "tensor_add", rsl, rsl, pb[:],
                       reads=[pbk, "racc"], writes=["racc"],
                       tag=f"ra{tc}_{e}_{oc}")
        for oc in range(8):
            ew(p, "vector", "scalar_tensor_tensor",
               zf[:, oc * tch:(oc + 1) * tch],
               racc[:, oc * tch:(oc + 1) * tch], 1.0 / 3.0,
               x3f[bi][:, oc * tch:(oc + 1) * tch], ALU.mult, ALU.add,
               reads=["racc", f"x3f{bi}"], writes=["zf"], tag=f"fz{tc}_{oc}")
        ln_fm(p, f"lnf_{tc}", zf, "zf", x3f[bi], f"x3f{bi}", aux, EPS_LN_OUTER)
        p.dma("sync", y_r[:, :, s], x3f[bi][:], reads=[f"x3f{bi}"],
              tag=f"sy{tc}")

    p.final_wait("sync")
    p.build()
    p.finish()
    return nc
